# version 05
# baseline (speedup 1.0000x reference)
import numpy as np
from contextlib import ExitStack

import concourse.bass as bass
import concourse.mybir as mybir
from concourse.bass_utils import run_bass_kernel_spmd

F32 = mybir.dt.float32
BF16 = mybir.dt.bfloat16
AF = mybir.ActivationFunctionType
ALU = mybir.AluOpType
AX = mybir.AxisListType

S = 4096
D = 1024
HC = 8
NT = 32
NKC = 32
QG = 512
NQG = S // QG
NEXP = 8
NIC = 4
NPAIR = 8
EPS = 1e-5
NPAIRS5 = 5
QN = 4


def ordered_prev(d, key, back=1):
    ks = list(d.keys())
    i = ks.index(key)
    return d[ks[i - back]] if i - back >= 0 else None


class Sem:
    def __init__(self, nc, ctx, name):
        self.h = ctx.enter_context(nc.semaphore(name))
        self.n = 0

    def inc(self, inst, k=1):
        inst.then_inc(self.h, k)
        self.n += k
        return self.n

    def d(self, inst):
        return self.inc(inst, 16)


def build_nc(n_phases=99, debug=False):
    nc = bass.Bass("TRN2", target_bir_lowering=False, dynamic_dma_scratch_size=16384, detect_race_conditions=False)

    dt = lambda name, shape, dtype=F32, kind="ExternalInput": nc.dram_tensor(
        name, shape, dtype, kind=kind)
    x_in = dt("x", [S, D])
    wl_d = dt("wl", [128, 8]);  wr_d = dt("wr", [128, 8])
    gwT_d = dt("gwT", [D, 8])
    wgT_d = dt("wgT", [NEXP, D, 512]); wuT_d = dt("wuT", [NEXP, D, 512])
    wdT_d = dt("wdT", [NEXP, 512, D])
    wsgT_d = dt("wsgT", [D, D]); wsuT_d = dt("wsuT", [D, D]); wsdT_d = dt("wsdT", [D, D])
    wqT_d = dt("wqT", [D, D]); wkT_d = dt("wkT", [D, D])
    wvTs_d = dt("wvTs", [D, D]); woT_d = dt("woT", [D, D])
    bqr_d = dt("bqrow", [1, D]); bkr_d = dt("bkrow", [1, D])
    bvs_d = dt("bvsrow", [1, D]); bo_d = dt("borow", [1, D])
    cb_d = dt("cb", [D, D]); scb_d = dt("scb", [128, 8]); b1_d = dt("b1row", [1, D])
    id_d = dt("ident", [128, 128])
    onr_d = dt("onesrow", [1, 128]); onc_d = dt("onescol", [128, 1])
    esel_d = dt("esel", [8, 1024])
    out_d = dt("out", [S, D], F32, "ExternalOutput")

    xn_dram = dt("xn_dram", [S, D], BF16, "Internal")
    qkv_dram = dt("qkv_dram", [3, S, D], BF16, "Internal")
    x_resid = dt("x_resid", [S, D], F32, "Internal")
    g_scr = dt("g_scr", [S, D], F32, "Internal")
    ao_hbm = dt("ao_hbm", [NPAIR, S, 128], BF16, "Internal")

    with ExitStack() as ctx:
        sb = lambda name, shape, d=F32: ctx.enter_context(nc.sbuf_tensor(name, shape, d))
        sem = lambda name: Sem(nc, ctx, name)

        xn_f = sb("xn_f", [128, HC * S], BF16)
        logits_sb = sb("logits", [128, NT * 8])
        combT = sb("combT", [8, NT * 128], BF16)
        id_bf = sb("id_bf", [128, 128], BF16)
        id_f = sb("id_f", [128, 128])
        onr_bf = sb("onr_bf", [1, 128], BF16)
        onr_f = sb("onr_f", [1, 128])
        onc_bf = sb("onc_bf", [128, 1], BF16)
        esel_bf = sb("esel_bf", [8, 1024], BF16)
        w_vec = sb("w_vec", [128, 8])
        bndsel = sb("bndsel", [128, 16], BF16)
        gwT_sb = sb("gwT_sb", [128, HC * 8])
        scb_sb = sb("scb_sb", [128, 8])
        bqr_bf = sb("bqr_bf", [1, D], BF16); bkr_bf = sb("bkr_bf", [1, D], BF16)
        bvs_bf = sb("bvs_bf", [1, D], BF16); bo_bf = sb("bo_bf", [1, D], BF16)
        b1_bf = sb("b1_bf", [1, D], BF16)
        wlr = sb("wlr", [128, 16])
        eps_sb = sb("eps_sb", [128, 1])

        sW = sem("sW")
        sL = sem("sL")
        sS = sem("sS")
        sPE = sem("sPE")
        sPE2 = sem("sPE2")
        sPE3 = sem("sPE3")
        sACT = sem("sACT")
        sACT2 = sem("sACT2")
        sDVE = sem("sDVE")
        sDVE2 = sem("sDVE2")
        sEV = sem("sEV")
        sGD = sem("sGD")

        def vg(eng, inst):
            n = sGD.inc(inst)
            eng.wait_ge(sGD.h, n)

        with nc.Block() as blk:
            @blk.gpsimd
            def _(g):
                for dst, src in [(id_bf, id_d), (onr_bf, onr_d), (onc_bf, onc_d),
                                 (esel_bf, esel_d), (bqr_bf, bqr_d),
                                 (bkr_bf, bkr_d), (bvs_bf, bvs_d),
                                 (bo_bf, bo_d), (b1_bf, b1_d)]:
                    sW.d(g.dma_start(dst[:, :], src[:, :]))
                g.wait_ge(sW.h, sW.n)

            @blk.sync
            def _(s):
                for dst, src in [(id_f, id_d), (onr_f, onr_d),
                                 (scb_sb, scb_d)]:
                    sL.d(s.dma_start(dst[:, :], src[:, :]))
                sL.d(s.dma_start(wlr[:, 0:8], wl_d[:, :]))
                sL.d(s.dma_start(wlr[:, 8:16], wr_d[:, :]))
                for c in range(HC):
                    sL.d(s.dma_start(gwT_sb[:, c * 8:(c + 1) * 8],
                                       gwT_d[c * 128:(c + 1) * 128, :]))
                s.wait_ge(sL.h, sL.n)

            @blk.vector
            def _(v):
                v.wait_ge(sL.h, sL.n)
                v.wait_ge(sW.h, sW.n)
                v.memset(eps_sb[:, :], EPS)
                vg(v, v.tensor_mul(w_vec[:, :], wlr[:, 0:8], wlr[:, 8:16]))
                for c in range(HC):
                    v.tensor_copy(bndsel[:, 2 * c:2 * c + 1], onc_bf[:, :])
                    v.tensor_mul(bndsel[:, 2 * c + 1:2 * c + 2],
                                 w_vec[:, c:c + 1], w_vec[:, c:c + 1])

        def phase_ln(layer):
            src = x_in if layer == 0 else x_resid
            with ExitStack() as pc:
                psb = lambda n_, s_, d=F32: pc.enter_context(nc.sbuf_tensor(n_, s_, d))
                pps = lambda n_, s_, d=F32: pc.enter_context(nc.psum_tensor(n_, s_, d))
                xt = psb(f"ln{layer}_xt", [128, 2 * D])
                xc = psb(f"ln{layer}_xc", [128, D])
                sq = psb(f"ln{layer}_sq", [128, D])
                st4 = psb(f"ln{layer}_st", [128, 8])
                xn32 = psb(f"ln{layer}_xn32", [128, D])
                xnbf = psb(f"ln{layer}_xnbf", [128, 2 * D], BF16)
                if layer == 0:
                    xnT_sb = psb("ln0_xnT", [128, D])
                    ps_xnT = pps("ps_xnT0", [128, D])
                    ps_L = pps("ps_L0", [128, 8])

                L0, S0 = sL.n, sS.n
                wa, wb, wc, wp, wq_, wr_, ws_, wu, wv_ = ({} for _ in range(9))
                dn, an, pn = sDVE.n, sACT.n, sPE.n
                for i in range(NT):
                    dn += 1; wa[i] = dn
                    an += 1; wp[i] = an
                    dn += 1; wb[i] = dn
                    an += 1; wq_[i] = an
                    dn += 1; wc[i] = dn
                    if layer == 0:
                        pn += 1; wu[i] = pn
                        an += 1; wr_[i] = an
                        pn += 1; wv_[i] = pn
                        an += 1; ws_[i] = an

                with nc.Block() as blk:
                    @blk.sync
                    def _(s):
                        for i in range(2):
                            sL.d(s.dma_start(xt[:, (i % 2) * D:(i % 2 + 1) * D],
                                             src[i * 128:(i + 1) * 128, :]))
                        for i in range(NT):
                            s.wait_ge(sDVE.h, wc[i])
                            sS.d(s.dma_start(xn_dram[i * 128:(i + 1) * 128, :],
                                             xnbf[:, (i % 2) * D:(i % 2 + 1) * D]))
                            if i + 2 < NT:
                                s.wait_ge(sDVE.h, wa[i])
                                sL.d(s.dma_start(
                                    xt[:, (i % 2) * D:(i % 2 + 1) * D],
                                    src[(i + 2) * 128:(i + 3) * 128, :]))
                        s.wait_ge(sS.h, sS.n)

                    @blk.vector
                    def _(v):
                        for i in range(NT):
                            xti = xt[:, (i % 2) * D:(i % 2 + 1) * D]
                            v.wait_ge(sL.h, L0 + 16 * (i + 1))
                            if i >= 1:
                                v.wait_ge(sACT.h, wp[i - 1])
                            vg(v, v.reduce_sum(st4[:, 0:1], xti, AX.X))
                            vg(v, v.tensor_scalar(st4[:, 1:2], st4[:, 0:1],
                                                  1.0 / D, None, ALU.mult))
                            sDVE.inc(v.tensor_scalar(xc[:, :], xti, st4[:, 1:2],
                                                     None, ALU.subtract))
                            v.wait_ge(sACT.h, wp[i])
                            sDVE.inc(v.reduce_sum(st4[:, 2:3], sq[:, :], AX.X))
                            v.wait_ge(sACT.h, wq_[i])
                            if i >= 1 and layer == 0:
                                v.wait_ge(sPE.h, wu[i - 1])
                            if i >= 2:
                                v.wait_ge(sS.h, S0 + 16 * (i - 1))
                            v.tensor_scalar(xn32[:, :], xc[:, :], st4[:, 3:4],
                                            None, ALU.mult)
                            sDVE.inc(v.tensor_copy(
                                xnbf[:, (i % 2) * D:(i % 2 + 1) * D], xn32[:, :]))

                    @blk.scalar
                    def _(sc):
                        for i in range(NT):
                            sc.wait_ge(sDVE.h, wa[i])
                            sACT.inc(sc.activation(sq[:, :], xc[:, :], AF.Square))
                            sc.wait_ge(sDVE.h, wb[i])
                            sc.activation(st4[:, 3:4], st4[:, 2:3], AF.Ln,
                                          bias=eps_sb[:, 0:1], scale=1.0 / D)
                            sACT.inc(sc.activation(st4[:, 3:4], st4[:, 3:4],
                                                   AF.Exp, scale=-0.5))
                            if layer == 0:
                                sc.wait_ge(sPE.h, wu[i])
                                sACT.inc(sc.copy(xnT_sb[:, :], ps_xnT[:, :]))
                                sc.wait_ge(sPE.h, wv_[i])
                                sACT.inc(sc.copy(logits_sb[:, i * 8:(i + 1) * 8],
                                                 ps_L[:, :]))

                    if layer == 0:
                        @blk.tensor
                        def _(t):
                            for i in range(NT):
                                t.wait_ge(sDVE.h, wc[i])
                                if i >= 1:
                                    t.wait_ge(sACT.h, wr_[i - 1])
                                for j in range(HC):
                                    inst = t.transpose(
                                        ps_xnT[:, j * 128:(j + 1) * 128],
                                        xn32[:, j * 128:(j + 1) * 128], id_f[:, :])
                                sPE.inc(inst)
                                t.wait_ge(sACT.h, wr_[i])
                                if i >= 1:
                                    t.wait_ge(sACT.h, ws_[i - 1])
                                for j in range(HC):
                                    inst = t.matmul(ps_L[:, :],
                                                    xnT_sb[:, j * 128:(j + 1) * 128],
                                                    gwT_sb[:, j * 8:(j + 1) * 8],
                                                    start=(j == 0),
                                                    stop=(j == HC - 1))
                                sPE.inc(inst)

        def phase_xnf():
            with nc.Block() as blk:
                @blk.sync
                def _(s):
                    for c in range(HC):
                        sL.d(s.dma_start_transpose(
                            xn_f[:, c * S:(c + 1) * S],
                            xn_dram[:, c * 128:(c + 1) * 128]))
                    s.wait_ge(sL.h, sL.n)

        def phase_combine():
            with ExitStack() as pc:
                cw = pc.enter_context(nc.sbuf_tensor("cw", [128, 32], F32))
                comb = pc.enter_context(nc.sbuf_tensor("comb", [128, 8], F32))
                ps_cT = pc.enter_context(nc.psum_tensor("ps_cT", [8, 128], F32))
                wd1, wd2, wd3, wa1, wp1 = {}, {}, {}, {}, {}
                dn, an, pn = sDVE.n, sACT.n, sPE.n
                for i in range(NT):
                    dn += 1; wd1[i] = dn
                    an += 1; wa1[i] = an
                    dn += 1; wd2[i] = dn
                    pn += 1; wp1[i] = pn
                    dn += 1; wd3[i] = dn

                with nc.Block() as blk:
                    @blk.vector
                    def _(v):
                        for i in range(NT):
                            L = logits_sb[:, i * 8:(i + 1) * 8]
                            m1, m2 = cw[:, 0:1], cw[:, 1:2]
                            w1, w2 = cw[:, 2:3], cw[:, 3:4]
                            den, rinv = cw[:, 4:5], cw[:, 5:6]
                            c1, c2 = cw[:, 6:7], cw[:, 7:8]
                            msk1, msk2 = cw[:, 8:16], cw[:, 16:24]
                            Lm = cw[:, 24:32]
                            vg(v, v.reduce_max(m1, L, AX.X))
                            vg(v, v.tensor_scalar(msk1, L, m1, None,
                                                  ALU.is_equal))
                            vg(v, v.tensor_scalar(Lm, msk1, -1e30, None,
                                                  ALU.mult))
                            vg(v, v.tensor_add(Lm, Lm, L))
                            inst = v.reduce_max(m2, Lm, AX.X)
                            sDVE.inc(inst)
                            v.wait_ge(sDVE.h, sDVE.n)
                            v.tensor_scalar(msk2, Lm, m2, None, ALU.is_equal)
                            v.wait_ge(sACT.h, wa1[i])
                            vg(v, v.tensor_add(den, w1, w2))
                            vg(v, v.tensor_scalar(den, den, 1e-20, None, ALU.add))
                            vg(v, v.reciprocal(rinv, den))
                            vg(v, v.tensor_scalar(c1, w1, rinv, 2.5, ALU.mult,
                                                  ALU.mult))
                            vg(v, v.tensor_scalar(c2, w2, rinv, 2.5, ALU.mult,
                                                  ALU.mult))
                            if i >= 1:
                                v.wait_ge(sPE.h, wp1[i - 1])
                            v.tensor_scalar(comb[:, :], msk1, c1, None, ALU.mult)
                            vg(v, v.tensor_scalar(Lm, msk2, c2, None, ALU.mult))
                            sDVE.inc(v.tensor_add(comb[:, :], comb[:, :], Lm))
                            v.wait_ge(sPE.h, wp1[i])
                            sDVE.inc(v.tensor_copy(
                                combT[:, i * 128:(i + 1) * 128], ps_cT[:, :]))

                    @blk.scalar
                    def _(sc):
                        for i in range(NT):
                            sc.wait_ge(sDVE.h, wd1[i])
                            sc.activation(cw[:, 2:3], cw[:, 0:1], AF.Sigmoid)
                            sACT.inc(sc.activation(cw[:, 3:4], cw[:, 1:2],
                                                   AF.Sigmoid))

                    @blk.tensor
                    def _(t):
                        for i in range(NT):
                            t.wait_ge(sDVE.h, wd2[i])
                            if i >= 1:
                                t.wait_ge(sDVE.h, wd3[i - 1])
                            sPE.inc(t.transpose(ps_cT[:, :], comb[:, :],
                                                id_f[:, :]))

        def phase_l0ffn():
            with ExitStack() as pc:
                psb = lambda n_, s_, d=F32: pc.enter_context(nc.sbuf_tensor(n_, s_, d))
                pps = lambda n_, s_, d=F32: pc.enter_context(nc.psum_tensor(n_, s_, d))
                acc = psb("acc", [128, 4 * D])
                wgt = [psb(f"wgt{b}", [128, 2 * HC * 512], BF16) for b in range(2)]
                wut = [psb(f"wut{b}", [128, 2 * HC * 512], BF16) for b in range(2)]
                wdt = [psb("wdt", [128, 2 * NIC * D], BF16)] * 2
                hmS = psb("hmS", [128, 2 * NIC * 512], BF16)
                silu_sb = psb("silu_b", [128, 512], BF16)
                bc_sb = psb("bc_sb", [128, 512], BF16)
                ps_g = pps("ps_g", [128, 512])
                ps_u = pps("ps_u", [128, 512])
                ps_bc = pps("ps_bc", [128, 512])
                ps_r = [pps(f"ps_r{b}", [128, D]) for b in range(2)]

                W0, S0 = sW.n, sS.n
                NW = 2 * (2 * HC + NIC)
                QN8 = 8
                NGI = QN8 * NPAIRS5

                wg_, wu_, wbc, wdn, wsil, wbce, whm, wev, wfs = \
                    {}, {}, {}, {}, {}, {}, {}, {}, {}
                pe, pe2, pe3 = sPE.n, sPE2.n, sPE3.n
                act, act2, dve, ev = sACT.n, sACT2.n, sDVE.n, sEV.n
                for gi in range(NGI):
                    q, p = divmod(gi, NPAIRS5)
                    for tg in range(1):
                        for slot in range(2):
                            if p > 0:
                                pe2 += 1; wbc[gi, tg, slot] = pe2
                                act2 += 1; wbce[gi, tg, slot] = act2
                            for ic in range(NIC):
                                pe += 1; wg_[gi, tg, slot, ic] = pe
                                act += 1; wsil[gi, tg, slot, ic] = act
                                pe2 += 1; wu_[gi, tg, slot, ic] = pe2
                                dve += 1; whm[gi, tg, slot, ic] = dve
                        for t1 in range(4):
                            pe3 += 1; wdn[gi, tg, t1] = pe3
                            ev += 1; wev[gi, tg, t1] = ev
                    if p == NPAIRS5 - 1:
                        for tl in range(4):
                            act2 += 1; wfs[q, tl] = act2
                wpd = {gi: wdn[gi, 0, 3] for gi in range(NGI)}
                stq = {q: S0 + 16 * 4 * (q + 1) for q in range(QN8)}

                def bc_prev_reader(gi, tg, slot):
                    ks = [k for k in whm.keys()
                          if k < (gi, tg, slot, 0) and (k[0] % NPAIRS5) != 0]
                    return whm[ks[-1]] if ks else None

                def wslot_aps(b, p, slot):
                    res = []
                    for hc in range(HC):
                        gd = wgt[b][:, slot * HC * 512 + hc * 512:
                                    slot * HC * 512 + (hc + 1) * 512]
                        ud = wut[b][:, slot * HC * 512 + hc * 512:
                                    slot * HC * 512 + (hc + 1) * 512]
                        if p == 0:
                            gs = wsgT_d[hc * 128:(hc + 1) * 128,
                                        slot * 512:(slot + 1) * 512]
                            us = wsuT_d[hc * 128:(hc + 1) * 128,
                                        slot * 512:(slot + 1) * 512]
                        else:
                            e = (p - 1) * 2 + slot
                            gs = wgT_d[e, hc * 128:(hc + 1) * 128, :]
                            us = wuT_d[e, hc * 128:(hc + 1) * 128, :]
                        res.append((gd, gs, False)); res.append((ud, us, False))
                    for ic in range(NIC):
                        dd = wdt[b][:, slot * NIC * D + ic * D:
                                    slot * NIC * D + (ic + 1) * D]
                        if p == 0:
                            ds = wsdT_d[slot * 512 + ic * 128:
                                        slot * 512 + (ic + 1) * 128, :]
                        else:
                            e = (p - 1) * 2 + slot
                            ds = wdT_d[e, ic * 128:(ic + 1) * 128, :]
                        res.append((dd, ds, True))
                    return res

                with nc.Block() as blk:
                    @blk.gpsimd
                    def _(g):
                        for gi in range(NGI):
                            q, p = divmod(gi, NPAIRS5)
                            if gi >= 2:
                                g.wait_ge(sPE3.h, wpd[gi - 2])
                            gu, dd = [], []
                            for slot in range(2):
                                for pair_ in wslot_aps(gi % 2, p, slot):
                                    (dd if pair_[2] else gu).append(pair_)
                            for dst, srcap, _ in gu:
                                sW.d(g.dma_start(dst, srcap))
                            if gi >= 1:
                                g.wait_ge(sPE3.h, wpd[gi - 1])
                            for dst, srcap, _ in dd:
                                sW.d(g.dma_start(dst, srcap))
                        g.wait_ge(sW.h, sW.n)

                    @blk.tensor
                    def _(t):
                        for gi in range(NGI):
                            q, p = divmod(gi, NPAIRS5)
                            b = gi % 2
                            t.wait_ge(sW.h, W0 + 16 * (NW * gi + 32))
                            for tg in range(1):
                                tcol = q * 512
                                for slot in range(2):
                                    if p > 0:
                                        e = (p - 1) * 2 + slot
                                        prev = ordered_prev(wbce, (gi, tg, slot))
                                        if prev is not None:
                                            t.wait_ge(sACT2.h, prev)
                                        sPE2.inc(t.matmul(
                                            ps_bc[:, :],
                                            esel_bf[:, e * 128:(e + 1) * 128],
                                            combT[:, tcol:tcol + 512],
                                            start=True, stop=True))
                                    for ic in range(NIC):
                                        prev = ordered_prev(wsil, (gi, tg, slot, ic))
                                        if prev is not None:
                                            t.wait_ge(sACT.h, prev)
                                        for hc in range(HC):
                                            col = (slot * HC * 512 + hc * 512
                                                   + ic * 128)
                                            inst = t.matmul(
                                                ps_g[:, :], wgt[b][:, col:col + 128],
                                                xn_f[:, hc * S + tcol:
                                                     hc * S + tcol + 512],
                                                start=(hc == 0), stop=(hc == HC - 1))
                                        sPE.inc(inst)
                                        prev = ordered_prev(whm, (gi, tg, slot, ic))
                                        if prev is not None:
                                            t.wait_ge(sDVE.h, prev)
                                        for hc in range(HC):
                                            col = (slot * HC * 512 + hc * 512
                                                   + ic * 128)
                                            inst = t.matmul(
                                                ps_u[:, :], wut[b][:, col:col + 128],
                                                xn_f[:, hc * S + tcol:
                                                     hc * S + tcol + 512],
                                                start=(hc == 0), stop=(hc == HC - 1))
                                        sPE2.inc(inst)
                                for t1 in range(4):
                                    if tg == 0 and t1 == 0:
                                        t.wait_ge(sW.h, W0 + 16 * NW * (gi + 1))
                                    t.wait_ge(sDVE.h, whm[gi, tg, 1, NIC - 1])
                                    prev = ordered_prev(wev, (gi, tg, t1), back=2)
                                    if prev is not None:
                                        t.wait_ge(sEV.h, prev)
                                    for slot in range(2):
                                        for ic in range(NIC):
                                            for nh in range(2):
                                                inst = t.matmul(
                                                    ps_r[t1 % 2][:, nh * 512:
                                                                 (nh + 1) * 512],
                                                    hmS[:, slot * NIC * 512
                                                        + ic * 512 + t1 * 128:
                                                        slot * NIC * 512
                                                        + ic * 512 + (t1 + 1) * 128],
                                                    wdt[b][:, slot * NIC * D + ic * D
                                                           + nh * 512:
                                                           slot * NIC * D + ic * D
                                                           + (nh + 1) * 512],
                                                    start=(slot == 0 and ic == 0),
                                                    stop=(slot == 1 and
                                                          ic == NIC - 1))
                                    sPE3.inc(inst)

                    @blk.scalar
                    def _(sc):
                        for gi in range(NGI):
                            q, p = divmod(gi, NPAIRS5)
                            for tg in range(1):
                                for slot in range(2):
                                    if p > 0:
                                        sc.wait_ge(sPE2.h, wbc[gi, tg, slot])
                                        prev = bc_prev_reader(gi, tg, slot)
                                        if prev is not None:
                                            sc.wait_ge(sDVE.h, prev)
                                        sACT2.inc(sc.copy(bc_sb[:, :], ps_bc[:, :]))
                                    for ic in range(NIC):
                                        sc.wait_ge(sPE.h, wg_[gi, tg, slot, ic])
                                        prev = ordered_prev(whm, (gi, tg, slot, ic))
                                        if prev is not None:
                                            sc.wait_ge(sDVE.h, prev)
                                        sACT.inc(sc.activation(
                                            silu_sb[:, :], ps_g[:, :], AF.Silu))
                            if p == NPAIRS5 - 1:
                                for tl in range(4):
                                    sc.wait_ge(sEV.h, wev[gi, 0, tl])
                                    a_sl = acc[:, tl * D:(tl + 1) * D]
                                    sACT2.inc(sc.activation(a_sl, a_sl, AF.Silu))

                    @blk.vector
                    def _(v):
                        for gi in range(NGI):
                            q, p = divmod(gi, NPAIRS5)
                            for tg in range(1):
                                for slot in range(2):
                                    for ic in range(NIC):
                                        v.wait_ge(sACT.h, wsil[gi, tg, slot, ic])
                                        v.wait_ge(sPE2.h, wu_[gi, tg, slot, ic])
                                        if ic == 0 and slot == 0:
                                            if gi >= 1:
                                                v.wait_ge(sPE3.h, wdn[gi - 1, 0, 3])
                                        dst = hmS[:, slot * NIC * 512 + ic * 512:
                                                  slot * NIC * 512 + (ic + 1) * 512]
                                        if p > 0:
                                            v.wait_ge(sACT2.h, wbce[gi, tg, slot])
                                            v.tensor_tensor(dst, silu_sb[:, :],
                                                            ps_u[:, :], ALU.mult)
                                            sDVE.inc(v.tensor_tensor(
                                                dst, dst, bc_sb[:, :], ALU.mult))
                                        else:
                                            sDVE.inc(v.tensor_tensor(
                                                dst, silu_sb[:, :], ps_u[:, :],
                                                ALU.mult))
                                for t1 in range(4):
                                    v.wait_ge(sPE3.h, wdn[gi, tg, t1])
                                    tl = tg * 4 + t1
                                    a_sl = acc[:, tl * D:(tl + 1) * D]
                                    if p == 0:
                                        if q > 0:
                                            v.wait_ge(sS.h, stq[q - 1])
                                        sEV.inc(v.tensor_copy(a_sl,
                                                              ps_r[t1 % 2][:, :]))
                                    else:
                                        sEV.inc(v.tensor_tensor(
                                            a_sl, a_sl, ps_r[t1 % 2][:, :],
                                            ALU.add))

                    @blk.sync
                    def _(s):
                        for q in range(QN8):
                            for tl in range(4):
                                s.wait_ge(sACT2.h, wfs[q, tl])
                                sS.d(s.dma_start(
                                    g_scr[q * 512 + tl * 128:
                                          q * 512 + (tl + 1) * 128, :],
                                    acc[:, tl * D:(tl + 1) * D]))
                        s.wait_ge(sS.h, sS.n)

        def phase_l0attn():
            QG0 = 256
            NQG0 = S // QG0
            with ExitStack() as pc:
                psb = lambda n_, s_, d=F32: pc.enter_context(nc.sbuf_tensor(n_, s_, d))
                pps = lambda n_, s_, d=F32: pc.enter_context(nc.psum_tensor(n_, s_, d))
                xnt = psb("xnt", [128, NKC * D], BF16)
                pT = psb("pT0", [128, NKC * QG0], BF16)
                qts = psb("qts", [128, HC * QG0], BF16)
                sq2 = qts
                bnd = psb("bnd", [1, 8])
                bnd2 = psb("bnd2", [2, 16])
                bias0 = psb("bias0", [128, 1])
                rinv_sb = psb("rinv0", [128, 1])
                gbuf = psb("gbuf", [128, 2 * D])
                xbuf = psb("xbuf", [128, 2 * D])
                ps_sT = [pps(f"ps_sT{b_}", [128, 1024]) for b_ in range(2)]
                ps_a = pps("ps_a0", [128, 512])
                ps_b = pps("ps_b0", [128, 513])
                ps_bnd = pps("ps_bnd", [2, 512])

                L0, S0 = sL.n, sS.n
                wsqd, wbmm, wbrd = {}, {}, {}
                act2, pe3, dve2 = sACT2.n, sPE3.n, sDVE2.n
                for ts in range(8):
                    for c in range(HC):
                        act2 += 1; wsqd[ts, c] = act2
                        pe3 += 1; wbmm[ts, c] = pe3
                    dve2 += 1; wbrd[ts] = dve2
                dve2 += 1; w_cmb = dve2
                pe3 += 1; w_e2t = pe3
                dve2 += 1; w_prod = dve2
                act2 += 1; w_bM = act2
                pe3 += 1; w_bc0 = pe3
                dve2 += 1; w_bias0 = dve2

                wqts, wqk, wexp, wpv, wup = {}, {}, {}, {}, {}
                dve, pe, act, pe2, ev = sDVE.n, sPE.n, sACT.n, sPE2.n, sEV.n
                for qg in range(NQG0):
                    dve += 1; wqts[qg] = dve
                    for kq in range(8):
                        pe += 1; wqk[qg, kq] = pe
                        act += 1; wexp[qg, kq] = act
                    for qs in range(2):
                        pe2 += 1; wpv[qg, qs] = pe2
                        ev += 1; wup[qg, qs] = ev

                with nc.Block() as blk:
                    @blk.sync
                    def _(s):
                        for kc in range(NKC):
                            sL.d(s.dma_start(
                                xnt[:, kc * D:(kc + 1) * D],
                                xn_dram[kc * 128:(kc + 1) * 128, :]))
                        for n in range(2):
                            row = n * 128
                            sL.d(s.dma_start(gbuf[:, (n % 2) * D:(n % 2 + 1) * D],
                                             g_scr[row:row + 128, :]))
                            sL.d(s.dma_start(xbuf[:, (n % 2) * D:(n % 2 + 1) * D],
                                             x_in[row:row + 128, :]))
                        for n in range(NT):
                            qg, qs = divmod(n, 2)
                            row = n * 128
                            s.wait_ge(sEV.h, wup[qg, qs])
                            sS.d(s.dma_start(
                                x_resid[row:row + 128, :],
                                xbuf[:, (n % 2) * D:(n % 2 + 1) * D]))
                            if n + 2 < NT:
                                s.wait_ge(sS.h, S0 + 16 * (n + 1))
                                row2 = (n + 2) * 128
                                sL.d(s.dma_start(
                                    gbuf[:, (n % 2) * D:(n % 2 + 1) * D],
                                    g_scr[row2:row2 + 128, :]))
                                sL.d(s.dma_start(
                                    xbuf[:, (n % 2) * D:(n % 2 + 1) * D],
                                    x_in[row2:row2 + 128, :]))
                        s.wait_ge(sS.h, sS.n)

                    @blk.scalar
                    def _(sc):
                        for ts in range(8):
                            for c in range(HC):
                                nsq = ts * HC + c
                                if nsq >= 2:
                                    pts, pcc = divmod(nsq - 2, HC)
                                    sc.wait_ge(sPE3.h, wbmm[pts, pcc])
                                sACT2.inc(sc.activation(
                                    sq2[:, (c % 2) * 512:(c % 2 + 1) * 512],
                                    xn_f[:, c * S + ts * 512:
                                         c * S + (ts + 1) * 512],
                                    AF.Square))
                        sc.wait_ge(sDVE2.h, w_prod)
                        sc.activation(bnd[:, 2:3], bnd[:, 1:2], AF.Ln)
                        sACT2.inc(sc.activation(bnd[:, 2:3], bnd[:, 2:3],
                                                AF.Exp, scale=0.5))
                        for qg in range(NQG0):
                            for kq in range(8):
                                sc.wait_ge(sPE.h, wqk[qg, kq])
                                if qg == 0 and kq == 0:
                                    sc.wait_ge(sDVE2.h, w_bias0)
                                if qg >= 1 and kq == 0:
                                    sc.wait_ge(sPE2.h, wpv[qg - 1, 1])
                                sACT.inc(sc.activation(
                                    pT[:, kq * 1024:(kq + 1) * 1024],
                                    ps_sT[kq % 2][:, :],
                                    AF.Exp, bias=bias0[:, 0:1], scale=1.0))

                    @blk.tensor
                    def _(t):
                        for ts in range(8):
                            if ts >= 1:
                                t.wait_ge(sDVE2.h, wbrd[ts - 1])
                            for c in range(HC):
                                t.wait_ge(sACT2.h, wsqd[ts, c])
                                sPE3.inc(t.matmul(
                                    ps_bnd[:, :], bndsel[:, 2 * c:2 * c + 2],
                                    sq2[:, (c % 2) * 512:(c % 2 + 1) * 512],
                                    start=(c == 0), stop=(c == HC - 1)))
                        t.wait_ge(sDVE2.h, w_cmb)
                        sPE3.inc(t.transpose(ps_bnd[0:1, 0:2], bnd2[:, 8:9],
                                             id_f[0:2, 0:2]))
                        t.wait_ge(sACT2.h, w_bM)
                        sPE3.inc(t.matmul(ps_a[:, 0:1], onr_f[:, :], bnd[:, 2:3],
                                          start=True, stop=True))
                        for qg in range(NQG0):
                            t.wait_ge(sDVE.h, wqts[qg])
                            for kq in range(8):
                                prev = ordered_prev(wexp, (qg, kq), back=2)
                                if prev is not None:
                                    t.wait_ge(sACT.h, prev)
                                for kci in range(4):
                                    kc = kq * 4 + kci
                                    for c in range(HC):
                                        inst = t.matmul(
                                            ps_sT[kq % 2][:, kci * QG0:
                                                          (kci + 1) * QG0],
                                            xn_f[:, c * S + kc * 128:
                                                 c * S + (kc + 1) * 128],
                                            qts[:, c * QG0:(c + 1) * QG0],
                                            start=(c == 0), stop=(c == HC - 1))
                                sPE.inc(inst)
                            t.wait_ge(sACT.h, wexp[qg, 7])
                            for qs in range(2):
                                if qg == 0 and qs == 0:
                                    t.wait_ge(sDVE2.h, w_bias0)
                                prev = ordered_prev(wup, (qg, qs))
                                if prev is not None:
                                    t.wait_ge(sEV.h, prev)
                                for kc in range(NKC):
                                    lhs = pT[:, kc * QG0 + qs * 128:
                                             kc * QG0 + (qs + 1) * 128]
                                    st_ = (kc == 0); sp_ = (kc == NKC - 1)
                                    t.matmul(ps_a[:, :], lhs,
                                             xnt[:, kc * D:kc * D + 512],
                                             start=st_, stop=sp_)
                                    t.matmul(ps_b[:, 0:512], lhs,
                                             xnt[:, kc * D + 512:(kc + 1) * D],
                                             start=st_, stop=sp_)
                                    inst = t.matmul(ps_b[:, 512:513], lhs,
                                                    onc_bf[:, :],
                                                    start=st_, stop=sp_)
                                sPE2.inc(inst)

                    @blk.vector
                    def _(v):
                        for ts in range(8):
                            v.wait_ge(sPE3.h, wbmm[ts, HC - 1])
                            sDVE2.inc(v.reduce_max(bnd2[:, ts:ts + 1],
                                                   ps_bnd[:, :], AX.X))
                        v.wait_ge(sDVE2.h, wbrd[7])
                        sDVE2.inc(v.reduce_max(bnd2[:, 8:9], bnd2[:, 0:8], AX.X))
                        v.wait_ge(sPE3.h, w_e2t)
                        vg(v, v.tensor_copy(bnd[:, 4:6], ps_bnd[0:1, 0:2]))
                        sDVE2.inc(v.tensor_mul(bnd[:, 1:2], bnd[:, 4:5],
                                               bnd[:, 5:6]))
                        v.wait_ge(sPE3.h, w_bc0)
                        sDVE2.inc(v.tensor_scalar(bias0[:, :], ps_a[:, 0:1],
                                                  -1.02, None, ALU.mult))
                        n = 0
                        for qg in range(NQG0):
                            if qg == 0:
                                v.wait_ge(sPE3.h, wbmm[7, HC - 1])
                            if qg >= 1:
                                v.wait_ge(sPE.h, wqk[qg - 1, 7])
                            for c in range(HC):
                                inst = v.tensor_scalar(
                                    qts[:, c * QG0:(c + 1) * QG0],
                                    xn_f[:, c * S + qg * QG0:
                                         c * S + (qg + 1) * QG0],
                                    w_vec[:, c:c + 1], None, ALU.mult)
                            sDVE.inc(inst)
                            for qs in range(2):
                                par = n % 2
                                v.wait_ge(sPE2.h, wpv[qg, qs])
                                v.wait_ge(sL.h, L0 + 16 * NKC + 32 * (n + 1))
                                vg(v, v.reciprocal(rinv_sb[:, :],
                                                   ps_b[:, 512:513]))
                                gb1 = gbuf[:, par * D:par * D + 512]
                                gb2 = gbuf[:, par * D + 512:(par + 1) * D]
                                v.tensor_scalar(gb1, gb1, rinv_sb[:, 0:1],
                                                None, ALU.mult)
                                v.tensor_tensor(gb1, gb1, ps_a[:, :], ALU.mult)
                                v.tensor_add(xbuf[:, par * D:par * D + 512],
                                             gb1,
                                             xbuf[:, par * D:par * D + 512])
                                v.tensor_scalar(gb2, gb2, rinv_sb[:, 0:1],
                                                None, ALU.mult)
                                v.tensor_tensor(gb2, gb2, ps_b[:, 0:512],
                                                ALU.mult)
                                sEV.inc(v.tensor_add(
                                    xbuf[:, par * D + 512:(par + 1) * D],
                                    gb2,
                                    xbuf[:, par * D + 512:(par + 1) * D]))
                                n += 1

        def phase_l1ffn():
            with ExitStack() as pc:
                psb = lambda n_, s_, d=F32: pc.enter_context(nc.sbuf_tensor(n_, s_, d))
                pps = lambda n_, s_, d=F32: pc.enter_context(nc.psum_tensor(n_, s_, d))
                cbn = psb("cbn", [128, 2 * D])
                cbs = psb("cbs", [128, 2 * D], BF16)
                CBT = psb("CBT", [128, HC * D], BF16)
                g1 = psb("g1", [128, 2 * D])
                ps_t = [pps(f"ps_t1{b}", [128, 128], BF16) for b in range(2)]
                ps_f = [pps(f"ps_f{b}", [128, D]) for b in range(2)]

                L0, S0 = sL.n, sS.n
                wsc, wtr, wcp = {}, {}, {}
                dve, pe3, dve2 = sDVE.n, sPE3.n, sDVE2.n
                for nck in range(HC):
                    dve += 1; wsc[nck] = dve
                    for kc in range(HC):
                        pe3 += 1; wtr[nck, kc] = pe3
                        dve2 += 1; wcp[nck, kc] = dve2
                wmm, wsl = {}, {}
                pe, act = sPE.n, sACT.n
                for i in range(NT):
                    pe += 1; wmm[i] = pe
                    act += 1; wsl[i] = act

                with nc.Block() as blk:
                    @blk.sync
                    def _(s):
                        for nck in range(HC):
                            if nck >= 2:
                                s.wait_ge(sDVE.h, wsc[nck - 2])
                            sL.d(s.dma_start(
                                cbn[:, (nck % 2) * D:(nck % 2 + 1) * D],
                                cb_d[nck * 128:(nck + 1) * 128, :]))
                        for i in range(NT):
                            s.wait_ge(sACT.h, wsl[i])
                            sS.d(s.dma_start(
                                g_scr[i * 128:(i + 1) * 128, :],
                                g1[:, (i % 2) * D:(i % 2 + 1) * D]))
                        s.wait_ge(sS.h, sS.n)

                    @blk.vector
                    def _(v):
                        for nck in range(HC):
                            v.wait_ge(sL.h, L0 + 16 * (nck + 1))
                            if nck >= 2:
                                v.wait_ge(sPE3.h, wtr[nck - 2, HC - 1])
                            sDVE.inc(v.tensor_scalar(
                                cbs[:, (nck % 2) * D:(nck % 2 + 1) * D],
                                cbn[:, (nck % 2) * D:(nck % 2 + 1) * D],
                                scb_sb[:, nck:nck + 1], 1.0 / 127.0,
                                ALU.mult, ALU.mult))
                            for kc in range(HC):
                                v.wait_ge(sPE3.h, wtr[nck, kc])
                                dstv = CBT.rearrange("p (k n) -> p k n", k=HC)
                                sDVE2.inc(v.tensor_copy(
                                    dstv[:, kc, nck * 128:(nck + 1) * 128],
                                    ps_t[kc % 2][:, :]))

                    @blk.tensor
                    def _(t):
                        for nck in range(HC):
                            t.wait_ge(sDVE.h, wsc[nck])
                            for kc in range(HC):
                                prev = ordered_prev(wcp, (nck, kc), back=2)
                                if prev is not None:
                                    t.wait_ge(sDVE2.h, prev)
                                sPE3.inc(t.transpose(
                                    ps_t[kc % 2][:, :],
                                    cbs[:, (nck % 2) * D + kc * 128:
                                        (nck % 2) * D + (kc + 1) * 128],
                                    id_bf[:, :]))
                        for i in range(NT):
                            if i >= 2:
                                t.wait_ge(sACT.h, wsl[i - 2])
                            if i == 0:
                                t.wait_ge(sDVE2.h, wcp[HC - 1, HC - 1])
                            for nh in range(2):
                                for kc in range(HC):
                                    t.matmul(ps_f[i % 2][:, nh * 512:(nh + 1) * 512],
                                             xn_f[:, kc * S + i * 128:
                                                  kc * S + (i + 1) * 128],
                                             CBT[:, kc * D + nh * 512:
                                                 kc * D + (nh + 1) * 512],
                                             start=(kc == 0), stop=False)
                                inst = t.matmul(
                                    ps_f[i % 2][:, nh * 512:(nh + 1) * 512],
                                    onr_bf[:, :],
                                    b1_bf[0:1, nh * 512:(nh + 1) * 512],
                                    start=False, stop=True)
                            sPE.inc(inst)

                    @blk.scalar
                    def _(sc):
                        for i in range(NT):
                            sc.wait_ge(sPE.h, wmm[i])
                            if i >= 2:
                                sc.wait_ge(sS.h, S0 + 16 * (i - 1))
                            sACT.inc(sc.activation(
                                g1[:, (i % 2) * D:(i % 2 + 1) * D],
                                ps_f[i % 2][:, :], AF.Silu))

        def phase_l1attn():
            with ExitStack() as pc:
                psb = lambda n_, s_, d=F32: pc.enter_context(nc.sbuf_tensor(n_, s_, d))
                pps = lambda n_, s_, d=F32: pc.enter_context(nc.psum_tensor(n_, s_, d))
                wq_sb = psb("wq_sb", [128, HC * D], BF16)
                wk_sb = psb("wk_sb", [128, HC * D], BF16)
                wv_sb = psb("wv_sb", [128, HC * D], BF16)
                pbuf = [psb(f"pj{w}", [128, 2 * D], BF16) for w in range(3)]
                qb = psb("qb", [128, 2 * D], BF16)
                kb = psb("kb", [128, 2 * D], BF16)
                vb = psb("vb", [128, 2 * D], BF16)
                prod = psb("prodF", [128, 16384], BF16)
                SC = psb("SCf", [128, 256])
                sc2 = psb("sc2f", [128, 256])
                pp = psb("ppf", [128, 256], BF16)
                at = psb("atf", [128, 256], BF16)
                sm = psb("smf", [128, 64])
                Ob = psb("Obf", [128, 2 * D])
                ps_p = [pps(f"ps_p{w}", [128, D]) for w in range(3)]

                W0, L0, S0 = sW.n, sL.n, sS.n
                wpm, wpe, wex = {}, {}, {}
                wdA, wdB, wdC = {}, {}, {}
                pe, dve, act, dve2 = sPE.n, sDVE.n, sACT.n, sDVE2.n
                for i in range(NT):
                    for w in range(3):
                        pe += 1; wpm[i, w] = pe
                        dve += 1; wpe[i, w] = dve
                for i in range(NT):
                    dve2 += 1; wdA[i] = dve2
                    act += 1; wex[i] = act
                    dve2 += 1; wdB[i] = dve2
                    dve2 += 1; wdC[i] = dve2
                stF1 = {i: S0 + 16 * 3 * (i + 1) for i in range(NT)}
                WB = W0 + 16 * 24
                stF2 = {i: WB + 16 * (i + 1) for i in range(NT)}
                L1_ = L0 + 16 * 24
                ldF2 = {i: L0 + 16 * 3 * (i + 1) for i in range(NT)}

                with nc.Block() as blk:
                    @blk.gpsimd
                    def _(g):
                        for wsrc, wbuf in ((wqT_d, wq_sb), (wkT_d, wk_sb),
                                           (wvTs_d, wv_sb)):
                            for hc in range(HC):
                                sW.d(g.dma_start(
                                    wbuf[:, hc * D:(hc + 1) * D],
                                    wsrc[hc * 128:(hc + 1) * 128, :]))
                        for i in range(NT):
                            g.wait_ge(sDVE2.h, wdC[i])
                            sW.d(g.dma_start(
                                xn_dram[i * 128:(i + 1) * 128, :],
                                Ob[:, (i % 2) * D:(i % 2 + 1) * D]))
                        g.wait_ge(sW.h, stF2[NT - 1])
                        otok = xn_dram
                        ov = otok.rearrange("(r s) c -> r s c", s=16)
                        for p in range(NPAIR):
                            dst = ao_hbm[p].rearrange("(q rr) (h d) -> q rr h d",
                                                      q=16, h=2)
                            for q in range(16):
                                sW.d(g.dma_start(
                                    dst[q, :, :, :],
                                    ov[:, 2 * p:2 * p + 2,
                                       q * 64:(q + 1) * 64]))
                        g.wait_ge(sW.h, sW.n)

                    @blk.tensor
                    def _(t):
                        t.wait_ge(sW.h, W0 + 16 * 24)
                        for i in range(NT):
                            for w in range(3):
                                wsb = (wq_sb, wk_sb, wv_sb)[w]
                                brow = (bqr_bf, bkr_bf, bvs_bf)[w]
                                prev = ordered_prev(wpe, (i, w), back=3)
                                if prev is not None:
                                    t.wait_ge(sDVE.h, prev)
                                for hc in range(HC):
                                    for nh in range(2):
                                        t.matmul(
                                            ps_p[w][:, nh * 512:(nh + 1) * 512],
                                            xn_f[:, hc * S + i * 128:
                                                 hc * S + (i + 1) * 128],
                                            wsb[:, hc * D + nh * 512:
                                                hc * D + (nh + 1) * 512],
                                            start=(hc == 0), stop=False)
                                for nh in range(2):
                                    inst = t.matmul(
                                        ps_p[w][:, nh * 512:(nh + 1) * 512],
                                        onr_bf[:, :],
                                        brow[0:1, nh * 512:(nh + 1) * 512],
                                        start=False, stop=True)
                                sPE.inc(inst)

                    @blk.vector
                    def _(v):
                        for i in range(NT):
                            for w in range(3):
                                v.wait_ge(sPE.h, wpm[i, w])
                                if i >= 2:
                                    v.wait_ge(sS.h, stF1[i - 2])
                                sDVE.inc(v.tensor_copy(
                                    pbuf[w][:, (i % 2) * D:(i % 2 + 1) * D],
                                    ps_p[w][:, :]))
                        for i in range(NT):
                            par = i % 2
                            v.wait_ge(sL.h, ldF2[i])
                            qi = qb[:, par * D:(par + 1) * D]
                            ki = kb[:, par * D:(par + 1) * D]
                            vi = vb[:, par * D:(par + 1) * D]
                            qv = bass.AP(qb, par * D,
                                         [[2 * D, 128], [64, 16], [0, 16], [1, 64]])
                            kv = bass.AP(kb, par * D,
                                         [[2 * D, 128], [0, 16], [64, 16], [1, 64]])
                            v.tensor_tensor(prod[:, :], qv, kv, ALU.mult)
                            vg(v, v.reduce_sum(
                                SC[:, :],
                                prod.rearrange("p (s d) -> p s d", d=64), AX.X))
                            vg(v, v.tensor_reduce(
                                sm[:, 0:16],
                                SC.rearrange("p (a b) -> p a b", b=16),
                                AX.X, ALU.max, negate=True))
                            for hq in range(16):
                                inst = v.tensor_scalar(
                                    sc2[:, hq * 16:(hq + 1) * 16],
                                    SC[:, hq * 16:(hq + 1) * 16],
                                    sm[:, hq:hq + 1], None, ALU.add)
                            sDVE2.inc(inst)
                            v.wait_ge(sACT.h, wex[i])
                            vg(v, v.reduce_sum(
                                sm[:, 16:32],
                                pp.rearrange("p (a b) -> p a b", b=16), AX.X))
                            vg(v, v.reciprocal(sm[:, 32:48], sm[:, 16:32]))
                            for hq in range(16):
                                inst = v.tensor_scalar(
                                    at[:, hq * 16:(hq + 1) * 16],
                                    pp[:, hq * 16:(hq + 1) * 16],
                                    sm[:, 32 + hq:33 + hq], None, ALU.mult)
                            sDVE2.inc(inst)
                            av = bass.AP(at, 0,
                                         [[256, 128], [16, 16], [0, 64], [1, 16]])
                            vv = bass.AP(vb, par * D,
                                         [[2 * D, 128], [0, 16], [16, 64], [1, 16]])
                            v.tensor_tensor(prod[:, :], av, vv, ALU.mult)
                            if i >= 2:
                                v.wait_ge(sW.h, stF2[i - 2])
                            sDVE2.inc(v.reduce_sum(
                                Ob[:, par * D:(par + 1) * D],
                                prod.rearrange("p (s d) -> p s d", d=16),
                                AX.X))

                    @blk.scalar
                    def _(sc):
                        for i in range(NT):
                            sc.wait_ge(sDVE2.h, wdA[i])
                            if i >= 1:
                                sc.wait_ge(sDVE2.h, wdB[i - 1])
                            sACT.inc(sc.activation(pp[:, :], sc2[:, :], AF.Exp,
                                                   scale=0.125))

                    @blk.sync
                    def _(s):
                        for i in range(NT):
                            for w in range(3):
                                s.wait_ge(sDVE.h, wpe[i, w])
                                sS.d(s.dma_start(
                                    qkv_dram[w, i * 128:(i + 1) * 128, :],
                                    pbuf[w][:, (i % 2) * D:(i % 2 + 1) * D]))
                        for i in range(2):
                            for w, buf in ((0, qb), (1, kb), (2, vb)):
                                sL.d(s.dma_start(
                                    buf[:, (i % 2) * D:(i % 2 + 1) * D],
                                    qkv_dram[w, i * 128:(i + 1) * 128, :]))
                        for i in range(NT):
                            if i + 2 < NT:
                                s.wait_ge(sDVE2.h, wdC[i])
                                for w, buf in ((0, qb), (1, kb), (2, vb)):
                                    sL.d(s.dma_start(
                                        buf[:, (i % 2) * D:(i % 2 + 1) * D],
                                        qkv_dram[w, (i + 2) * 128:(i + 3) * 128, :]))
                        s.wait_ge(sS.h, sS.n)

        def phase_l1out():
            with ExitStack() as pc:
                psb = lambda n_, s_, d=F32: pc.enter_context(nc.sbuf_tensor(n_, s_, d))
                pps = lambda n_, s_, d=F32: pc.enter_context(nc.psum_tensor(n_, s_, d))
                woT_sb = psb("woT_sb", [128, HC * D], BF16)
                aoT = psb("aoT", [128, HC * S], BF16)
                g1b = psb("g1b", [128, 2 * D])
                x1b = psb("x1b", [128, 2 * D])
                ob = psb("ob", [128, 2 * D])
                tmpG = psb("tmpG", [128, 512])
                ps_s = [pps(f"ps_s{b}", [128, D]) for b in range(2)]

                W0, L0, S0 = sW.n, sL.n, sS.n
                wmmG, wupG = {}, {}
                pe, ev = sPE.n, sEV.n
                for i in range(NT):
                    pe += 1; wmmG[i] = pe
                    ev += 1; wupG[i] = ev

                with nc.Block() as blk:
                    @blk.gpsimd
                    def _(g):
                        for pch in range(HC):
                            sW.d(g.dma_start(
                                woT_sb[:, pch * D:(pch + 1) * D],
                                woT_d[pch * 128:(pch + 1) * 128, :]))
                        g.wait_ge(sW.h, sW.n)

                    @blk.sync
                    def _(s):
                        for p in range(NPAIR):
                            sL.d(s.dma_start_transpose(
                                aoT[:, p * S:(p + 1) * S], ao_hbm[p][:, :]))
                        for i in range(2):
                            sL.d(s.dma_start(
                                g1b[:, (i % 2) * D:(i % 2 + 1) * D],
                                g_scr[i * 128:(i + 1) * 128, :]))
                            sL.d(s.dma_start(
                                x1b[:, (i % 2) * D:(i % 2 + 1) * D],
                                x_resid[i * 128:(i + 1) * 128, :]))
                        for i in range(NT):
                            s.wait_ge(sEV.h, wupG[i])
                            sS.d(s.dma_start(
                                out_d[i * 128:(i + 1) * 128, :],
                                ob[:, (i % 2) * D:(i % 2 + 1) * D]))
                            if i + 2 < NT:
                                s.wait_ge(sEV.h, wupG[i])
                                sL.d(s.dma_start(
                                    g1b[:, (i % 2) * D:(i % 2 + 1) * D],
                                    g_scr[(i + 2) * 128:(i + 3) * 128, :]))
                                sL.d(s.dma_start(
                                    x1b[:, (i % 2) * D:(i % 2 + 1) * D],
                                    x_resid[(i + 2) * 128:(i + 3) * 128, :]))
                        s.wait_ge(sS.h, sS.n)

                    @blk.tensor
                    def _(t):
                        t.wait_ge(sW.h, sW.n)
                        t.wait_ge(sL.h, L0 + 16 * NPAIR)
                        for i in range(NT):
                            if i >= 2:
                                t.wait_ge(sEV.h, wupG[i - 2])
                            for nh in range(2):
                                for pch in range(HC):
                                    t.matmul(ps_s[i % 2][:, nh * 512:(nh + 1) * 512],
                                             aoT[:, pch * S + i * 128:
                                                  pch * S + (i + 1) * 128],
                                             woT_sb[:, pch * D + nh * 512:
                                                    pch * D + (nh + 1) * 512],
                                             start=(pch == 0), stop=False)
                                inst = t.matmul(
                                    ps_s[i % 2][:, nh * 512:(nh + 1) * 512],
                                    onr_bf[:, :],
                                    bo_bf[0:1, nh * 512:(nh + 1) * 512],
                                    start=False, stop=True)
                            sPE.inc(inst)

                    @blk.vector
                    def _(v):
                        for i in range(NT):
                            par = i % 2
                            v.wait_ge(sPE.h, wmmG[i])
                            v.wait_ge(sL.h, L0 + 16 * NPAIR + 32 * (i + 1))
                            if i >= 2:
                                v.wait_ge(sS.h, S0 + 16 * (i - 1))
                            for nh in range(2):
                                v.tensor_tensor(
                                    tmpG[:, :],
                                    ps_s[par][:, nh * 512:(nh + 1) * 512],
                                    g1b[:, par * D + nh * 512:
                                        par * D + (nh + 1) * 512], ALU.mult)
                                inst = v.tensor_add(
                                    ob[:, par * D + nh * 512:
                                       par * D + (nh + 1) * 512],
                                    tmpG[:, :],
                                    x1b[:, par * D + nh * 512:
                                        par * D + (nh + 1) * 512])
                            sEV.inc(inst)

        phase_ln(0)
        if n_phases >= 2:
            phase_xnf()
            phase_combine()
        if n_phases >= 3:
            phase_l0ffn()
        if n_phases >= 4:
            phase_l0attn()
        if n_phases >= 5:
            phase_ln(1)
            phase_xnf()
        if n_phases >= 6:
            phase_l1ffn()
        if n_phases >= 7:
            phase_l1attn()
        if n_phases >= 8:
            phase_l1out()

        if debug:
            with nc.Block() as blk:
                dbgA = dt("dbgA", [128, NT * 8], F32, "ExternalOutput")
                dbgB = dt("dbgB", [8, NT * 128], F32, "ExternalOutput")
                dbgC = dt("dbgC", [128, HC * S], F32, "ExternalOutput")
                dbg_xnd = dt("dbg_xnd", [S, D], F32, "ExternalOutput")
                dbg_xres = dt("dbg_xres", [S, D], F32, "ExternalOutput")
                dbg_g = dt("dbg_g", [S, D], F32, "ExternalOutput")
                dbg_ao = dt("dbg_ao", [NPAIR, S, 128], F32, "ExternalOutput")

                @blk.gpsimd
                def _(g):
                    sW.d(g.dma_start(dbgA[:, :], logits_sb[:, :]))
                    sW.d(g.dma_start(dbg_xnd[:, :], xn_dram[:, :]))
                    if n_phases >= 2:
                        sW.d(g.dma_start(dbgB[:, :], combT[:, :]))
                        sW.d(g.dma_start(dbgC[:, :], xn_f[:, :]))
                    if n_phases >= 3:
                        sW.d(g.dma_start(dbg_g[:, :], g_scr[:, :]))
                    if n_phases >= 4:
                        sW.d(g.dma_start(dbg_xres[:, :], x_resid[:, :]))
                    if n_phases >= 7:
                        for p in range(NPAIR):
                            sW.d(g.dma_start(dbg_ao[p, :, :], ao_hbm[p, :, :]))
                    g.wait_ge(sW.h, sW.n)

    return nc


def _layout_inputs(inputs):
    f = np.ascontiguousarray
    esel = np.zeros((8, 1024), np.float32)
    for e in range(8):
        esel[e, e * 128:(e + 1) * 128] = 1.0
    common = {
        "wl": f(np.asarray(inputs["L0_attn_w_lhs"], np.float32).reshape(8, 128).T),
        "wr": f(np.asarray(inputs["L0_attn_w_rhs"], np.float32).reshape(8, 128).T),
        "gwT": f(np.asarray(inputs["L0_ffn_gate_weight"], np.float32).T),
        "wgT": f(np.transpose(np.asarray(inputs["L0_ffn_w_gate"], np.float32),
                              (0, 2, 1))),
        "wuT": f(np.transpose(np.asarray(inputs["L0_ffn_w_up"], np.float32),
                              (0, 2, 1))),
        "wdT": f(np.transpose(np.asarray(inputs["L0_ffn_w_down"], np.float32),
                              (0, 2, 1))),
        "wsgT": f(np.asarray(inputs["L0_ffn_ws_gate"], np.float32).T),
        "wsuT": f(np.asarray(inputs["L0_ffn_ws_up"], np.float32).T),
        "wsdT": f(np.asarray(inputs["L0_ffn_ws_down"], np.float32).T),
        "wqT": f(np.asarray(inputs["L1_attn_Wq"], np.float32).T),
        "wkT": f(np.asarray(inputs["L1_attn_Wk"], np.float32).T),
        "wvTs": f(np.asarray(inputs["L1_attn_Wv"], np.float32).T
                  .reshape(1024, 16, 64).transpose(0, 2, 1).reshape(1024, 1024)),
        "woT": f(np.asarray(inputs["L1_attn_Wo"], np.float32).T),
        "bqrow": f(np.asarray(inputs["L1_attn_bq"], np.float32).reshape(1, 1024)),
        "bkrow": f(np.asarray(inputs["L1_attn_bk"], np.float32).reshape(1, 1024)),
        "bvsrow": f(np.asarray(inputs["L1_attn_bv"], np.float32)
                    .reshape(16, 64).T.reshape(1, 1024)),
        "borow": f(np.asarray(inputs["L1_attn_bo"], np.float32).reshape(1, 1024)),
        "cb": f(np.asarray(inputs["L1_ffn_CB"], np.float32)),
        "scb": f(np.asarray(inputs["L1_ffn_SCB"], np.float32).reshape(8, 128).T),
        "b1row": f(np.asarray(inputs["L1_ffn_bias"], np.float32).reshape(1, 1024)),
        "ident": np.eye(128, dtype=np.float32),
        "onesrow": np.ones((1, 128), np.float32),
        "onescol": np.ones((128, 1), np.float32),
        "esel": esel,
    }
    x = np.asarray(inputs["L0_attn_x"], np.float32)
    return [dict(common, x=f(x[c])) for c in range(8)]


def kernel(**inputs):
    in_maps = _layout_inputs(inputs)
    nc = build_nc()
    res = run_bass_kernel_spmd(nc, in_maps, core_ids=list(range(8)))
    return np.stack([r["out"] for r in res.results], axis=0)


# version 09
# speedup vs baseline: 1.0682x; 1.0682x over previous
import numpy as np
from contextlib import ExitStack

import concourse.bass as bass
import concourse.mybir as mybir
from concourse.bass_utils import run_bass_kernel_spmd

F32 = mybir.dt.float32
BF16 = mybir.dt.bfloat16
AF = mybir.ActivationFunctionType
ALU = mybir.AluOpType
AX = mybir.AxisListType

S = 4096
D = 1024
HC = 8
NT = 32
NKC = 32
QG = 512
NQG = S // QG
NEXP = 8
NIC = 4
NPAIR = 8
EPS = 1e-5
NPAIRS5 = 5
QN = 4


def ordered_prev(d, key, back=1):
    ks = list(d.keys())
    i = ks.index(key)
    return d[ks[i - back]] if i - back >= 0 else None


class Sem:
    def __init__(self, nc, ctx, name):
        self.h = ctx.enter_context(nc.semaphore(name))
        self.n = 0

    def inc(self, inst, k=1):
        inst.then_inc(self.h, k)
        self.n += k
        return self.n

    def d(self, inst):
        return self.inc(inst, 16)


def build_nc(n_phases=99, debug=False):
    nc = bass.Bass("TRN2", target_bir_lowering=False, dynamic_dma_scratch_size=16384, detect_race_conditions=False)

    dt = lambda name, shape, dtype=F32, kind="ExternalInput": nc.dram_tensor(
        name, shape, dtype, kind=kind)
    x_in = dt("x", [S, D])
    wl_d = dt("wl", [128, 8]);  wr_d = dt("wr", [128, 8])
    gwT_d = dt("gwT", [D, 8])
    wgT_d = dt("wgT", [NEXP, D, 512]); wuT_d = dt("wuT", [NEXP, D, 512])
    wdT_d = dt("wdT", [NEXP, 512, D])
    wsgT_d = dt("wsgT", [D, D]); wsuT_d = dt("wsuT", [D, D]); wsdT_d = dt("wsdT", [D, D])
    wqT_d = dt("wqT", [D, D]); wkT_d = dt("wkT", [D, D])
    wvTs_d = dt("wvTs", [D, D]); woT_d = dt("woT", [D, D])
    bqr_d = dt("bqrow", [1, D]); bkr_d = dt("bkrow", [1, D])
    bvs_d = dt("bvsrow", [1, D]); bo_d = dt("borow", [1, D])
    cb_d = dt("cb", [D, D]); scb_d = dt("scb", [128, 8]); b1_d = dt("b1row", [1, D])
    id_d = dt("ident", [128, 128])
    onr_d = dt("onesrow", [1, 128]); onc_d = dt("onescol", [128, 1])
    esel_d = dt("esel", [8, 1024])
    out_d = dt("out", [S, D], F32, "ExternalOutput")

    xn_dram = dt("xn_dram", [S, D], BF16, "Internal")
    qkv_dram = dt("qkv_dram", [3, S, D], BF16, "Internal")
    wgT_s = dt("wgT_s", [NEXP, D, 512], BF16, "Internal")
    wuT_s = dt("wuT_s", [NEXP, D, 512], BF16, "Internal")
    wdT_s = dt("wdT_s", [NEXP, 512, D], BF16, "Internal")
    wsgT_s = dt("wsgT_s", [D, D], BF16, "Internal")
    wsuT_s = dt("wsuT_s", [D, D], BF16, "Internal")
    wsdT_s = dt("wsdT_s", [D, D], BF16, "Internal")
    x_resid = dt("x_resid", [S, D], F32, "Internal")
    g_scr = dt("g_scr", [S, D], F32, "Internal")
    ao_hbm = dt("ao_hbm", [NPAIR, S, 128], BF16, "Internal")

    with ExitStack() as ctx:
        sb = lambda name, shape, d=F32: ctx.enter_context(nc.sbuf_tensor(name, shape, d))
        sem = lambda name: Sem(nc, ctx, name)

        xn_f = sb("xn_f", [128, HC * S], BF16)
        logits_sb = sb("logits", [128, NT * 8])
        combT = sb("combT", [8, NT * 128], BF16)
        id_bf = sb("id_bf", [128, 128], BF16)
        id_f = sb("id_f", [128, 128])
        onr_bf = sb("onr_bf", [1, 128], BF16)
        onr_f = sb("onr_f", [1, 128])
        onc_bf = sb("onc_bf", [128, 1], BF16)
        esel_bf = sb("esel_bf", [8, 1024], BF16)
        w_vec = sb("w_vec", [128, 8])
        bndsel = sb("bndsel", [128, 16], BF16)
        gwT_sb = sb("gwT_sb", [128, HC * 8])
        gwst = sb("gwst", [128, HC * 16], BF16)
        gwlo_t = sb("gwlo_t", [128, HC * 8])
        scb_sb = sb("scb_sb", [128, 8])
        bqr_bf = sb("bqr_bf", [1, D], BF16); bkr_bf = sb("bkr_bf", [1, D], BF16)
        bvs_bf = sb("bvs_bf", [1, D], BF16); bo_bf = sb("bo_bf", [1, D], BF16)
        b1_bf = sb("b1_bf", [1, D], BF16)
        wlr = sb("wlr", [128, 16])
        eps_sb = sb("eps_sb", [128, 1])

        sW = sem("sW")
        sL = sem("sL")
        sS = sem("sS")
        sPE = sem("sPE")
        sPE2 = sem("sPE2")
        sPE3 = sem("sPE3")
        sACT = sem("sACT")
        sACT2 = sem("sACT2")
        sDVE = sem("sDVE")
        sDVE2 = sem("sDVE2")
        sEV = sem("sEV")
        sGD = sem("sGD")

        def vg(eng, inst):
            n = sGD.inc(inst)
            eng.wait_ge(sGD.h, n)

        with nc.Block() as blk:
            @blk.gpsimd
            def _(g):
                for dst, src in [(id_bf, id_d), (onr_bf, onr_d), (onc_bf, onc_d),
                                 (esel_bf, esel_d), (bqr_bf, bqr_d),
                                 (bkr_bf, bkr_d), (bvs_bf, bvs_d),
                                 (bo_bf, bo_d), (b1_bf, b1_d)]:
                    sW.d(g.dma_start(dst[:, :], src[:, :]))
                g.wait_ge(sW.h, sW.n)

            @blk.sync
            def _(s):
                for dst, src in [(id_f, id_d), (onr_f, onr_d),
                                 (scb_sb, scb_d)]:
                    sL.d(s.dma_start(dst[:, :], src[:, :]))
                sL.d(s.dma_start(wlr[:, 0:8], wl_d[:, :]))
                sL.d(s.dma_start(wlr[:, 8:16], wr_d[:, :]))
                for c in range(HC):
                    sL.d(s.dma_start(gwT_sb[:, c * 8:(c + 1) * 8],
                                       gwT_d[c * 128:(c + 1) * 128, :]))
                s.wait_ge(sL.h, sL.n)

            @blk.vector
            def _(v):
                v.wait_ge(sL.h, sL.n)
                v.wait_ge(sW.h, sW.n)
                v.memset(eps_sb[:, :], EPS)
                vg(v, v.tensor_mul(w_vec[:, :], wlr[:, 0:8], wlr[:, 8:16]))
                for c in range(HC):
                    v.tensor_copy(bndsel[:, 2 * c:2 * c + 1], onc_bf[:, :])
                    v.tensor_mul(bndsel[:, 2 * c + 1:2 * c + 2],
                                 w_vec[:, c:c + 1], w_vec[:, c:c + 1])
                for c in range(HC):
                    vg(v, v.tensor_copy(gwst[:, c * 16:c * 16 + 8],
                                        gwT_sb[:, c * 8:(c + 1) * 8]))
                    vg(v, v.tensor_copy(gwlo_t[:, c * 8:(c + 1) * 8],
                                        gwst[:, c * 16:c * 16 + 8]))
                    v.tensor_tensor(gwst[:, c * 16 + 8:(c + 1) * 16],
                                    gwT_sb[:, c * 8:(c + 1) * 8],
                                    gwlo_t[:, c * 8:(c + 1) * 8], ALU.subtract)

        def phase_ln(layer):
            src = x_in if layer == 0 else x_resid
            with ExitStack() as pc:
                psb = lambda n_, s_, d=F32: pc.enter_context(nc.sbuf_tensor(n_, s_, d))
                pps = lambda n_, s_, d=F32: pc.enter_context(nc.psum_tensor(n_, s_, d))
                xt = psb(f"ln{layer}_xt", [128, 2 * D])
                xc = psb(f"ln{layer}_xc", [128, D])
                sq = psb(f"ln{layer}_sq", [128, D])
                st4 = psb(f"ln{layer}_st", [128, 8])
                xn32 = psb(f"ln{layer}_xn32", [128, D])
                xnbf = psb(f"ln{layer}_xnbf", [128, 2 * D], BF16)
                L0, S0 = sL.n, sS.n
                wa, wb, wc, wp, wq_ = ({} for _ in range(5))
                dn, an = sDVE.n, sACT.n
                for i in range(NT):
                    dn += 1; wa[i] = dn
                    an += 1; wp[i] = an
                    dn += 1; wb[i] = dn
                    an += 1; wq_[i] = an
                    dn += 1; wc[i] = dn

                with nc.Block() as blk:
                    if layer == 0:
                        @blk.gpsimd
                        def _(g):
                            for a_, b_ in ((wsgT_d, wsgT_s), (wsuT_d, wsuT_s),
                                           (wsdT_d, wsdT_s)):
                                sW.d(g.dma_start(b_[:, :], a_[:, :]))
                            for e in range(NEXP):
                                sW.d(g.dma_start(wgT_s[e], wgT_d[e]))
                                sW.d(g.dma_start(wuT_s[e], wuT_d[e]))
                                sW.d(g.dma_start(wdT_s[e], wdT_d[e]))
                            g.wait_ge(sW.h, sW.n)

                    @blk.sync
                    def _(s):
                        for i in range(2):
                            sL.d(s.dma_start(xt[:, (i % 2) * D:(i % 2 + 1) * D],
                                             src[i * 128:(i + 1) * 128, :]))
                        for i in range(NT):
                            s.wait_ge(sDVE.h, wc[i])
                            sS.d(s.dma_start(xn_dram[i * 128:(i + 1) * 128, :],
                                             xnbf[:, (i % 2) * D:(i % 2 + 1) * D]))
                            if i + 2 < NT:
                                s.wait_ge(sDVE.h, wa[i])
                                sL.d(s.dma_start(
                                    xt[:, (i % 2) * D:(i % 2 + 1) * D],
                                    src[(i + 2) * 128:(i + 3) * 128, :]))
                        s.wait_ge(sS.h, sS.n)

                    @blk.vector
                    def _(v):
                        for i in range(NT):
                            xti = xt[:, (i % 2) * D:(i % 2 + 1) * D]
                            v.wait_ge(sL.h, L0 + 16 * (i + 1))
                            if i >= 1:
                                v.wait_ge(sACT.h, wp[i - 1])
                            vg(v, v.reduce_sum(st4[:, 0:1], xti, AX.X))
                            vg(v, v.tensor_scalar(st4[:, 1:2], st4[:, 0:1],
                                                  1.0 / D, None, ALU.mult))
                            sDVE.inc(v.tensor_scalar(xc[:, :], xti, st4[:, 1:2],
                                                     None, ALU.subtract))
                            v.wait_ge(sACT.h, wp[i])
                            sDVE.inc(v.reduce_sum(st4[:, 2:3], sq[:, :], AX.X))
                            v.wait_ge(sACT.h, wq_[i])
                            if i >= 2:
                                v.wait_ge(sS.h, S0 + 16 * (i - 1))
                            v.tensor_scalar(xn32[:, :], xc[:, :], st4[:, 3:4],
                                            None, ALU.mult)
                            sDVE.inc(v.tensor_copy(
                                xnbf[:, (i % 2) * D:(i % 2 + 1) * D], xn32[:, :]))

                    @blk.scalar
                    def _(sc):
                        for i in range(NT):
                            sc.wait_ge(sDVE.h, wa[i])
                            sACT.inc(sc.activation(sq[:, :], xc[:, :], AF.Square))
                            sc.wait_ge(sDVE.h, wb[i])
                            sc.activation(st4[:, 3:4], st4[:, 2:3], AF.Ln,
                                          bias=eps_sb[:, 0:1], scale=1.0 / D)
                            sACT.inc(sc.activation(st4[:, 3:4], st4[:, 3:4],
                                                   AF.Exp, scale=-0.5))

        def phase_xnf():
            with nc.Block() as blk:
                @blk.sync
                def _(s):
                    for c in range(HC):
                        sL.d(s.dma_start_transpose(
                            xn_f[:, c * S:(c + 1) * S],
                            xn_dram[:, c * 128:(c + 1) * 128]))
                    s.wait_ge(sL.h, sL.n)

        def phase_combine():
            with ExitStack() as pc:
                cw = pc.enter_context(nc.sbuf_tensor("cw", [128, 32], F32))
                comb = pc.enter_context(nc.sbuf_tensor("comb", [128, 8], F32))
                lgt = pc.enter_context(nc.sbuf_tensor("lgt", [128, 16], F32))
                ps_cT = pc.enter_context(nc.psum_tensor("ps_cT", [8, 128], F32))
                ps_lg = [pc.enter_context(nc.psum_tensor(f"ps_lg{b}", [128, 16],
                                                         F32)) for b in range(2)]
                wd0, wd1, wd2, wd3, wa1, wp1, wlg, wle = ({} for _ in range(8))
                dn, an, pn = sDVE.n, sACT.n, sPE.n
                pe2, ae2 = sPE2.n, sACT2.n
                for i in range(NT):
                    pe2 += 1; wlg[i] = pe2
                    ae2 += 1; wle[i] = ae2
                    dn += 1; wd0[i] = dn
                    dn += 1; wd1[i] = dn
                    an += 1; wa1[i] = an
                    dn += 1; wd2[i] = dn
                    pn += 1; wp1[i] = pn
                    dn += 1; wd3[i] = dn

                with nc.Block() as blk:
                    @blk.vector
                    def _(v):
                        for i in range(NT):
                            par = i % 2
                            L = logits_sb[:, i * 8:(i + 1) * 8]
                            v.wait_ge(sACT2.h, wle[i])
                            sDVE.inc(v.tensor_add(
                                L, ps_lg[par][:, 0:8],
                                lgt[:, par * 8:(par + 1) * 8]))
                            v.wait_ge(sDVE.h, wd0[i])
                            m1, m2 = cw[:, 0:1], cw[:, 1:2]
                            w1, w2 = cw[:, 2:3], cw[:, 3:4]
                            den, rinv = cw[:, 4:5], cw[:, 5:6]
                            c1, c2 = cw[:, 6:7], cw[:, 7:8]
                            msk1, msk2 = cw[:, 8:16], cw[:, 16:24]
                            Lm = cw[:, 24:32]
                            vg(v, v.reduce_max(m1, L, AX.X))
                            vg(v, v.tensor_scalar(msk1, L, m1, None,
                                                  ALU.is_equal))
                            vg(v, v.tensor_scalar(Lm, msk1, -1e30, None,
                                                  ALU.mult))
                            vg(v, v.tensor_add(Lm, Lm, L))
                            inst = v.reduce_max(m2, Lm, AX.X)
                            sDVE.inc(inst)
                            v.wait_ge(sDVE.h, sDVE.n)
                            v.tensor_scalar(msk2, Lm, m2, None, ALU.is_equal)
                            v.wait_ge(sACT.h, wa1[i])
                            vg(v, v.tensor_add(den, w1, w2))
                            vg(v, v.tensor_scalar(den, den, 1e-20, None, ALU.add))
                            vg(v, v.reciprocal(rinv, den))
                            vg(v, v.tensor_scalar(c1, w1, rinv, 2.5, ALU.mult,
                                                  ALU.mult))
                            vg(v, v.tensor_scalar(c2, w2, rinv, 2.5, ALU.mult,
                                                  ALU.mult))
                            if i >= 1:
                                v.wait_ge(sPE.h, wp1[i - 1])
                            v.tensor_scalar(comb[:, :], msk1, c1, None, ALU.mult)
                            vg(v, v.tensor_scalar(Lm, msk2, c2, None, ALU.mult))
                            sDVE.inc(v.tensor_add(comb[:, :], comb[:, :], Lm))
                            v.wait_ge(sPE.h, wp1[i])
                            sDVE.inc(v.tensor_copy(
                                combT[:, i * 128:(i + 1) * 128], ps_cT[:, :]))

                    @blk.scalar
                    def _(sc):
                        for i in range(NT):
                            par = i % 2
                            if i >= 2:
                                sc.wait_ge(sDVE.h, wd0[i - 2])
                            sc.wait_ge(sPE2.h, wlg[i])
                            sACT2.inc(sc.copy(lgt[:, par * 8:(par + 1) * 8],
                                              ps_lg[par][:, 8:16]))
                            sc.wait_ge(sDVE.h, wd1[i])
                            sc.activation(cw[:, 2:3], cw[:, 0:1], AF.Sigmoid)
                            sACT.inc(sc.activation(cw[:, 3:4], cw[:, 1:2],
                                                   AF.Sigmoid))

                    @blk.tensor
                    def _(t):
                        for i in range(NT):
                            par = i % 2
                            if i >= 2:
                                t.wait_ge(sDVE.h, wd0[i - 2])
                                t.wait_ge(sACT2.h, wle[i - 2])
                            for hc in range(HC):
                                inst = t.matmul(
                                    ps_lg[par][:, :],
                                    xn_f[:, hc * S + i * 128:
                                         hc * S + (i + 1) * 128],
                                    gwst[:, hc * 16:(hc + 1) * 16],
                                    start=(hc == 0), stop=(hc == HC - 1))
                            sPE2.inc(inst)
                            t.wait_ge(sDVE.h, wd2[i])
                            if i >= 1:
                                t.wait_ge(sDVE.h, wd3[i - 1])
                            sPE.inc(t.transpose(ps_cT[:, :], comb[:, :],
                                                id_f[:, :]))

        def phase_l0ffn():
            with ExitStack() as pc:
                psb = lambda n_, s_, d=F32: pc.enter_context(nc.sbuf_tensor(n_, s_, d))
                pps = lambda n_, s_, d=F32: pc.enter_context(nc.psum_tensor(n_, s_, d))
                acc = psb("acc", [128, 4 * D])
                wgt = [psb(f"wgt{b}", [128, 2 * HC * 512], BF16) for b in range(2)]
                wut = [psb(f"wut{b}", [128, 2 * HC * 512], BF16) for b in range(2)]
                wdt = [psb("wdt", [128, 2 * NIC * D], BF16)] * 2
                hmS = psb("hmS", [128, 2 * NIC * 512], BF16)
                silu_sb = psb("silu_b", [128, 512], BF16)
                bc_sb = psb("bc_sb", [128, 512], BF16)
                ps_g = pps("ps_g", [128, 512])
                ps_u = pps("ps_u", [128, 512])
                ps_bc = pps("ps_bc", [128, 512])
                ps_r = [pps(f"ps_r{b}", [128, D]) for b in range(2)]

                W0, S0, L0W = sW.n, sS.n, sL.n
                NW = 2 * (2 * HC + NIC)
                QN8 = 8
                NGI = QN8 * NPAIRS5

                wg_, wu_, wbc, wdn, wsil, wbce, whm, wev, wfs = \
                    {}, {}, {}, {}, {}, {}, {}, {}, {}
                pe, pe2, pe3 = sPE.n, sPE2.n, sPE3.n
                act, act2, dve, ev = sACT.n, sACT2.n, sDVE.n, sEV.n
                for gi in range(NGI):
                    q, p = divmod(gi, NPAIRS5)
                    for tg in range(1):
                        for slot in range(2):
                            if p > 0:
                                pe2 += 1; wbc[gi, tg, slot] = pe2
                                act2 += 1; wbce[gi, tg, slot] = act2
                            for ic in range(NIC):
                                pe += 1; wg_[gi, tg, slot, ic] = pe
                                act += 1; wsil[gi, tg, slot, ic] = act
                                pe2 += 1; wu_[gi, tg, slot, ic] = pe2
                                dve += 1; whm[gi, tg, slot, ic] = dve
                        for t1 in range(4):
                            pe3 += 1; wdn[gi, tg, t1] = pe3
                            ev += 1; wev[gi, tg, t1] = ev
                    if p == NPAIRS5 - 1:
                        for tl in range(4):
                            act2 += 1; wfs[q, tl] = act2
                wpd = {gi: wdn[gi, 0, 3] for gi in range(NGI)}
                stq = {q: S0 + 16 * 4 * (q + 1) for q in range(QN8)}

                def bc_prev_reader(gi, tg, slot):
                    ks = [k for k in whm.keys()
                          if k < (gi, tg, slot, 0) and (k[0] % NPAIRS5) != 0]
                    return whm[ks[-1]] if ks else None

                def wslot_aps(b, p, slot):
                    res = []
                    for hc in range(HC):
                        gd = wgt[b][:, slot * HC * 512 + hc * 512:
                                    slot * HC * 512 + (hc + 1) * 512]
                        ud = wut[b][:, slot * HC * 512 + hc * 512:
                                    slot * HC * 512 + (hc + 1) * 512]
                        if p == 0:
                            gs = wsgT_s[hc * 128:(hc + 1) * 128,
                                        slot * 512:(slot + 1) * 512]
                            us = wsuT_s[hc * 128:(hc + 1) * 128,
                                        slot * 512:(slot + 1) * 512]
                        else:
                            e = (p - 1) * 2 + slot
                            gs = wgT_s[e, hc * 128:(hc + 1) * 128, :]
                            us = wuT_s[e, hc * 128:(hc + 1) * 128, :]
                        res.append((gd, gs, False)); res.append((ud, us, False))
                    for ic in range(NIC):
                        dd = wdt[b][:, slot * NIC * D + ic * D:
                                    slot * NIC * D + (ic + 1) * D]
                        if p == 0:
                            ds = wsdT_s[slot * 512 + ic * 128:
                                        slot * 512 + (ic + 1) * 128, :]
                        else:
                            e = (p - 1) * 2 + slot
                            ds = wdT_s[e, ic * 128:(ic + 1) * 128, :]
                        res.append((dd, ds, True))
                    return res

                with nc.Block() as blk:
                    @blk.gpsimd
                    def _(g):
                        for gi in range(NGI):
                            q, p = divmod(gi, NPAIRS5)
                            if gi >= 2:
                                g.wait_ge(sPE3.h, wpd[gi - 2])
                            s1 = wslot_aps(gi % 2, p, 1)
                            for dst, srcap, isd in s1:
                                if not isd:
                                    sW.d(g.dma_start(dst, srcap))
                            if gi >= 1:
                                g.wait_ge(sPE3.h, wpd[gi - 1])
                            for slot in range(2):
                                for dst, srcap, isd in wslot_aps(gi % 2, p, slot):
                                    if isd:
                                        sW.d(g.dma_start(dst, srcap))
                        g.wait_ge(sW.h, sW.n)

                    @blk.sync
                    def _(s):
                        for gi in range(NGI):
                            q, p = divmod(gi, NPAIRS5)
                            if gi >= 2:
                                s.wait_ge(sPE3.h, wpd[gi - 2])
                            for dst, srcap, isd in wslot_aps(gi % 2, p, 0):
                                if not isd:
                                    sL.d(s.dma_start(dst, srcap))
                        s.wait_ge(sL.h, sL.n)

                    @blk.tensor
                    def _(t):
                        for gi in range(NGI):
                            q, p = divmod(gi, NPAIRS5)
                            b = gi % 2
                            t.wait_ge(sL.h, L0W + 16 * 16 * (gi + 1))
                            t.wait_ge(sW.h, W0 + 16 * (24 * gi + 16))
                            for tg in range(1):
                                tcol = q * 512
                                for slot in range(2):
                                    if p > 0:
                                        e = (p - 1) * 2 + slot
                                        prev = ordered_prev(wbce, (gi, tg, slot))
                                        if prev is not None:
                                            t.wait_ge(sACT2.h, prev)
                                        sPE2.inc(t.matmul(
                                            ps_bc[:, :],
                                            esel_bf[:, e * 128:(e + 1) * 128],
                                            combT[:, tcol:tcol + 512],
                                            start=True, stop=True))
                                    for ic in range(NIC):
                                        prev = ordered_prev(wsil, (gi, tg, slot, ic))
                                        if prev is not None:
                                            t.wait_ge(sACT.h, prev)
                                        for hc in range(HC):
                                            col = (slot * HC * 512 + hc * 512
                                                   + ic * 128)
                                            inst = t.matmul(
                                                ps_g[:, :], wgt[b][:, col:col + 128],
                                                xn_f[:, hc * S + tcol:
                                                     hc * S + tcol + 512],
                                                start=(hc == 0), stop=(hc == HC - 1))
                                        sPE.inc(inst)
                                        prev = ordered_prev(whm, (gi, tg, slot, ic))
                                        if prev is not None:
                                            t.wait_ge(sDVE.h, prev)
                                        for hc in range(HC):
                                            col = (slot * HC * 512 + hc * 512
                                                   + ic * 128)
                                            inst = t.matmul(
                                                ps_u[:, :], wut[b][:, col:col + 128],
                                                xn_f[:, hc * S + tcol:
                                                     hc * S + tcol + 512],
                                                start=(hc == 0), stop=(hc == HC - 1))
                                        sPE2.inc(inst)
                                for t1 in range(4):
                                    if tg == 0 and t1 == 0:
                                        t.wait_ge(sW.h, W0 + 16 * 24 * (gi + 1))
                                    t.wait_ge(sDVE.h, whm[gi, tg, 1, NIC - 1])
                                    prev = ordered_prev(wev, (gi, tg, t1), back=2)
                                    if prev is not None:
                                        t.wait_ge(sEV.h, prev)
                                    for slot in range(2):
                                        for ic in range(NIC):
                                            for nh in range(2):
                                                inst = t.matmul(
                                                    ps_r[t1 % 2][:, nh * 512:
                                                                 (nh + 1) * 512],
                                                    hmS[:, slot * NIC * 512
                                                        + ic * 512 + t1 * 128:
                                                        slot * NIC * 512
                                                        + ic * 512 + (t1 + 1) * 128],
                                                    wdt[b][:, slot * NIC * D + ic * D
                                                           + nh * 512:
                                                           slot * NIC * D + ic * D
                                                           + (nh + 1) * 512],
                                                    start=(slot == 0 and ic == 0),
                                                    stop=(slot == 1 and
                                                          ic == NIC - 1))
                                    sPE3.inc(inst)

                    @blk.scalar
                    def _(sc):
                        for gi in range(NGI):
                            q, p = divmod(gi, NPAIRS5)
                            for tg in range(1):
                                for slot in range(2):
                                    if p > 0:
                                        sc.wait_ge(sPE2.h, wbc[gi, tg, slot])
                                        prev = bc_prev_reader(gi, tg, slot)
                                        if prev is not None:
                                            sc.wait_ge(sDVE.h, prev)
                                        sACT2.inc(sc.copy(bc_sb[:, :], ps_bc[:, :]))
                                    for ic in range(NIC):
                                        sc.wait_ge(sPE.h, wg_[gi, tg, slot, ic])
                                        prev = ordered_prev(whm, (gi, tg, slot, ic))
                                        if prev is not None:
                                            sc.wait_ge(sDVE.h, prev)
                                        sACT.inc(sc.activation(
                                            silu_sb[:, :], ps_g[:, :], AF.Silu))
                            if p == NPAIRS5 - 1:
                                for tl in range(4):
                                    sc.wait_ge(sEV.h, wev[gi, 0, tl])
                                    a_sl = acc[:, tl * D:(tl + 1) * D]
                                    sACT2.inc(sc.activation(a_sl, a_sl, AF.Silu))
                                    sS.d(sc.dma_start(
                                        g_scr[q * 512 + tl * 128:
                                              q * 512 + (tl + 1) * 128, :],
                                        a_sl))
                        sc.wait_ge(sS.h, sS.n)

                    @blk.vector
                    def _(v):
                        for gi in range(NGI):
                            q, p = divmod(gi, NPAIRS5)
                            for tg in range(1):
                                for slot in range(2):
                                    for ic in range(NIC):
                                        v.wait_ge(sACT.h, wsil[gi, tg, slot, ic])
                                        v.wait_ge(sPE2.h, wu_[gi, tg, slot, ic])
                                        if ic == 0 and slot == 0:
                                            if gi >= 1:
                                                v.wait_ge(sPE3.h, wdn[gi - 1, 0, 3])
                                        dst = hmS[:, slot * NIC * 512 + ic * 512:
                                                  slot * NIC * 512 + (ic + 1) * 512]
                                        if p > 0:
                                            v.wait_ge(sACT2.h, wbce[gi, tg, slot])
                                            v.tensor_tensor(dst, silu_sb[:, :],
                                                            ps_u[:, :], ALU.mult)
                                            sDVE.inc(v.tensor_tensor(
                                                dst, dst, bc_sb[:, :], ALU.mult))
                                        else:
                                            sDVE.inc(v.tensor_tensor(
                                                dst, silu_sb[:, :], ps_u[:, :],
                                                ALU.mult))
                                for t1 in range(4):
                                    v.wait_ge(sPE3.h, wdn[gi, tg, t1])
                                    tl = tg * 4 + t1
                                    a_sl = acc[:, tl * D:(tl + 1) * D]
                                    if p == 0:
                                        if q > 0:
                                            v.wait_ge(sS.h, stq[q - 1])
                                        sEV.inc(v.tensor_copy(a_sl,
                                                              ps_r[t1 % 2][:, :]))
                                    else:
                                        sEV.inc(v.tensor_tensor(
                                            a_sl, a_sl, ps_r[t1 % 2][:, :],
                                            ALU.add))



        def phase_l0attn():
            QG0 = 256
            NQG0 = S // QG0
            with ExitStack() as pc:
                psb = lambda n_, s_, d=F32: pc.enter_context(nc.sbuf_tensor(n_, s_, d))
                pps = lambda n_, s_, d=F32: pc.enter_context(nc.psum_tensor(n_, s_, d))
                xnt = psb("xnt", [128, NKC * D], BF16)
                pT = [psb(f"pT0{b_}", [128, NKC * QG0], BF16) for b_ in range(2)]
                qts = psb("qts", [128, HC * QG0], BF16)
                sq2 = qts
                bnd = psb("bnd", [1, 8])
                bnd2 = psb("bnd2", [2, 16])
                bias0 = psb("bias0", [128, 1])
                rinv_sb = psb("rinv0", [128, 1])
                gbuf = psb("gbuf", [128, 2 * D])
                xbuf = psb("xbuf", [128, 2 * D])
                ps_sT = [pps(f"ps_sT{b_}", [128, 1024]) for b_ in range(2)]
                ps_a = pps("ps_a0", [128, 512])
                ps_b = pps("ps_b0", [128, 513])
                ps_bnd = pps("ps_bnd", [2, 512])

                L0, S0 = sL.n, sS.n
                wsqd, wbmm, wbrd = {}, {}, {}
                act2, pe3, dve2 = sACT2.n, sPE3.n, sDVE2.n
                for ts in range(8):
                    for c in range(HC):
                        act2 += 1; wsqd[ts, c] = act2
                        pe3 += 1; wbmm[ts, c] = pe3
                    dve2 += 1; wbrd[ts] = dve2
                dve2 += 1; w_cmb = dve2
                pe3 += 1; w_e2t = pe3
                dve2 += 1; w_prod = dve2
                act2 += 1; w_bM = act2
                pe3 += 1; w_bc0 = pe3
                dve2 += 1; w_bias0 = dve2

                wqts, wqk, wexp, wpv, wup = {}, {}, {}, {}, {}
                dve, pe, act, pe2, ev = sDVE.n, sPE.n, sACT.n, sPE2.n, sEV.n
                for qg in range(NQG0):
                    dve += 1; wqts[qg] = dve
                    for kq in range(8):
                        pe += 1; wqk[qg, kq] = pe
                        act += 1; wexp[qg, kq] = act
                    for qs in range(2):
                        pe2 += 1; wpv[qg, qs] = pe2
                        ev += 1; wup[qg, qs] = ev

                with nc.Block() as blk:
                    @blk.sync
                    def _(s):
                        for kc in range(NKC):
                            sL.d(s.dma_start(
                                xnt[:, kc * D:(kc + 1) * D],
                                xn_dram[kc * 128:(kc + 1) * 128, :]))
                        for n in range(2):
                            row = n * 128
                            sL.d(s.dma_start(gbuf[:, (n % 2) * D:(n % 2 + 1) * D],
                                             g_scr[row:row + 128, :]))
                            sL.d(s.dma_start(xbuf[:, (n % 2) * D:(n % 2 + 1) * D],
                                             x_in[row:row + 128, :]))
                        for n in range(NT):
                            qg, qs = divmod(n, 2)
                            row = n * 128
                            s.wait_ge(sEV.h, wup[qg, qs])
                            sS.d(s.dma_start(
                                x_resid[row:row + 128, :],
                                xbuf[:, (n % 2) * D:(n % 2 + 1) * D]))
                            if n + 2 < NT:
                                s.wait_ge(sS.h, S0 + 16 * (n + 1))
                                row2 = (n + 2) * 128
                                sL.d(s.dma_start(
                                    gbuf[:, (n % 2) * D:(n % 2 + 1) * D],
                                    g_scr[row2:row2 + 128, :]))
                                sL.d(s.dma_start(
                                    xbuf[:, (n % 2) * D:(n % 2 + 1) * D],
                                    x_in[row2:row2 + 128, :]))
                        s.wait_ge(sS.h, sS.n)

                    @blk.scalar
                    def _(sc):
                        for ts in range(8):
                            for c in range(HC):
                                nsq = ts * HC + c
                                if nsq >= 2:
                                    pts, pcc = divmod(nsq - 2, HC)
                                    sc.wait_ge(sPE3.h, wbmm[pts, pcc])
                                sACT2.inc(sc.activation(
                                    sq2[:, (c % 2) * 512:(c % 2 + 1) * 512],
                                    xn_f[:, c * S + ts * 512:
                                         c * S + (ts + 1) * 512],
                                    AF.Square))
                        sc.wait_ge(sDVE2.h, w_prod)
                        sc.activation(bnd[:, 2:3], bnd[:, 1:2], AF.Ln)
                        sACT2.inc(sc.activation(bnd[:, 2:3], bnd[:, 2:3],
                                                AF.Exp, scale=0.5))
                        for qg in range(NQG0):
                            for kq in range(8):
                                sc.wait_ge(sPE.h, wqk[qg, kq])
                                if qg == 0 and kq == 0:
                                    sc.wait_ge(sDVE2.h, w_bias0)
                                if qg >= 2 and kq == 0:
                                    sc.wait_ge(sPE2.h, wpv[qg - 2, 1])
                                sACT.inc(sc.activation(
                                    pT[qg % 2][:, kq * 1024:(kq + 1) * 1024],
                                    ps_sT[kq % 2][:, :],
                                    AF.Exp, bias=bias0[:, 0:1], scale=1.0))

                    @blk.tensor
                    def _(t):
                        for ts in range(8):
                            if ts >= 1:
                                t.wait_ge(sDVE2.h, wbrd[ts - 1])
                            for c in range(HC):
                                t.wait_ge(sACT2.h, wsqd[ts, c])
                                sPE3.inc(t.matmul(
                                    ps_bnd[:, :], bndsel[:, 2 * c:2 * c + 2],
                                    sq2[:, (c % 2) * 512:(c % 2 + 1) * 512],
                                    start=(c == 0), stop=(c == HC - 1)))
                        t.wait_ge(sDVE2.h, w_cmb)
                        sPE3.inc(t.transpose(ps_bnd[0:1, 0:2], bnd2[:, 8:9],
                                             id_f[0:2, 0:2]))
                        t.wait_ge(sACT2.h, w_bM)
                        sPE3.inc(t.matmul(ps_a[:, 0:1], onr_f[:, :], bnd[:, 2:3],
                                          start=True, stop=True))
                        for qg in range(NQG0):
                            t.wait_ge(sDVE.h, wqts[qg])
                            for kq in range(8):
                                prev = ordered_prev(wexp, (qg, kq), back=2)
                                if prev is not None:
                                    t.wait_ge(sACT.h, prev)
                                for kci in range(4):
                                    kc = kq * 4 + kci
                                    for c in range(HC):
                                        inst = t.matmul(
                                            ps_sT[kq % 2][:, kci * QG0:
                                                          (kci + 1) * QG0],
                                            xn_f[:, c * S + kc * 128:
                                                 c * S + (kc + 1) * 128],
                                            qts[:, c * QG0:(c + 1) * QG0],
                                            start=(c == 0), stop=(c == HC - 1))
                                sPE.inc(inst)
                            t.wait_ge(sACT.h, wexp[qg, 7])
                            for qs in range(2):
                                if qg == 0 and qs == 0:
                                    t.wait_ge(sDVE2.h, w_bias0)
                                prev = ordered_prev(wup, (qg, qs))
                                if prev is not None:
                                    t.wait_ge(sEV.h, prev)
                                for kc in range(NKC):
                                    lhs = pT[qg % 2][:, kc * QG0 + qs * 128:
                                             kc * QG0 + (qs + 1) * 128]
                                    st_ = (kc == 0); sp_ = (kc == NKC - 1)
                                    t.matmul(ps_a[:, :], lhs,
                                             xnt[:, kc * D:kc * D + 512],
                                             start=st_, stop=sp_)
                                    t.matmul(ps_b[:, 0:512], lhs,
                                             xnt[:, kc * D + 512:(kc + 1) * D],
                                             start=st_, stop=sp_)
                                    inst = t.matmul(ps_b[:, 512:513], lhs,
                                                    onc_bf[:, :],
                                                    start=st_, stop=sp_)
                                sPE2.inc(inst)

                    @blk.vector
                    def _(v):
                        for ts in range(8):
                            v.wait_ge(sPE3.h, wbmm[ts, HC - 1])
                            sDVE2.inc(v.reduce_max(bnd2[:, ts:ts + 1],
                                                   ps_bnd[:, :], AX.X))
                        v.wait_ge(sDVE2.h, wbrd[7])
                        sDVE2.inc(v.reduce_max(bnd2[:, 8:9], bnd2[:, 0:8], AX.X))
                        v.wait_ge(sPE3.h, w_e2t)
                        vg(v, v.tensor_copy(bnd[:, 4:6], ps_bnd[0:1, 0:2]))
                        sDVE2.inc(v.tensor_mul(bnd[:, 1:2], bnd[:, 4:5],
                                               bnd[:, 5:6]))
                        v.wait_ge(sPE3.h, w_bc0)
                        sDVE2.inc(v.tensor_scalar(bias0[:, :], ps_a[:, 0:1],
                                                  -1.02, None, ALU.mult))
                        n = 0
                        for qg in range(NQG0):
                            if qg == 0:
                                v.wait_ge(sPE3.h, wbmm[7, HC - 1])
                            if qg >= 1:
                                v.wait_ge(sPE.h, wqk[qg - 1, 7])
                            for c in range(HC):
                                inst = v.tensor_scalar(
                                    qts[:, c * QG0:(c + 1) * QG0],
                                    xn_f[:, c * S + qg * QG0:
                                         c * S + (qg + 1) * QG0],
                                    w_vec[:, c:c + 1], None, ALU.mult)
                            sDVE.inc(inst)
                            for qs in range(2):
                                par = n % 2
                                v.wait_ge(sPE2.h, wpv[qg, qs])
                                v.wait_ge(sL.h, L0 + 16 * NKC + 32 * (n + 1))
                                vg(v, v.reciprocal(rinv_sb[:, :],
                                                   ps_b[:, 512:513]))
                                gb1 = gbuf[:, par * D:par * D + 512]
                                gb2 = gbuf[:, par * D + 512:(par + 1) * D]
                                v.tensor_scalar(gb1, gb1, rinv_sb[:, 0:1],
                                                None, ALU.mult)
                                v.tensor_tensor(gb1, gb1, ps_a[:, :], ALU.mult)
                                v.tensor_add(xbuf[:, par * D:par * D + 512],
                                             gb1,
                                             xbuf[:, par * D:par * D + 512])
                                v.tensor_scalar(gb2, gb2, rinv_sb[:, 0:1],
                                                None, ALU.mult)
                                v.tensor_tensor(gb2, gb2, ps_b[:, 0:512],
                                                ALU.mult)
                                sEV.inc(v.tensor_add(
                                    xbuf[:, par * D + 512:(par + 1) * D],
                                    gb2,
                                    xbuf[:, par * D + 512:(par + 1) * D]))
                                n += 1

        def phase_l1ffn():
            with ExitStack() as pc:
                psb = lambda n_, s_, d=F32: pc.enter_context(nc.sbuf_tensor(n_, s_, d))
                pps = lambda n_, s_, d=F32: pc.enter_context(nc.psum_tensor(n_, s_, d))
                cbn = psb("cbn", [128, 2 * D])
                cbs = psb("cbs", [128, 2 * D], BF16)
                CBT = psb("CBT", [128, HC * D], BF16)
                g1 = psb("g1", [128, 2 * D])
                ps_t = [pps(f"ps_t1{b}", [128, 128], BF16) for b in range(2)]
                ps_f = [pps(f"ps_f{b}", [128, D]) for b in range(2)]

                L0, S0 = sL.n, sS.n
                wsc, wtr, wcp = {}, {}, {}
                dve, pe3, dve2 = sDVE.n, sPE3.n, sDVE2.n
                for nck in range(HC):
                    dve += 1; wsc[nck] = dve
                    for kc in range(HC):
                        pe3 += 1; wtr[nck, kc] = pe3
                        dve2 += 1; wcp[nck, kc] = dve2
                wmm, wsl = {}, {}
                pe, act = sPE.n, sACT.n
                for i in range(NT):
                    pe += 1; wmm[i] = pe
                    act += 1; wsl[i] = act

                with nc.Block() as blk:
                    @blk.sync
                    def _(s):
                        for nck in range(HC):
                            if nck >= 2:
                                s.wait_ge(sDVE.h, wsc[nck - 2])
                            sL.d(s.dma_start(
                                cbn[:, (nck % 2) * D:(nck % 2 + 1) * D],
                                cb_d[nck * 128:(nck + 1) * 128, :]))
                        for i in range(NT):
                            s.wait_ge(sACT.h, wsl[i])
                            sS.d(s.dma_start(
                                g_scr[i * 128:(i + 1) * 128, :],
                                g1[:, (i % 2) * D:(i % 2 + 1) * D]))
                        s.wait_ge(sS.h, sS.n)

                    @blk.vector
                    def _(v):
                        for nck in range(HC):
                            v.wait_ge(sL.h, L0 + 16 * (nck + 1))
                            if nck >= 2:
                                v.wait_ge(sPE3.h, wtr[nck - 2, HC - 1])
                            sDVE.inc(v.tensor_scalar(
                                cbs[:, (nck % 2) * D:(nck % 2 + 1) * D],
                                cbn[:, (nck % 2) * D:(nck % 2 + 1) * D],
                                scb_sb[:, nck:nck + 1], 1.0 / 127.0,
                                ALU.mult, ALU.mult))
                            for kc in range(HC):
                                v.wait_ge(sPE3.h, wtr[nck, kc])
                                dstv = CBT.rearrange("p (k n) -> p k n", k=HC)
                                sDVE2.inc(v.tensor_copy(
                                    dstv[:, kc, nck * 128:(nck + 1) * 128],
                                    ps_t[kc % 2][:, :]))

                    @blk.tensor
                    def _(t):
                        for nck in range(HC):
                            t.wait_ge(sDVE.h, wsc[nck])
                            for kc in range(HC):
                                prev = ordered_prev(wcp, (nck, kc), back=2)
                                if prev is not None:
                                    t.wait_ge(sDVE2.h, prev)
                                sPE3.inc(t.transpose(
                                    ps_t[kc % 2][:, :],
                                    cbs[:, (nck % 2) * D + kc * 128:
                                        (nck % 2) * D + (kc + 1) * 128],
                                    id_bf[:, :]))
                        for i in range(NT):
                            if i >= 2:
                                t.wait_ge(sACT.h, wsl[i - 2])
                            if i == 0:
                                t.wait_ge(sDVE2.h, wcp[HC - 1, HC - 1])
                            for nh in range(2):
                                for kc in range(HC):
                                    t.matmul(ps_f[i % 2][:, nh * 512:(nh + 1) * 512],
                                             xn_f[:, kc * S + i * 128:
                                                  kc * S + (i + 1) * 128],
                                             CBT[:, kc * D + nh * 512:
                                                 kc * D + (nh + 1) * 512],
                                             start=(kc == 0), stop=False)
                                inst = t.matmul(
                                    ps_f[i % 2][:, nh * 512:(nh + 1) * 512],
                                    onr_bf[:, :],
                                    b1_bf[0:1, nh * 512:(nh + 1) * 512],
                                    start=False, stop=True)
                            sPE.inc(inst)

                    @blk.scalar
                    def _(sc):
                        for i in range(NT):
                            sc.wait_ge(sPE.h, wmm[i])
                            if i >= 2:
                                sc.wait_ge(sS.h, S0 + 16 * (i - 1))
                            sACT.inc(sc.activation(
                                g1[:, (i % 2) * D:(i % 2 + 1) * D],
                                ps_f[i % 2][:, :], AF.Silu))

        def phase_l1attn():
            with ExitStack() as pc:
                psb = lambda n_, s_, d=F32: pc.enter_context(nc.sbuf_tensor(n_, s_, d))
                pps = lambda n_, s_, d=F32: pc.enter_context(nc.psum_tensor(n_, s_, d))
                wq_sb = psb("wq_sb", [128, HC * D], BF16)
                wk_sb = psb("wk_sb", [128, HC * D], BF16)
                wv_sb = psb("wv_sb", [128, HC * D], BF16)
                pbuf = [psb(f"pj{w}", [128, 2 * D], BF16) for w in range(3)]
                qb = psb("qb", [128, 2 * D], BF16)
                kb = psb("kb", [128, 2 * D], BF16)
                vb = psb("vb", [128, 2 * D], BF16)
                prod = psb("prodF", [128, 16384], BF16)
                SC = psb("SCf", [128, 256])
                sc2 = psb("sc2f", [128, 256])
                pp = psb("ppf", [128, 256], BF16)
                at = psb("atf", [128, 256], BF16)
                sm = psb("smf", [128, 64])
                Ob = psb("Obf", [128, 2 * D])
                ps_p = [pps(f"ps_p{w}", [128, D]) for w in range(3)]

                W0, L0, S0 = sW.n, sL.n, sS.n
                wpm, wpe, wex = {}, {}, {}
                wdA, wdB, wdC = {}, {}, {}
                pe, dve, act, dve2 = sPE.n, sDVE.n, sACT.n, sDVE2.n
                for i in range(NT):
                    for w in range(3):
                        pe += 1; wpm[i, w] = pe
                        dve += 1; wpe[i, w] = dve
                for i in range(NT):
                    dve2 += 1; wdA[i] = dve2
                    act += 1; wex[i] = act
                    dve2 += 1; wdB[i] = dve2
                    dve2 += 1; wdC[i] = dve2
                stF1 = {i: S0 + 16 * 3 * (i + 1) for i in range(NT)}
                WB = W0 + 16 * 24
                stF2 = {i: WB + 16 * (i + 1) for i in range(NT)}
                L1_ = L0 + 16 * 24
                ldF2 = {i: L0 + 16 * 3 * (i + 1) for i in range(NT)}

                with nc.Block() as blk:
                    @blk.gpsimd
                    def _(g):
                        for wsrc, wbuf in ((wqT_d, wq_sb), (wkT_d, wk_sb),
                                           (wvTs_d, wv_sb)):
                            for hc in range(HC):
                                sW.d(g.dma_start(
                                    wbuf[:, hc * D:(hc + 1) * D],
                                    wsrc[hc * 128:(hc + 1) * 128, :]))
                        for i in range(NT):
                            g.wait_ge(sDVE2.h, wdC[i])
                            sW.d(g.dma_start(
                                xn_dram[i * 128:(i + 1) * 128, :],
                                Ob[:, (i % 2) * D:(i % 2 + 1) * D]))
                        g.wait_ge(sW.h, stF2[NT - 1])
                        otok = xn_dram
                        ov = otok.rearrange("(r s) c -> r s c", s=16)
                        for p in range(NPAIR):
                            dst = ao_hbm[p].rearrange("(q rr) (h d) -> q rr h d",
                                                      q=16, h=2)
                            for q in range(16):
                                sW.d(g.dma_start(
                                    dst[q, :, :, :],
                                    ov[:, 2 * p:2 * p + 2,
                                       q * 64:(q + 1) * 64]))
                        g.wait_ge(sW.h, sW.n)

                    @blk.tensor
                    def _(t):
                        t.wait_ge(sW.h, W0 + 16 * 24)
                        for i in range(NT):
                            for w in range(3):
                                wsb = (wq_sb, wk_sb, wv_sb)[w]
                                brow = (bqr_bf, bkr_bf, bvs_bf)[w]
                                prev = ordered_prev(wpe, (i, w), back=3)
                                if prev is not None:
                                    t.wait_ge(sDVE.h, prev)
                                for hc in range(HC):
                                    for nh in range(2):
                                        t.matmul(
                                            ps_p[w][:, nh * 512:(nh + 1) * 512],
                                            xn_f[:, hc * S + i * 128:
                                                 hc * S + (i + 1) * 128],
                                            wsb[:, hc * D + nh * 512:
                                                hc * D + (nh + 1) * 512],
                                            start=(hc == 0), stop=False)
                                for nh in range(2):
                                    inst = t.matmul(
                                        ps_p[w][:, nh * 512:(nh + 1) * 512],
                                        onr_bf[:, :],
                                        brow[0:1, nh * 512:(nh + 1) * 512],
                                        start=False, stop=True)
                                sPE.inc(inst)

                    @blk.vector
                    def _(v):
                        for ii in range(NT + 4):
                          if ii < NT:
                            i = ii
                            for w in range(3):
                                v.wait_ge(sPE.h, wpm[i, w])
                                if i >= 2:
                                    v.wait_ge(sS.h, stF1[i - 2])
                                sDVE.inc(v.tensor_copy(
                                    pbuf[w][:, (i % 2) * D:(i % 2 + 1) * D],
                                    ps_p[w][:, :]))
                          if ii >= 4:
                            i = ii - 4
                            par = i % 2
                            v.wait_ge(sL.h, ldF2[i])
                            qv = bass.AP(qb, par * D,
                                         [[2 * D, 128], [64, 16], [0, 16],
                                          [1, 64]])
                            kv = bass.AP(kb, par * D,
                                         [[2 * D, 128], [0, 16], [64, 16],
                                          [1, 64]])
                            v.tensor_tensor(prod[:, :], qv, kv, ALU.mult)
                            vg(v, v.reduce_sum(
                                SC[:, :],
                                prod.rearrange("p (s d) -> p s d", d=64), AX.X))
                            vg(v, v.tensor_reduce(
                                sm[:, 0:16],
                                SC.rearrange("p (a b) -> p a b", b=16),
                                AX.X, ALU.max, negate=True))
                            for hq in range(16):
                                inst = v.tensor_scalar(
                                    sc2[:, hq * 16:(hq + 1) * 16],
                                    SC[:, hq * 16:(hq + 1) * 16],
                                    sm[:, hq:hq + 1], None, ALU.add)
                            sDVE2.inc(inst)
                            v.wait_ge(sACT.h, wex[i])
                            vg(v, v.reduce_sum(
                                sm[:, 16:32],
                                pp.rearrange("p (a b) -> p a b", b=16), AX.X))
                            vg(v, v.reciprocal(sm[:, 32:48], sm[:, 16:32]))
                            for hq in range(16):
                                inst = v.tensor_scalar(
                                    at[:, hq * 16:(hq + 1) * 16],
                                    pp[:, hq * 16:(hq + 1) * 16],
                                    sm[:, 32 + hq:33 + hq], None, ALU.mult)
                            sDVE2.inc(inst)
                            av = bass.AP(at, 0,
                                         [[256, 128], [16, 16], [0, 64],
                                          [1, 16]])
                            vv = bass.AP(vb, par * D,
                                         [[2 * D, 128], [0, 16], [16, 64],
                                          [1, 16]])
                            v.tensor_tensor(prod[:, :], av, vv, ALU.mult)
                            if i >= 2:
                                v.wait_ge(sW.h, stF2[i - 2])
                            sDVE2.inc(v.reduce_sum(
                                Ob[:, par * D:(par + 1) * D],
                                prod.rearrange("p (s d) -> p s d", d=16),
                                AX.X))

                    @blk.scalar
                    def _(sc):
                        for i in range(NT):
                            sc.wait_ge(sDVE2.h, wdA[i])
                            if i >= 1:
                                sc.wait_ge(sDVE2.h, wdB[i - 1])
                            sACT.inc(sc.activation(pp[:, :], sc2[:, :], AF.Exp,
                                                   scale=0.125))

                    @blk.sync
                    def _(s):
                        for i in range(NT + 2):
                            if i < NT:
                                for w in range(3):
                                    s.wait_ge(sDVE.h, wpe[i, w])
                                    sS.d(s.dma_start(
                                        qkv_dram[w, i * 128:(i + 1) * 128, :],
                                        pbuf[w][:, (i % 2) * D:(i % 2 + 1) * D]))
                            j = i - 2
                            if 0 <= j < NT:
                                s.wait_ge(sS.h, stF1[j])
                                if j >= 2:
                                    s.wait_ge(sDVE2.h, wdC[j - 2])
                                for w, buf in ((0, qb), (1, kb), (2, vb)):
                                    sL.d(s.dma_start(
                                        buf[:, (j % 2) * D:(j % 2 + 1) * D],
                                        qkv_dram[w, j * 128:(j + 1) * 128, :]))
                        s.wait_ge(sS.h, sS.n)

        def phase_l1out():
            with ExitStack() as pc:
                psb = lambda n_, s_, d=F32: pc.enter_context(nc.sbuf_tensor(n_, s_, d))
                pps = lambda n_, s_, d=F32: pc.enter_context(nc.psum_tensor(n_, s_, d))
                woT_sb = psb("woT_sb", [128, HC * D], BF16)
                aoT = psb("aoT", [128, HC * S], BF16)
                g1b = psb("g1b", [128, 2 * D])
                x1b = psb("x1b", [128, 2 * D])
                ob = psb("ob", [128, 2 * D])
                tmpG = psb("tmpG", [128, 512])
                ps_s = [pps(f"ps_s{b}", [128, D]) for b in range(2)]

                W0, L0, S0 = sW.n, sL.n, sS.n
                wmmG, wupG = {}, {}
                pe, ev = sPE.n, sEV.n
                for i in range(NT):
                    pe += 1; wmmG[i] = pe
                    ev += 1; wupG[i] = ev

                with nc.Block() as blk:
                    @blk.gpsimd
                    def _(g):
                        for pch in range(HC):
                            sW.d(g.dma_start(
                                woT_sb[:, pch * D:(pch + 1) * D],
                                woT_d[pch * 128:(pch + 1) * 128, :]))
                        g.wait_ge(sW.h, sW.n)

                    @blk.sync
                    def _(s):
                        for p in range(NPAIR):
                            sL.d(s.dma_start_transpose(
                                aoT[:, p * S:(p + 1) * S], ao_hbm[p][:, :]))
                        for i in range(2):
                            sL.d(s.dma_start(
                                g1b[:, (i % 2) * D:(i % 2 + 1) * D],
                                g_scr[i * 128:(i + 1) * 128, :]))
                            sL.d(s.dma_start(
                                x1b[:, (i % 2) * D:(i % 2 + 1) * D],
                                x_resid[i * 128:(i + 1) * 128, :]))
                        for i in range(NT):
                            s.wait_ge(sEV.h, wupG[i])
                            sS.d(s.dma_start(
                                out_d[i * 128:(i + 1) * 128, :],
                                ob[:, (i % 2) * D:(i % 2 + 1) * D]))
                            if i + 2 < NT:
                                s.wait_ge(sEV.h, wupG[i])
                                sL.d(s.dma_start(
                                    g1b[:, (i % 2) * D:(i % 2 + 1) * D],
                                    g_scr[(i + 2) * 128:(i + 3) * 128, :]))
                                sL.d(s.dma_start(
                                    x1b[:, (i % 2) * D:(i % 2 + 1) * D],
                                    x_resid[(i + 2) * 128:(i + 3) * 128, :]))
                        s.wait_ge(sS.h, sS.n)

                    @blk.tensor
                    def _(t):
                        t.wait_ge(sW.h, sW.n)
                        t.wait_ge(sL.h, L0 + 16 * NPAIR)
                        for i in range(NT):
                            if i >= 2:
                                t.wait_ge(sEV.h, wupG[i - 2])
                            for nh in range(2):
                                for pch in range(HC):
                                    t.matmul(ps_s[i % 2][:, nh * 512:(nh + 1) * 512],
                                             aoT[:, pch * S + i * 128:
                                                  pch * S + (i + 1) * 128],
                                             woT_sb[:, pch * D + nh * 512:
                                                    pch * D + (nh + 1) * 512],
                                             start=(pch == 0), stop=False)
                                inst = t.matmul(
                                    ps_s[i % 2][:, nh * 512:(nh + 1) * 512],
                                    onr_bf[:, :],
                                    bo_bf[0:1, nh * 512:(nh + 1) * 512],
                                    start=False, stop=True)
                            sPE.inc(inst)

                    @blk.vector
                    def _(v):
                        for i in range(NT):
                            par = i % 2
                            v.wait_ge(sPE.h, wmmG[i])
                            v.wait_ge(sL.h, L0 + 16 * NPAIR + 32 * (i + 1))
                            if i >= 2:
                                v.wait_ge(sS.h, S0 + 16 * (i - 1))
                            for nh in range(2):
                                v.tensor_tensor(
                                    tmpG[:, :],
                                    ps_s[par][:, nh * 512:(nh + 1) * 512],
                                    g1b[:, par * D + nh * 512:
                                        par * D + (nh + 1) * 512], ALU.mult)
                                inst = v.tensor_add(
                                    ob[:, par * D + nh * 512:
                                       par * D + (nh + 1) * 512],
                                    tmpG[:, :],
                                    x1b[:, par * D + nh * 512:
                                        par * D + (nh + 1) * 512])
                            sEV.inc(inst)

        phase_ln(0)
        if n_phases >= 2:
            phase_xnf()
            phase_combine()
        if n_phases >= 3:
            phase_l0ffn()
        if n_phases >= 4:
            phase_l0attn()
        if n_phases >= 5:
            phase_ln(1)
            phase_xnf()
        if n_phases >= 6:
            phase_l1ffn()
        if n_phases >= 7:
            phase_l1attn()
        if n_phases >= 8:
            phase_l1out()

        if debug:
            with nc.Block() as blk:
                dbgA = dt("dbgA", [128, NT * 8], F32, "ExternalOutput")
                dbgB = dt("dbgB", [8, NT * 128], F32, "ExternalOutput")
                dbgC = dt("dbgC", [128, HC * S], F32, "ExternalOutput")
                dbg_xnd = dt("dbg_xnd", [S, D], F32, "ExternalOutput")
                dbg_xres = dt("dbg_xres", [S, D], F32, "ExternalOutput")
                dbg_g = dt("dbg_g", [S, D], F32, "ExternalOutput")
                dbg_ao = dt("dbg_ao", [NPAIR, S, 128], F32, "ExternalOutput")

                @blk.gpsimd
                def _(g):
                    sW.d(g.dma_start(dbgA[:, :], logits_sb[:, :]))
                    sW.d(g.dma_start(dbg_xnd[:, :], xn_dram[:, :]))
                    if n_phases >= 2:
                        sW.d(g.dma_start(dbgB[:, :], combT[:, :]))
                        sW.d(g.dma_start(dbgC[:, :], xn_f[:, :]))
                    if n_phases >= 3:
                        sW.d(g.dma_start(dbg_g[:, :], g_scr[:, :]))
                    if n_phases >= 4:
                        sW.d(g.dma_start(dbg_xres[:, :], x_resid[:, :]))
                    if n_phases >= 7:
                        for p in range(NPAIR):
                            sW.d(g.dma_start(dbg_ao[p, :, :], ao_hbm[p, :, :]))
                    g.wait_ge(sW.h, sW.n)

    return nc


def _layout_inputs(inputs):
    f = np.ascontiguousarray
    esel = np.zeros((8, 1024), np.float32)
    for e in range(8):
        esel[e, e * 128:(e + 1) * 128] = 1.0
    common = {
        "wl": f(np.asarray(inputs["L0_attn_w_lhs"], np.float32).reshape(8, 128).T),
        "wr": f(np.asarray(inputs["L0_attn_w_rhs"], np.float32).reshape(8, 128).T),
        "gwT": f(np.asarray(inputs["L0_ffn_gate_weight"], np.float32).T),
        "wgT": f(np.transpose(np.asarray(inputs["L0_ffn_w_gate"], np.float32),
                              (0, 2, 1))),
        "wuT": f(np.transpose(np.asarray(inputs["L0_ffn_w_up"], np.float32),
                              (0, 2, 1))),
        "wdT": f(np.transpose(np.asarray(inputs["L0_ffn_w_down"], np.float32),
                              (0, 2, 1))),
        "wsgT": f(np.asarray(inputs["L0_ffn_ws_gate"], np.float32).T),
        "wsuT": f(np.asarray(inputs["L0_ffn_ws_up"], np.float32).T),
        "wsdT": f(np.asarray(inputs["L0_ffn_ws_down"], np.float32).T),
        "wqT": f(np.asarray(inputs["L1_attn_Wq"], np.float32).T),
        "wkT": f(np.asarray(inputs["L1_attn_Wk"], np.float32).T),
        "wvTs": f(np.asarray(inputs["L1_attn_Wv"], np.float32).T
                  .reshape(1024, 16, 64).transpose(0, 2, 1).reshape(1024, 1024)),
        "woT": f(np.asarray(inputs["L1_attn_Wo"], np.float32).T),
        "bqrow": f(np.asarray(inputs["L1_attn_bq"], np.float32).reshape(1, 1024)),
        "bkrow": f(np.asarray(inputs["L1_attn_bk"], np.float32).reshape(1, 1024)),
        "bvsrow": f(np.asarray(inputs["L1_attn_bv"], np.float32)
                    .reshape(16, 64).T.reshape(1, 1024)),
        "borow": f(np.asarray(inputs["L1_attn_bo"], np.float32).reshape(1, 1024)),
        "cb": f(np.asarray(inputs["L1_ffn_CB"], np.float32)),
        "scb": f(np.asarray(inputs["L1_ffn_SCB"], np.float32).reshape(8, 128).T),
        "b1row": f(np.asarray(inputs["L1_ffn_bias"], np.float32).reshape(1, 1024)),
        "ident": np.eye(128, dtype=np.float32),
        "onesrow": np.ones((1, 128), np.float32),
        "onescol": np.ones((128, 1), np.float32),
        "esel": esel,
    }
    x = np.asarray(inputs["L0_attn_x"], np.float32)
    return [dict(common, x=f(x[c])) for c in range(8)]


def kernel(**inputs):
    in_maps = _layout_inputs(inputs)
    nc = build_nc()
    res = run_bass_kernel_spmd(nc, in_maps, core_ids=list(range(8)))
    return np.stack([r["out"] for r in res.results], axis=0)


# version 12
# speedup vs baseline: 1.1838x; 1.1083x over previous
import numpy as np
from contextlib import ExitStack

import concourse.bass as bass
import concourse.mybir as mybir
from concourse import library_config
from concourse.bass_utils import run_bass_kernel_spmd

F32 = mybir.dt.float32
BF16 = mybir.dt.bfloat16
AF = mybir.ActivationFunctionType
ALU = mybir.AluOpType
AX = mybir.AxisListType

S = 4096
D = 1024
HC = 8
NT = 32
NKC = 32
QG = 512
NQG = S // QG
NEXP = 8
NIC = 4
NPAIR = 8
EPS = 1e-5
NPAIRS5 = 5
QN = 4


def ordered_prev(d, key, back=1):
    ks = list(d.keys())
    i = ks.index(key)
    return d[ks[i - back]] if i - back >= 0 else None


class Sem:
    def __init__(self, nc, ctx, name):
        self.h = ctx.enter_context(nc.semaphore(name))
        self.n = 0

    def inc(self, inst, k=1):
        inst.then_inc(self.h, k)
        self.n += k
        return self.n

    def d(self, inst):
        return self.inc(inst, 16)


def build_nc(n_phases=99, debug=False):
    nc = bass.Bass("TRN2", target_bir_lowering=False, dynamic_dma_scratch_size=16384, detect_race_conditions=False)

    dt = lambda name, shape, dtype=F32, kind="ExternalInput": nc.dram_tensor(
        name, shape, dtype, kind=kind)
    x_in = dt("x", [S, D])
    wl_d = dt("wl", [128, 8]);  wr_d = dt("wr", [128, 8])
    gwT_d = dt("gwT", [D, 8])
    wgT_d = dt("wgT", [NEXP, D, 512]); wuT_d = dt("wuT", [NEXP, D, 512])
    wdT_d = dt("wdT", [NEXP, 512, D])
    wsgT_d = dt("wsgT", [D, D]); wsuT_d = dt("wsuT", [D, D]); wsdT_d = dt("wsdT", [D, D])
    wqT_d = dt("wqT", [D, D]); wkT_d = dt("wkT", [D, D])
    wvTs_d = dt("wvTs", [D, D]); woT_d = dt("woT", [D, D])
    bqr_d = dt("bqrow", [1, D]); bkr_d = dt("bkrow", [1, D])
    bvs_d = dt("bvsrow", [1, D]); bo_d = dt("borow", [1, D])
    cb_d = dt("cb", [D, D]); scb_d = dt("scb", [128, 8]); b1_d = dt("b1row", [1, D])
    id_d = dt("ident", [128, 128])
    onr_d = dt("onesrow", [1, 128]); onc_d = dt("onescol", [128, 1])
    esel_d = dt("esel", [8, 1024])
    out_d = dt("out", [S, D], F32, "ExternalOutput")

    xn_dram = dt("xn_dram", [S, D], BF16, "Internal")
    qkv_dram = dt("qkv_dram", [3, S, D], BF16, "Internal")
    wgT_s = dt("wgT_s", [NEXP, D, 512], BF16, "Internal")
    wuT_s = dt("wuT_s", [NEXP, D, 512], BF16, "Internal")
    wdT_s = dt("wdT_s", [NEXP, 512, D], BF16, "Internal")
    wsgT_s = dt("wsgT_s", [D, D], BF16, "Internal")
    wsuT_s = dt("wsuT_s", [D, D], BF16, "Internal")
    wsdT_s = dt("wsdT_s", [D, D], BF16, "Internal")
    x_resid = dt("x_resid", [S, D], F32, "Internal")
    g_scr = dt("g_scr", [S, D], F32, "Internal")
    ao_hbm = dt("ao_hbm", [NPAIR, S, 128], BF16, "Internal")

    with ExitStack() as ctx:
        sb = lambda name, shape, d=F32: ctx.enter_context(nc.sbuf_tensor(name, shape, d))
        sem = lambda name: Sem(nc, ctx, name)

        xn_f = sb("xn_f", [128, HC * S], BF16)
        logits_sb = sb("logits", [128, NT * 8])
        combT = sb("combT", [8, NT * 128], BF16)
        id_bf = sb("id_bf", [128, 128], BF16)
        id_f = sb("id_f", [128, 128])
        onr_bf = sb("onr_bf", [1, 128], BF16)
        onr_f = sb("onr_f", [1, 128])
        onc_bf = sb("onc_bf", [128, 1], BF16)
        esel_bf = sb("esel_bf", [8, 1024], BF16)
        w_vec = sb("w_vec", [128, 8])
        bndsel = sb("bndsel", [128, 16], BF16)
        gwT_sb = sb("gwT_sb", [128, HC * 8])
        gwst = sb("gwst", [128, HC * 16], BF16)
        gwlo_t = sb("gwlo_t", [128, HC * 8])
        scb_sb = sb("scb_sb", [128, 8])
        bqr_bf = sb("bqr_bf", [1, D], BF16); bkr_bf = sb("bkr_bf", [1, D], BF16)
        bvs_bf = sb("bvs_bf", [1, D], BF16); bo_bf = sb("bo_bf", [1, D], BF16)
        b1_bf = sb("b1_bf", [1, D], BF16)
        wlr = sb("wlr", [128, 16])
        eps_sb = sb("eps_sb", [128, 1])

        sW = sem("sW")
        sL = sem("sL")
        sS = sem("sS")
        sPE = sem("sPE")
        sPE2 = sem("sPE2")
        sPE3 = sem("sPE3")
        sACT = sem("sACT")
        sACT2 = sem("sACT2")
        sDVE = sem("sDVE")
        sDVE2 = sem("sDVE2")
        sEV = sem("sEV")
        sGD = sem("sGD")
        sPL = sem("sPL")
        sST = sem("sST")

        def vg(eng, inst):
            n = sGD.inc(inst)
            eng.wait_ge(sGD.h, n)

        with nc.Block() as blk:
            @blk.gpsimd
            def _(g):
                for dst, src in [(id_bf, id_d), (onr_bf, onr_d), (onc_bf, onc_d),
                                 (esel_bf, esel_d), (bqr_bf, bqr_d),
                                 (bkr_bf, bkr_d), (bvs_bf, bvs_d),
                                 (bo_bf, bo_d), (b1_bf, b1_d)]:
                    sW.d(g.dma_start(dst[:, :], src[:, :]))
                g.wait_ge(sW.h, sW.n)
                for a_, b_ in ((wsgT_d, wsgT_s), (wsuT_d, wsuT_s),
                               (wsdT_d, wsdT_s)):
                    sST.d(g.dma_start(b_[:, :], a_[:, :]))
                for e in range(NEXP):
                    sST.d(g.dma_start(wgT_s[e], wgT_d[e]))
                    sST.d(g.dma_start(wuT_s[e], wuT_d[e]))
                    sST.d(g.dma_start(wdT_s[e], wdT_d[e]))

            @blk.sync
            def _(s):
                for dst, src in [(id_f, id_d), (onr_f, onr_d),
                                 (scb_sb, scb_d)]:
                    sL.d(s.dma_start(dst[:, :], src[:, :]))
                sL.d(s.dma_start(wlr[:, 0:8], wl_d[:, :]))
                sL.d(s.dma_start(wlr[:, 8:16], wr_d[:, :]))
                for c in range(HC):
                    sL.d(s.dma_start(gwT_sb[:, c * 8:(c + 1) * 8],
                                       gwT_d[c * 128:(c + 1) * 128, :]))
                s.wait_ge(sL.h, sL.n)

            @blk.vector
            def _(v):
                v.wait_ge(sL.h, sL.n)
                v.wait_ge(sW.h, sW.n)
                v.memset(eps_sb[:, :], EPS)
                vg(v, v.tensor_mul(w_vec[:, :], wlr[:, 0:8], wlr[:, 8:16]))
                for c in range(HC):
                    v.tensor_copy(bndsel[:, 2 * c:2 * c + 1], onc_bf[:, :])
                    v.tensor_mul(bndsel[:, 2 * c + 1:2 * c + 2],
                                 w_vec[:, c:c + 1], w_vec[:, c:c + 1])
                for c in range(HC):
                    vg(v, v.tensor_copy(gwst[:, c * 16:c * 16 + 8],
                                        gwT_sb[:, c * 8:(c + 1) * 8]))
                    vg(v, v.tensor_copy(gwlo_t[:, c * 8:(c + 1) * 8],
                                        gwst[:, c * 16:c * 16 + 8]))
                    v.tensor_tensor(gwst[:, c * 16 + 8:(c + 1) * 16],
                                    gwT_sb[:, c * 8:(c + 1) * 8],
                                    gwlo_t[:, c * 8:(c + 1) * 8], ALU.subtract)

        def phase_ln(layer):
            src = x_in if layer == 0 else x_resid
            with ExitStack() as pc:
                psb = lambda n_, s_, d=F32: pc.enter_context(nc.sbuf_tensor(n_, s_, d))
                pps = lambda n_, s_, d=F32: pc.enter_context(nc.psum_tensor(n_, s_, d))
                xt = psb(f"ln{layer}_xt", [128, 2 * D])
                xc = psb(f"ln{layer}_xc", [128, D])
                sq = psb(f"ln{layer}_sq", [128, D])
                st4 = psb(f"ln{layer}_st", [128, 8])
                xn32 = psb(f"ln{layer}_xn32", [128, D])
                xnbf = psb(f"ln{layer}_xnbf", [128, 2 * D], BF16)
                L0, S0 = sL.n, sS.n
                wa, wb, wc, wp, wq_ = ({} for _ in range(5))
                dn, an = sDVE.n, sACT.n
                for i in range(NT):
                    dn += 1; wa[i] = dn
                    an += 1; wp[i] = an
                    dn += 1; wb[i] = dn
                    an += 1; wq_[i] = an
                    dn += 1; wc[i] = dn

                with nc.Block() as blk:
                    @blk.sync
                    def _(s):
                        for i in range(2):
                            sL.d(s.dma_start(xt[:, (i % 2) * D:(i % 2 + 1) * D],
                                             src[i * 128:(i + 1) * 128, :]))
                        for i in range(NT):
                            s.wait_ge(sDVE.h, wc[i])
                            sS.d(s.dma_start(xn_dram[i * 128:(i + 1) * 128, :],
                                             xnbf[:, (i % 2) * D:(i % 2 + 1) * D]))
                            if i + 2 < NT:
                                s.wait_ge(sDVE.h, wa[i])
                                sL.d(s.dma_start(
                                    xt[:, (i % 2) * D:(i % 2 + 1) * D],
                                    src[(i + 2) * 128:(i + 3) * 128, :]))
                        s.wait_ge(sS.h, sS.n)

                    @blk.vector
                    def _(v):
                        for i in range(NT):
                            xti = xt[:, (i % 2) * D:(i % 2 + 1) * D]
                            v.wait_ge(sL.h, L0 + 16 * (i + 1))
                            if i >= 1:
                                v.wait_ge(sACT.h, wp[i - 1])
                            vg(v, v.reduce_sum(st4[:, 0:1], xti, AX.X))
                            vg(v, v.tensor_scalar(st4[:, 1:2], st4[:, 0:1],
                                                  1.0 / D, None, ALU.mult))
                            sDVE.inc(v.tensor_scalar(xc[:, :], xti, st4[:, 1:2],
                                                     None, ALU.subtract))
                            v.wait_ge(sACT.h, wp[i])
                            sDVE.inc(v.reduce_sum(st4[:, 2:3], sq[:, :], AX.X))
                            v.wait_ge(sACT.h, wq_[i])
                            if i >= 2:
                                v.wait_ge(sS.h, S0 + 16 * (i - 1))
                            v.tensor_scalar(xn32[:, :], xc[:, :], st4[:, 3:4],
                                            None, ALU.mult)
                            sDVE.inc(v.tensor_copy(
                                xnbf[:, (i % 2) * D:(i % 2 + 1) * D], xn32[:, :]))

                    @blk.scalar
                    def _(sc):
                        for i in range(NT):
                            sc.wait_ge(sDVE.h, wa[i])
                            sACT.inc(sc.activation(sq[:, :], xc[:, :], AF.Square))
                            sc.wait_ge(sDVE.h, wb[i])
                            sc.activation(st4[:, 3:4], st4[:, 2:3], AF.Ln,
                                          bias=eps_sb[:, 0:1], scale=1.0 / D)
                            sACT.inc(sc.activation(st4[:, 3:4], st4[:, 3:4],
                                                   AF.Exp, scale=-0.5))

        def phase_xnf():
            with nc.Block() as blk:
                @blk.sync
                def _(s):
                    for c in range(HC):
                        sL.d(s.dma_start_transpose(
                            xn_f[:, c * S:(c + 1) * S],
                            xn_dram[:, c * 128:(c + 1) * 128]))
                    s.wait_ge(sL.h, sL.n)

        def phase_combine():
            with ExitStack() as pc:
                cw = pc.enter_context(nc.sbuf_tensor("cw", [128, 32], F32))
                comb = pc.enter_context(nc.sbuf_tensor("comb", [128, 8], F32))
                lgt = pc.enter_context(nc.sbuf_tensor("lgt", [128, 16], F32))
                ps_cT = pc.enter_context(nc.psum_tensor("ps_cT", [8, 128], F32))
                ps_lg = [pc.enter_context(nc.psum_tensor(f"ps_lg{b}", [128, 16],
                                                         F32)) for b in range(2)]
                wd0, wd1, wd2, wd3, wa1, wp1, wlg, wle = ({} for _ in range(8))
                dn, an, pn = sDVE.n, sACT.n, sPE.n
                pe2, ae2 = sPE2.n, sACT2.n
                for i in range(NT):
                    pe2 += 1; wlg[i] = pe2
                    ae2 += 1; wle[i] = ae2
                    dn += 1; wd0[i] = dn
                    dn += 1; wd1[i] = dn
                    an += 1; wa1[i] = an
                    dn += 1; wd2[i] = dn
                    pn += 1; wp1[i] = pn
                    dn += 1; wd3[i] = dn

                with nc.Block() as blk:
                    @blk.vector
                    def _(v):
                        for i in range(NT):
                            par = i % 2
                            L = logits_sb[:, i * 8:(i + 1) * 8]
                            v.wait_ge(sACT2.h, wle[i])
                            sDVE.inc(v.tensor_add(
                                L, ps_lg[par][:, 0:8],
                                lgt[:, par * 8:(par + 1) * 8]))
                            v.wait_ge(sDVE.h, wd0[i])
                            m1, m2 = cw[:, 0:1], cw[:, 1:2]
                            w1, w2 = cw[:, 2:3], cw[:, 3:4]
                            den, rinv = cw[:, 4:5], cw[:, 5:6]
                            c1, c2 = cw[:, 6:7], cw[:, 7:8]
                            msk1, msk2 = cw[:, 8:16], cw[:, 16:24]
                            Lm = cw[:, 24:32]
                            vg(v, v.reduce_max(m1, L, AX.X))
                            vg(v, v.tensor_scalar(msk1, L, m1, None,
                                                  ALU.is_equal))
                            vg(v, v.tensor_scalar(Lm, msk1, -1e30, None,
                                                  ALU.mult))
                            vg(v, v.tensor_add(Lm, Lm, L))
                            inst = v.reduce_max(m2, Lm, AX.X)
                            sDVE.inc(inst)
                            v.wait_ge(sDVE.h, sDVE.n)
                            v.tensor_scalar(msk2, Lm, m2, None, ALU.is_equal)
                            v.wait_ge(sACT.h, wa1[i])
                            vg(v, v.tensor_add(den, w1, w2))
                            vg(v, v.tensor_scalar(den, den, 1e-20, None, ALU.add))
                            vg(v, v.reciprocal(rinv, den))
                            vg(v, v.tensor_scalar(c1, w1, rinv, 2.5, ALU.mult,
                                                  ALU.mult))
                            vg(v, v.tensor_scalar(c2, w2, rinv, 2.5, ALU.mult,
                                                  ALU.mult))
                            if i >= 1:
                                v.wait_ge(sPE.h, wp1[i - 1])
                            v.tensor_scalar(comb[:, :], msk1, c1, None, ALU.mult)
                            vg(v, v.tensor_scalar(Lm, msk2, c2, None, ALU.mult))
                            sDVE.inc(v.tensor_add(comb[:, :], comb[:, :], Lm))
                            v.wait_ge(sPE.h, wp1[i])
                            sDVE.inc(v.tensor_copy(
                                combT[:, i * 128:(i + 1) * 128], ps_cT[:, :]))

                    @blk.scalar
                    def _(sc):
                        for i in range(NT):
                            par = i % 2
                            if i >= 2:
                                sc.wait_ge(sDVE.h, wd0[i - 2])
                            sc.wait_ge(sPE2.h, wlg[i])
                            sACT2.inc(sc.copy(lgt[:, par * 8:(par + 1) * 8],
                                              ps_lg[par][:, 8:16]))
                            sc.wait_ge(sDVE.h, wd1[i])
                            sc.activation(cw[:, 2:3], cw[:, 0:1], AF.Sigmoid)
                            sACT.inc(sc.activation(cw[:, 3:4], cw[:, 1:2],
                                                   AF.Sigmoid))

                    @blk.tensor
                    def _(t):
                        for i in range(NT):
                            par = i % 2
                            if i >= 2:
                                t.wait_ge(sDVE.h, wd0[i - 2])
                                t.wait_ge(sACT2.h, wle[i - 2])
                            for hc in range(HC):
                                inst = t.matmul(
                                    ps_lg[par][:, :],
                                    xn_f[:, hc * S + i * 128:
                                         hc * S + (i + 1) * 128],
                                    gwst[:, hc * 16:(hc + 1) * 16],
                                    start=(hc == 0), stop=(hc == HC - 1))
                            sPE2.inc(inst)
                            t.wait_ge(sDVE.h, wd2[i])
                            if i >= 1:
                                t.wait_ge(sDVE.h, wd3[i - 1])
                            sPE.inc(t.transpose(ps_cT[:, :], comb[:, :],
                                                id_f[:, :]))

        def phase_l0ffn():
            with ExitStack() as pc:
                psb = lambda n_, s_, d=F32: pc.enter_context(nc.sbuf_tensor(n_, s_, d))
                pps = lambda n_, s_, d=F32: pc.enter_context(nc.psum_tensor(n_, s_, d))
                acc = psb("acc", [128, 4 * D])
                wgt = [psb(f"wgt{b}", [128, 2 * HC * 512], BF16) for b in range(2)]
                wut = [psb(f"wut{b}", [128, 2 * HC * 512], BF16) for b in range(2)]
                wdt = [psb("wdt", [128, 2 * NIC * D], BF16)] * 2
                hmS = psb("hmS", [128, 2 * NIC * 512], BF16)
                silu_sb = psb("silu_b", [128, 512], BF16)
                bc_sb = psb("bc_sb", [128, 512], BF16)
                ps_g = pps("ps_g", [128, 512])
                ps_u = pps("ps_u", [128, 512])
                ps_bc = pps("ps_bc", [128, 512])
                ps_r = [pps(f"ps_r{b}", [128, D]) for b in range(2)]

                W0, S0, L0W = sW.n, sS.n, sL.n
                NW = 2 * (2 * HC + NIC)
                QN8 = 8
                NGI = QN8 * NPAIRS5

                wg_, wu_, wbc, wdn, wsil, wbce, whm, wev, wfs = \
                    {}, {}, {}, {}, {}, {}, {}, {}, {}
                pe, pe2, pe3 = sPE.n, sPE2.n, sPE3.n
                act, act2, dve, ev = sACT.n, sACT2.n, sDVE.n, sEV.n
                for gi in range(NGI):
                    q, p = divmod(gi, NPAIRS5)
                    for tg in range(1):
                        for slot in range(2):
                            if p > 0:
                                pe2 += 1; wbc[gi, tg, slot] = pe2
                                act2 += 1; wbce[gi, tg, slot] = act2
                            for ic in range(NIC):
                                pe += 1; wg_[gi, tg, slot, ic] = pe
                                act += 1; wsil[gi, tg, slot, ic] = act
                                pe2 += 1; wu_[gi, tg, slot, ic] = pe2
                                dve += 1; whm[gi, tg, slot, ic] = dve
                        for t1 in range(4):
                            pe3 += 1; wdn[gi, tg, t1] = pe3
                            ev += 1; wev[gi, tg, t1] = ev
                    if p == NPAIRS5 - 1:
                        for tl in range(4):
                            act2 += 1; wfs[q, tl] = act2
                wpd = {gi: wdn[gi, 0, 3] for gi in range(NGI)}
                stq = {q: S0 + 16 * 4 * (q + 1) for q in range(QN8)}

                def bc_prev_reader(gi, tg, slot):
                    ks = [k for k in whm.keys()
                          if k < (gi, tg, slot, 0) and (k[0] % NPAIRS5) != 0]
                    return whm[ks[-1]] if ks else None

                def wslot_aps(b, p, slot):
                    res = []
                    for hc in range(HC):
                        gd = wgt[b][:, slot * HC * 512 + hc * 512:
                                    slot * HC * 512 + (hc + 1) * 512]
                        ud = wut[b][:, slot * HC * 512 + hc * 512:
                                    slot * HC * 512 + (hc + 1) * 512]
                        if p == 0:
                            gs = wsgT_s[hc * 128:(hc + 1) * 128,
                                        slot * 512:(slot + 1) * 512]
                            us = wsuT_s[hc * 128:(hc + 1) * 128,
                                        slot * 512:(slot + 1) * 512]
                        else:
                            e = (p - 1) * 2 + slot
                            gs = wgT_s[e, hc * 128:(hc + 1) * 128, :]
                            us = wuT_s[e, hc * 128:(hc + 1) * 128, :]
                        res.append((gd, gs, False)); res.append((ud, us, False))
                    for ic in range(NIC):
                        dd = wdt[b][:, slot * NIC * D + ic * D:
                                    slot * NIC * D + (ic + 1) * D]
                        if p == 0:
                            ds = wsdT_s[slot * 512 + ic * 128:
                                        slot * 512 + (ic + 1) * 128, :]
                        else:
                            e = (p - 1) * 2 + slot
                            ds = wdT_s[e, ic * 128:(ic + 1) * 128, :]
                        res.append((dd, ds, True))
                    return res

                with nc.Block() as blk:
                    @blk.gpsimd
                    def _(g):
                        for gi in range(NGI):
                            q, p = divmod(gi, NPAIRS5)
                            if q == 0:
                                g.wait_ge(sST.h, 16 * (3 + 6 * p if p else 3))
                            if gi >= 2:
                                g.wait_ge(sPE3.h, wpd[gi - 2])
                            s1 = wslot_aps(gi % 2, p, 1)
                            for dst, srcap, isd in s1:
                                if not isd:
                                    sW.d(g.dma_start(dst, srcap))
                            if gi >= 1:
                                g.wait_ge(sPE3.h, wpd[gi - 1])
                            for slot in range(2):
                                for dst, srcap, isd in wslot_aps(gi % 2, p, slot):
                                    if isd:
                                        sW.d(g.dma_start(dst, srcap))
                        g.wait_ge(sW.h, sW.n)
                        g.wait_ge(sST.h, sST.n)

                    @blk.sync
                    def _(s):
                        for gi in range(NGI):
                            q, p = divmod(gi, NPAIRS5)
                            if q == 0:
                                s.wait_ge(sST.h, 16 * (3 + 6 * p if p else 3))
                            if gi >= 2:
                                s.wait_ge(sPE3.h, wpd[gi - 2])
                            for dst, srcap, isd in wslot_aps(gi % 2, p, 0):
                                if not isd:
                                    sL.d(s.dma_start(dst, srcap))
                        s.wait_ge(sL.h, sL.n)

                    @blk.tensor
                    def _(t):
                        for gi in range(NGI):
                            q, p = divmod(gi, NPAIRS5)
                            b = gi % 2
                            t.wait_ge(sL.h, L0W + 16 * 16 * (gi + 1))
                            t.wait_ge(sW.h, W0 + 16 * (24 * gi + 16))
                            for tg in range(1):
                                tcol = q * 512
                                for slot in range(2):
                                    if p > 0:
                                        e = (p - 1) * 2 + slot
                                        prev = ordered_prev(wbce, (gi, tg, slot))
                                        if prev is not None:
                                            t.wait_ge(sACT2.h, prev)
                                        sPE2.inc(t.matmul(
                                            ps_bc[:, :],
                                            esel_bf[:, e * 128:(e + 1) * 128],
                                            combT[:, tcol:tcol + 512],
                                            start=True, stop=True))
                                    for ic in range(NIC):
                                        prev = ordered_prev(wsil, (gi, tg, slot, ic))
                                        if prev is not None:
                                            t.wait_ge(sACT.h, prev)
                                        for hc in range(HC):
                                            col = (slot * HC * 512 + hc * 512
                                                   + ic * 128)
                                            inst = t.matmul(
                                                ps_g[:, :], wgt[b][:, col:col + 128],
                                                xn_f[:, hc * S + tcol:
                                                     hc * S + tcol + 512],
                                                start=(hc == 0), stop=(hc == HC - 1))
                                        sPE.inc(inst)
                                        prev = ordered_prev(whm, (gi, tg, slot, ic))
                                        if prev is not None:
                                            t.wait_ge(sDVE.h, prev)
                                        for hc in range(HC):
                                            col = (slot * HC * 512 + hc * 512
                                                   + ic * 128)
                                            inst = t.matmul(
                                                ps_u[:, :], wut[b][:, col:col + 128],
                                                xn_f[:, hc * S + tcol:
                                                     hc * S + tcol + 512],
                                                start=(hc == 0), stop=(hc == HC - 1))
                                        sPE2.inc(inst)
                                for t1 in range(4):
                                    if tg == 0 and t1 == 0:
                                        t.wait_ge(sW.h, W0 + 16 * 24 * (gi + 1))
                                    t.wait_ge(sDVE.h, whm[gi, tg, 1, NIC - 1])
                                    prev = ordered_prev(wev, (gi, tg, t1), back=2)
                                    if prev is not None:
                                        t.wait_ge(sEV.h, prev)
                                    for slot in range(2):
                                        for ic in range(NIC):
                                            for nh in range(2):
                                                inst = t.matmul(
                                                    ps_r[t1 % 2][:, nh * 512:
                                                                 (nh + 1) * 512],
                                                    hmS[:, slot * NIC * 512
                                                        + ic * 512 + t1 * 128:
                                                        slot * NIC * 512
                                                        + ic * 512 + (t1 + 1) * 128],
                                                    wdt[b][:, slot * NIC * D + ic * D
                                                           + nh * 512:
                                                           slot * NIC * D + ic * D
                                                           + (nh + 1) * 512],
                                                    start=(slot == 0 and ic == 0),
                                                    stop=(slot == 1 and
                                                          ic == NIC - 1))
                                    sPE3.inc(inst)

                    @blk.scalar
                    def _(sc):
                        for gi in range(NGI):
                            q, p = divmod(gi, NPAIRS5)
                            for tg in range(1):
                                for slot in range(2):
                                    if p > 0:
                                        sc.wait_ge(sPE2.h, wbc[gi, tg, slot])
                                        prev = bc_prev_reader(gi, tg, slot)
                                        if prev is not None:
                                            sc.wait_ge(sDVE.h, prev)
                                        sACT2.inc(sc.copy(bc_sb[:, :], ps_bc[:, :]))
                                    for ic in range(NIC):
                                        sc.wait_ge(sPE.h, wg_[gi, tg, slot, ic])
                                        prev = ordered_prev(whm, (gi, tg, slot, ic))
                                        if prev is not None:
                                            sc.wait_ge(sDVE.h, prev)
                                        sACT.inc(sc.activation(
                                            silu_sb[:, :], ps_g[:, :], AF.Silu))
                            if p == NPAIRS5 - 1:
                                for tl in range(4):
                                    sc.wait_ge(sEV.h, wev[gi, 0, tl])
                                    a_sl = acc[:, tl * D:(tl + 1) * D]
                                    sACT2.inc(sc.activation(a_sl, a_sl, AF.Silu))
                                    sS.d(sc.dma_start(
                                        g_scr[q * 512 + tl * 128:
                                              q * 512 + (tl + 1) * 128, :],
                                        a_sl))
                        sc.wait_ge(sS.h, sS.n)

                    @blk.vector
                    def _(v):
                        for gi in range(NGI):
                            q, p = divmod(gi, NPAIRS5)
                            for tg in range(1):
                                for slot in range(2):
                                    for ic in range(NIC):
                                        v.wait_ge(sACT.h, wsil[gi, tg, slot, ic])
                                        v.wait_ge(sPE2.h, wu_[gi, tg, slot, ic])
                                        if ic == 0 and slot == 0:
                                            if gi >= 1:
                                                v.wait_ge(sPE3.h, wdn[gi - 1, 0, 3])
                                        dst = hmS[:, slot * NIC * 512 + ic * 512:
                                                  slot * NIC * 512 + (ic + 1) * 512]
                                        if p > 0:
                                            v.wait_ge(sACT2.h, wbce[gi, tg, slot])
                                            v.tensor_tensor(dst, silu_sb[:, :],
                                                            ps_u[:, :], ALU.mult)
                                            sDVE.inc(v.tensor_tensor(
                                                dst, dst, bc_sb[:, :], ALU.mult))
                                        else:
                                            sDVE.inc(v.tensor_tensor(
                                                dst, silu_sb[:, :], ps_u[:, :],
                                                ALU.mult))
                                for t1 in range(4):
                                    v.wait_ge(sPE3.h, wdn[gi, tg, t1])
                                    tl = tg * 4 + t1
                                    a_sl = acc[:, tl * D:(tl + 1) * D]
                                    if p == 0:
                                        if q > 0:
                                            v.wait_ge(sS.h, stq[q - 1])
                                        sEV.inc(v.tensor_copy(a_sl,
                                                              ps_r[t1 % 2][:, :]))
                                    else:
                                        sEV.inc(v.tensor_tensor(
                                            a_sl, a_sl, ps_r[t1 % 2][:, :],
                                            ALU.add))



        def phase_l0attn():
            QG0 = 256
            NQG0 = S // QG0
            with ExitStack() as pc:
                psb = lambda n_, s_, d=F32: pc.enter_context(nc.sbuf_tensor(n_, s_, d))
                pps = lambda n_, s_, d=F32: pc.enter_context(nc.psum_tensor(n_, s_, d))
                xnt = psb("xnt", [128, NKC * D], BF16)
                pT = [psb(f"pT0{b_}", [128, NKC * QG0], BF16) for b_ in range(2)]
                qts = psb("qts", [128, HC * QG0], BF16)
                sq2 = qts
                bnd = psb("bnd", [1, 8])
                bnd2 = psb("bnd2", [2, 16])
                bias0 = psb("bias0", [128, 1])
                rinv_sb = psb("rinv0", [128, 1])
                gbuf = psb("gbuf", [128, 2 * D])
                xbuf = psb("xbuf", [128, 2 * D])
                ps_sT = [pps(f"ps_sT{b_}", [128, 1024]) for b_ in range(2)]
                ps_a = pps("ps_a0", [128, 512])
                ps_b = pps("ps_b0", [128, 513])
                ps_bnd = pps("ps_bnd", [2, 512])

                L0, S0 = sL.n, sS.n
                wsqd, wbmm, wbrd = {}, {}, {}
                act2, pe3, dve2 = sACT2.n, sPE3.n, sDVE2.n
                for ts in range(8):
                    for c in range(HC):
                        act2 += 1; wsqd[ts, c] = act2
                        pe3 += 1; wbmm[ts, c] = pe3
                    dve2 += 1; wbrd[ts] = dve2
                dve2 += 1; w_cmb = dve2
                pe3 += 1; w_e2t = pe3
                dve2 += 1; w_prod = dve2
                act2 += 1; w_bM = act2
                pe3 += 1; w_bc0 = pe3
                dve2 += 1; w_bias0 = dve2

                wqts, wqk, wexp, wpv, wup = {}, {}, {}, {}, {}
                dve, pe, act, pe2, ev = sDVE.n, sPE.n, sACT.n, sPE2.n, sEV.n
                for qg in range(NQG0):
                    dve += 1; wqts[qg] = dve
                    for kq in range(8):
                        pe += 1; wqk[qg, kq] = pe
                        act += 1; wexp[qg, kq] = act
                    for qs in range(2):
                        pe2 += 1; wpv[qg, qs] = pe2
                        ev += 1; wup[qg, qs] = ev

                with nc.Block() as blk:
                    @blk.sync
                    def _(s):
                        for kc in range(NKC):
                            sL.d(s.dma_start(
                                xnt[:, kc * D:(kc + 1) * D],
                                xn_dram[kc * 128:(kc + 1) * 128, :]))
                        for n in range(2):
                            row = n * 128
                            sL.d(s.dma_start(gbuf[:, (n % 2) * D:(n % 2 + 1) * D],
                                             g_scr[row:row + 128, :]))
                            sL.d(s.dma_start(xbuf[:, (n % 2) * D:(n % 2 + 1) * D],
                                             x_in[row:row + 128, :]))
                        for n in range(NT):
                            qg, qs = divmod(n, 2)
                            row = n * 128
                            s.wait_ge(sEV.h, wup[qg, qs])
                            sS.d(s.dma_start(
                                x_resid[row:row + 128, :],
                                xbuf[:, (n % 2) * D:(n % 2 + 1) * D]))
                            if n + 2 < NT:
                                s.wait_ge(sS.h, S0 + 16 * (n + 1))
                                row2 = (n + 2) * 128
                                sL.d(s.dma_start(
                                    gbuf[:, (n % 2) * D:(n % 2 + 1) * D],
                                    g_scr[row2:row2 + 128, :]))
                                sL.d(s.dma_start(
                                    xbuf[:, (n % 2) * D:(n % 2 + 1) * D],
                                    x_in[row2:row2 + 128, :]))
                        s.wait_ge(sS.h, sS.n)

                    @blk.scalar
                    def _(sc):
                        for ts in range(8):
                            for c in range(HC):
                                nsq = ts * HC + c
                                if nsq >= 2:
                                    pts, pcc = divmod(nsq - 2, HC)
                                    sc.wait_ge(sPE3.h, wbmm[pts, pcc])
                                sACT2.inc(sc.activation(
                                    sq2[:, (c % 2) * 512:(c % 2 + 1) * 512],
                                    xn_f[:, c * S + ts * 512:
                                         c * S + (ts + 1) * 512],
                                    AF.Square))
                        sc.wait_ge(sDVE2.h, w_prod)
                        sc.activation(bnd[:, 2:3], bnd[:, 1:2], AF.Ln)
                        sACT2.inc(sc.activation(bnd[:, 2:3], bnd[:, 2:3],
                                                AF.Exp, scale=0.5))
                        for qg in range(NQG0):
                            for kq in range(8):
                                sc.wait_ge(sPE.h, wqk[qg, kq])
                                if qg == 0 and kq == 0:
                                    sc.wait_ge(sDVE2.h, w_bias0)
                                if qg >= 2 and kq == 0:
                                    sc.wait_ge(sPE2.h, wpv[qg - 2, 1])
                                sACT.inc(sc.activation(
                                    pT[qg % 2][:, kq * 1024:(kq + 1) * 1024],
                                    ps_sT[kq % 2][:, :],
                                    AF.Exp, bias=bias0[:, 0:1], scale=1.0))

                    @blk.tensor
                    def _(t):
                        for ts in range(8):
                            if ts >= 1:
                                t.wait_ge(sDVE2.h, wbrd[ts - 1])
                            for c in range(HC):
                                t.wait_ge(sACT2.h, wsqd[ts, c])
                                sPE3.inc(t.matmul(
                                    ps_bnd[:, :], bndsel[:, 2 * c:2 * c + 2],
                                    sq2[:, (c % 2) * 512:(c % 2 + 1) * 512],
                                    start=(c == 0), stop=(c == HC - 1)))
                        t.wait_ge(sDVE2.h, w_cmb)
                        sPE3.inc(t.transpose(ps_bnd[0:1, 0:2], bnd2[:, 8:9],
                                             id_f[0:2, 0:2]))
                        t.wait_ge(sACT2.h, w_bM)
                        sPE3.inc(t.matmul(ps_a[:, 0:1], onr_f[:, :], bnd[:, 2:3],
                                          start=True, stop=True))
                        for qg in range(NQG0):
                            t.wait_ge(sDVE.h, wqts[qg])
                            for kq in range(8):
                                prev = ordered_prev(wexp, (qg, kq), back=2)
                                if prev is not None:
                                    t.wait_ge(sACT.h, prev)
                                for kci in range(4):
                                    kc = kq * 4 + kci
                                    for c in range(HC):
                                        inst = t.matmul(
                                            ps_sT[kq % 2][:, kci * QG0:
                                                          (kci + 1) * QG0],
                                            xn_f[:, c * S + kc * 128:
                                                 c * S + (kc + 1) * 128],
                                            qts[:, c * QG0:(c + 1) * QG0],
                                            start=(c == 0), stop=(c == HC - 1))
                                sPE.inc(inst)
                            t.wait_ge(sACT.h, wexp[qg, 7])
                            for qs in range(2):
                                if qg == 0 and qs == 0:
                                    t.wait_ge(sDVE2.h, w_bias0)
                                prev = ordered_prev(wup, (qg, qs))
                                if prev is not None:
                                    t.wait_ge(sEV.h, prev)
                                for kc in range(NKC):
                                    lhs = pT[qg % 2][:, kc * QG0 + qs * 128:
                                             kc * QG0 + (qs + 1) * 128]
                                    st_ = (kc == 0); sp_ = (kc == NKC - 1)
                                    t.matmul(ps_a[:, :], lhs,
                                             xnt[:, kc * D:kc * D + 512],
                                             start=st_, stop=sp_)
                                    t.matmul(ps_b[:, 0:512], lhs,
                                             xnt[:, kc * D + 512:(kc + 1) * D],
                                             start=st_, stop=sp_)
                                    inst = t.matmul(ps_b[:, 512:513], lhs,
                                                    onc_bf[:, :],
                                                    start=st_, stop=sp_)
                                sPE2.inc(inst)

                    @blk.vector
                    def _(v):
                        for ts in range(8):
                            v.wait_ge(sPE3.h, wbmm[ts, HC - 1])
                            sDVE2.inc(v.reduce_max(bnd2[:, ts:ts + 1],
                                                   ps_bnd[:, :], AX.X))
                        v.wait_ge(sDVE2.h, wbrd[7])
                        sDVE2.inc(v.reduce_max(bnd2[:, 8:9], bnd2[:, 0:8], AX.X))
                        v.wait_ge(sPE3.h, w_e2t)
                        vg(v, v.tensor_copy(bnd[:, 4:6], ps_bnd[0:1, 0:2]))
                        sDVE2.inc(v.tensor_mul(bnd[:, 1:2], bnd[:, 4:5],
                                               bnd[:, 5:6]))
                        v.wait_ge(sPE3.h, w_bc0)
                        sDVE2.inc(v.tensor_scalar(bias0[:, :], ps_a[:, 0:1],
                                                  -1.02, None, ALU.mult))
                        n = 0
                        for qg in range(NQG0):
                            if qg == 0:
                                v.wait_ge(sPE3.h, wbmm[7, HC - 1])
                            if qg >= 1:
                                v.wait_ge(sPE.h, wqk[qg - 1, 7])
                            for c in range(HC):
                                inst = v.tensor_scalar(
                                    qts[:, c * QG0:(c + 1) * QG0],
                                    xn_f[:, c * S + qg * QG0:
                                         c * S + (qg + 1) * QG0],
                                    w_vec[:, c:c + 1], None, ALU.mult)
                            sDVE.inc(inst)
                            for qs in range(2):
                                par = n % 2
                                v.wait_ge(sPE2.h, wpv[qg, qs])
                                v.wait_ge(sL.h, L0 + 16 * NKC + 32 * (n + 1))
                                vg(v, v.reciprocal(rinv_sb[:, :],
                                                   ps_b[:, 512:513]))
                                gb1 = gbuf[:, par * D:par * D + 512]
                                gb2 = gbuf[:, par * D + 512:(par + 1) * D]
                                v.tensor_scalar(gb1, gb1, rinv_sb[:, 0:1],
                                                None, ALU.mult)
                                v.tensor_tensor(gb1, gb1, ps_a[:, :], ALU.mult)
                                v.tensor_add(xbuf[:, par * D:par * D + 512],
                                             gb1,
                                             xbuf[:, par * D:par * D + 512])
                                v.tensor_scalar(gb2, gb2, rinv_sb[:, 0:1],
                                                None, ALU.mult)
                                v.tensor_tensor(gb2, gb2, ps_b[:, 0:512],
                                                ALU.mult)
                                sEV.inc(v.tensor_add(
                                    xbuf[:, par * D + 512:(par + 1) * D],
                                    gb2,
                                    xbuf[:, par * D + 512:(par + 1) * D]))
                                n += 1

        def phase_l1ffn():
            with ExitStack() as pc:
                psb = lambda n_, s_, d=F32: pc.enter_context(nc.sbuf_tensor(n_, s_, d))
                pps = lambda n_, s_, d=F32: pc.enter_context(nc.psum_tensor(n_, s_, d))
                cbn = psb("cbn", [128, 2 * D])
                cbs = psb("cbs", [128, 2 * D], BF16)
                CBT = psb("CBT", [128, HC * D], BF16)
                g1 = psb("g1", [128, 2 * D])
                ps_t = [pps(f"ps_t1{b}", [128, 128], BF16) for b in range(2)]
                ps_f = [pps(f"ps_f{b}", [128, D]) for b in range(2)]

                L0, S0 = sL.n, sS.n
                wsc, wtr, wcp = {}, {}, {}
                dve, pe3, dve2 = sDVE.n, sPE3.n, sDVE2.n
                for nck in range(HC):
                    dve += 1; wsc[nck] = dve
                    for kc in range(HC):
                        pe3 += 1; wtr[nck, kc] = pe3
                        dve2 += 1; wcp[nck, kc] = dve2
                wmm, wsl = {}, {}
                pe, act = sPE.n, sACT.n
                for i in range(NT):
                    pe += 1; wmm[i] = pe
                    act += 1; wsl[i] = act

                with nc.Block() as blk:
                    @blk.sync
                    def _(s):
                        for nck in range(HC):
                            if nck >= 2:
                                s.wait_ge(sDVE.h, wsc[nck - 2])
                            sL.d(s.dma_start(
                                cbn[:, (nck % 2) * D:(nck % 2 + 1) * D],
                                cb_d[nck * 128:(nck + 1) * 128, :]))
                        for i in range(NT):
                            s.wait_ge(sACT.h, wsl[i])
                            sS.d(s.dma_start(
                                g_scr[i * 128:(i + 1) * 128, :],
                                g1[:, (i % 2) * D:(i % 2 + 1) * D]))
                        s.wait_ge(sS.h, sS.n)

                    @blk.vector
                    def _(v):
                        for nck in range(HC):
                            v.wait_ge(sL.h, L0 + 16 * (nck + 1))
                            if nck >= 2:
                                v.wait_ge(sPE3.h, wtr[nck - 2, HC - 1])
                            sDVE.inc(v.tensor_scalar(
                                cbs[:, (nck % 2) * D:(nck % 2 + 1) * D],
                                cbn[:, (nck % 2) * D:(nck % 2 + 1) * D],
                                scb_sb[:, nck:nck + 1], 1.0 / 127.0,
                                ALU.mult, ALU.mult))
                            for kc in range(HC):
                                v.wait_ge(sPE3.h, wtr[nck, kc])
                                dstv = CBT.rearrange("p (k n) -> p k n", k=HC)
                                sDVE2.inc(v.tensor_copy(
                                    dstv[:, kc, nck * 128:(nck + 1) * 128],
                                    ps_t[kc % 2][:, :]))

                    @blk.tensor
                    def _(t):
                        for nck in range(HC):
                            t.wait_ge(sDVE.h, wsc[nck])
                            for kc in range(HC):
                                prev = ordered_prev(wcp, (nck, kc), back=2)
                                if prev is not None:
                                    t.wait_ge(sDVE2.h, prev)
                                sPE3.inc(t.transpose(
                                    ps_t[kc % 2][:, :],
                                    cbs[:, (nck % 2) * D + kc * 128:
                                        (nck % 2) * D + (kc + 1) * 128],
                                    id_bf[:, :]))
                        for i in range(NT):
                            if i >= 2:
                                t.wait_ge(sACT.h, wsl[i - 2])
                            if i == 0:
                                t.wait_ge(sDVE2.h, wcp[HC - 1, HC - 1])
                            for nh in range(2):
                                for kc in range(HC):
                                    t.matmul(ps_f[i % 2][:, nh * 512:(nh + 1) * 512],
                                             xn_f[:, kc * S + i * 128:
                                                  kc * S + (i + 1) * 128],
                                             CBT[:, kc * D + nh * 512:
                                                 kc * D + (nh + 1) * 512],
                                             start=(kc == 0), stop=False)
                                inst = t.matmul(
                                    ps_f[i % 2][:, nh * 512:(nh + 1) * 512],
                                    onr_bf[:, :],
                                    b1_bf[0:1, nh * 512:(nh + 1) * 512],
                                    start=False, stop=True)
                            sPE.inc(inst)

                    @blk.scalar
                    def _(sc):
                        for i in range(NT):
                            sc.wait_ge(sPE.h, wmm[i])
                            if i >= 2:
                                sc.wait_ge(sS.h, S0 + 16 * (i - 1))
                            sACT.inc(sc.activation(
                                g1[:, (i % 2) * D:(i % 2 + 1) * D],
                                ps_f[i % 2][:, :], AF.Silu))

        def phase_l1attn():
            with ExitStack() as pc:
                psb = lambda n_, s_, d=F32: pc.enter_context(nc.sbuf_tensor(n_, s_, d))
                pps = lambda n_, s_, d=F32: pc.enter_context(nc.psum_tensor(n_, s_, d))
                wq_sb = psb("wq_sb", [128, HC * D], BF16)
                wk_sb = psb("wk_sb", [128, HC * D], BF16)
                wv_sb = psb("wv_sb", [128, HC * D], BF16)
                pbuf = [psb(f"pj{w}", [128, 4 * D], BF16) for w in range(3)]
                prodV = psb("prodV", [128, 8192], BF16)
                pairV = psb("pairV", [128, 6144], BF16)
                SCV = psb("SCV", [128, 256])
                ppV = psb("ppV", [128, 256], BF16)
                atV = psb("atV", [128, 256], BF16)
                sumV = psb("sumV", [128, 32])
                Ob = psb("Obf", [128, 2 * D])
                ps_p = [pps(f"ps_p{w}", [128, D]) for w in range(3)]

                W0 = sW.n
                wpm, wev, wexV = {}, {}, {}
                wSCV, wATV, wP2V, wObV = {}, {}, {}, {}
                pe, a2, act, d2 = sPE.n, sACT2.n, sACT.n, sDVE2.n
                for i in range(NT):
                    for w in range(3):
                        pe += 1; wpm[i, w] = pe
                        a2 += 1; wev[i, w] = a2
                for i in range(NT):
                    act += 1; wexV[i] = act
                    d2 += 1; wSCV[i] = d2
                    d2 += 1; wATV[i] = d2
                    d2 += 1; wP2V[i] = d2
                    d2 += 1; wObV[i] = d2
                WB = W0 + 16 * 24
                stF2 = {i: WB + 16 * (i + 1) for i in range(NT)}

                with nc.Block() as blk:
                    @blk.gpsimd
                    def _(g):
                        for wsrc, wbuf in ((wqT_d, wq_sb), (wkT_d, wk_sb),
                                           (wvTs_d, wv_sb)):
                            for hc in range(HC):
                                sW.d(g.dma_start(
                                    wbuf[:, hc * D:(hc + 1) * D],
                                    wsrc[hc * 128:(hc + 1) * 128, :]))
                        for i in range(NT):
                            g.wait_ge(sDVE2.h, wObV[i])
                            sW.d(g.dma_start(
                                xn_dram[i * 128:(i + 1) * 128, :],
                                Ob[:, (i % 2) * D:(i % 2 + 1) * D]))
                        g.wait_ge(sW.h, stF2[NT - 1])
                        otok = xn_dram
                        ov = otok.rearrange("(r s) c -> r s c", s=16)
                        for p in range(NPAIR):
                            dst = ao_hbm[p].rearrange("(q rr) (h d) -> q rr h d",
                                                      q=16, h=2)
                            for q in range(16):
                                sW.d(g.dma_start(
                                    dst[q, :, :, :],
                                    ov[:, 2 * p:2 * p + 2,
                                       q * 64:(q + 1) * 64]))
                        g.wait_ge(sW.h, sW.n)

                    @blk.tensor
                    def _(t):
                        t.wait_ge(sW.h, W0 + 16 * 24)
                        for i in range(NT):
                            for w in range(3):
                                wsb = (wq_sb, wk_sb, wv_sb)[w]
                                brow = (bqr_bf, bkr_bf, bvs_bf)[w]
                                prev = ordered_prev(wev, (i, w), back=3)
                                if prev is not None:
                                    t.wait_ge(sACT2.h, prev)
                                for hc in range(HC):
                                    for nh in range(2):
                                        t.matmul(
                                            ps_p[w][:, nh * 512:(nh + 1) * 512],
                                            xn_f[:, hc * S + i * 128:
                                                 hc * S + (i + 1) * 128],
                                            wsb[:, hc * D + nh * 512:
                                                hc * D + (nh + 1) * 512],
                                            start=(hc == 0), stop=False)
                                for nh in range(2):
                                    inst = t.matmul(
                                        ps_p[w][:, nh * 512:(nh + 1) * 512],
                                        onr_bf[:, :],
                                        brow[0:1, nh * 512:(nh + 1) * 512],
                                        start=False, stop=True)
                                sPE.inc(inst)

                    @blk.vector
                    def _(v):
                        for i in range(NT):
                            par = i % 2
                            s4 = (i % 4) * D
                            v.wait_ge(sACT2.h, wev[i, 1])
                            if i >= 1:
                                v.wait_ge(sACT.h, wexV[i - 1])
                            for c in range(2):
                                qv = bass.AP(pbuf[0], s4 + c * 512,
                                             [[4 * D, 128], [64, 8], [0, 16],
                                              [1, 64]])
                                kv = bass.AP(pbuf[1], s4,
                                             [[4 * D, 128], [0, 8], [64, 16],
                                              [1, 64]])
                                v.tensor_tensor(prodV[:, :], qv, kv, ALU.mult)
                                v.tensor_tensor(
                                    bass.AP(pairV, 0,
                                            [[6144, 128], [32, 128], [1, 32]]),
                                    bass.AP(prodV, 0,
                                            [[8192, 128], [64, 128], [1, 32]]),
                                    bass.AP(prodV, 32,
                                            [[8192, 128], [64, 128], [1, 32]]),
                                    ALU.add)
                                v.tensor_tensor(
                                    bass.AP(pairV, 4096,
                                            [[6144, 128], [16, 128], [1, 16]]),
                                    bass.AP(pairV, 0,
                                            [[6144, 128], [32, 128], [1, 16]]),
                                    bass.AP(pairV, 16,
                                            [[6144, 128], [32, 128], [1, 16]]),
                                    ALU.add)
                                inst = v.reduce_sum(
                                    SCV[:, c * 128:(c + 1) * 128],
                                    bass.AP(pairV, 4096,
                                            [[6144, 128], [16, 128], [1, 16]]),
                                    AX.X)
                            sDVE2.inc(inst)
                            v.wait_ge(sACT.h, wexV[i])
                            vg(v, v.reduce_sum(
                                sumV[:, 0:16],
                                ppV.rearrange("p (a b) -> p a b", b=16), AX.X))
                            vg(v, v.reciprocal(sumV[:, 16:32], sumV[:, 0:16]))
                            rb = bass.AP(sumV, 16,
                                         [[32, 128], [1, 16], [0, 16]])
                            sDVE2.inc(v.tensor_tensor(atV[:, :], ppV[:, :], rb,
                                                      ALU.mult))
                            v.wait_ge(sACT2.h, wev[i, 2])
                            if i >= 2:
                                v.wait_ge(sW.h, stF2[i - 2])
                            for c in range(2):
                                av = bass.AP(atV, c * 128,
                                             [[256, 128], [16, 8], [0, 64],
                                              [1, 16]])
                                vv = bass.AP(pbuf[2], s4,
                                             [[4 * D, 128], [0, 8], [16, 64],
                                              [1, 16]])
                                inst = v.tensor_tensor(prodV[:, :], av, vv,
                                                       ALU.mult)
                                if c == 1:
                                    sDVE2.inc(inst)
                                v.tensor_tensor(
                                    bass.AP(pairV, 0,
                                            [[6144, 128], [8, 512], [1, 8]]),
                                    bass.AP(prodV, 0,
                                            [[8192, 128], [16, 512], [1, 8]]),
                                    bass.AP(prodV, 8,
                                            [[8192, 128], [16, 512], [1, 8]]),
                                    ALU.add)
                                v.tensor_tensor(
                                    bass.AP(pairV, 4096,
                                            [[6144, 128], [4, 512], [1, 4]]),
                                    bass.AP(pairV, 0,
                                            [[6144, 128], [8, 512], [1, 4]]),
                                    bass.AP(pairV, 4,
                                            [[6144, 128], [8, 512], [1, 4]]),
                                    ALU.add)
                                inst = v.reduce_sum(
                                    Ob[:, par * D + c * 512:
                                       par * D + (c + 1) * 512],
                                    bass.AP(pairV, 4096,
                                            [[6144, 128], [4, 512], [1, 4]]),
                                    AX.X)
                            sDVE2.inc(inst)

                    @blk.scalar
                    def _(sc):
                        for ii in range(NT + 2):
                            j = ii - 2
                            if j >= 0:
                                sc.wait_ge(sDVE2.h, wSCV[j])
                                if j >= 1:
                                    sc.wait_ge(sDVE2.h, wATV[j - 1])
                                sACT.inc(sc.activation(ppV[:, :], SCV[:, :],
                                                       AF.Exp, scale=0.125))
                            if ii < NT:
                                if ii >= 4:
                                    sc.wait_ge(sDVE2.h, wP2V[ii - 4])
                                for w in range(3):
                                    sc.wait_ge(sPE.h, wpm[ii, w])
                                    sACT2.inc(sc.copy(
                                        pbuf[w][:, (ii % 4) * D:
                                                (ii % 4 + 1) * D],
                                        ps_p[w][:, :]))

        def phase_l1out():
            with ExitStack() as pc:
                psb = lambda n_, s_, d=F32: pc.enter_context(nc.sbuf_tensor(n_, s_, d))
                pps = lambda n_, s_, d=F32: pc.enter_context(nc.psum_tensor(n_, s_, d))
                woT_sb = psb("woT_sb", [128, HC * D], BF16)
                aoT = psb("aoT", [128, HC * S], BF16)
                g1b = psb("g1b", [128, 2 * D])
                x1b = psb("x1b", [128, 2 * D])
                ob = psb("ob", [128, 2 * D])
                tmpG = psb("tmpG", [128, 512])
                ps_s = [pps(f"ps_s{b}", [128, D]) for b in range(2)]

                W0, L0, S0 = sW.n, sL.n, sS.n
                wmmG, wupG = {}, {}
                pe, ev = sPE.n, sEV.n
                for i in range(NT):
                    pe += 1; wmmG[i] = pe
                    ev += 1; wupG[i] = ev

                with nc.Block() as blk:
                    @blk.gpsimd
                    def _(g):
                        for pch in range(HC):
                            sW.d(g.dma_start(
                                woT_sb[:, pch * D:(pch + 1) * D],
                                woT_d[pch * 128:(pch + 1) * 128, :]))
                        g.wait_ge(sW.h, sW.n)

                    @blk.sync
                    def _(s):
                        for p in range(NPAIR):
                            sL.d(s.dma_start_transpose(
                                aoT[:, p * S:(p + 1) * S], ao_hbm[p][:, :]))
                        for i in range(2):
                            sL.d(s.dma_start(
                                g1b[:, (i % 2) * D:(i % 2 + 1) * D],
                                g_scr[i * 128:(i + 1) * 128, :]))
                            sL.d(s.dma_start(
                                x1b[:, (i % 2) * D:(i % 2 + 1) * D],
                                x_resid[i * 128:(i + 1) * 128, :]))
                        for i in range(NT):
                            s.wait_ge(sEV.h, wupG[i])
                            sS.d(s.dma_start(
                                out_d[i * 128:(i + 1) * 128, :],
                                ob[:, (i % 2) * D:(i % 2 + 1) * D]))
                            if i + 2 < NT:
                                s.wait_ge(sEV.h, wupG[i])
                                sL.d(s.dma_start(
                                    g1b[:, (i % 2) * D:(i % 2 + 1) * D],
                                    g_scr[(i + 2) * 128:(i + 3) * 128, :]))
                                sL.d(s.dma_start(
                                    x1b[:, (i % 2) * D:(i % 2 + 1) * D],
                                    x_resid[(i + 2) * 128:(i + 3) * 128, :]))
                        s.wait_ge(sS.h, sS.n)

                    @blk.tensor
                    def _(t):
                        t.wait_ge(sW.h, sW.n)
                        t.wait_ge(sL.h, L0 + 16 * NPAIR)
                        for i in range(NT):
                            if i >= 2:
                                t.wait_ge(sEV.h, wupG[i - 2])
                            for nh in range(2):
                                for pch in range(HC):
                                    t.matmul(ps_s[i % 2][:, nh * 512:(nh + 1) * 512],
                                             aoT[:, pch * S + i * 128:
                                                  pch * S + (i + 1) * 128],
                                             woT_sb[:, pch * D + nh * 512:
                                                    pch * D + (nh + 1) * 512],
                                             start=(pch == 0), stop=False)
                                inst = t.matmul(
                                    ps_s[i % 2][:, nh * 512:(nh + 1) * 512],
                                    onr_bf[:, :],
                                    bo_bf[0:1, nh * 512:(nh + 1) * 512],
                                    start=False, stop=True)
                            sPE.inc(inst)

                    @blk.vector
                    def _(v):
                        for i in range(NT):
                            par = i % 2
                            v.wait_ge(sPE.h, wmmG[i])
                            v.wait_ge(sL.h, L0 + 16 * NPAIR + 32 * (i + 1))
                            if i >= 2:
                                v.wait_ge(sS.h, S0 + 16 * (i - 1))
                            for nh in range(2):
                                v.tensor_tensor(
                                    tmpG[:, :],
                                    ps_s[par][:, nh * 512:(nh + 1) * 512],
                                    g1b[:, par * D + nh * 512:
                                        par * D + (nh + 1) * 512], ALU.mult)
                                inst = v.tensor_add(
                                    ob[:, par * D + nh * 512:
                                       par * D + (nh + 1) * 512],
                                    tmpG[:, :],
                                    x1b[:, par * D + nh * 512:
                                        par * D + (nh + 1) * 512])
                            sEV.inc(inst)

        phase_ln(0)
        if n_phases >= 2:
            phase_xnf()
            phase_combine()
        if n_phases >= 3:
            phase_l0ffn()
        if n_phases >= 4:
            phase_l0attn()
        if n_phases >= 5:
            phase_ln(1)
            phase_xnf()
        if n_phases >= 6:
            phase_l1ffn()
        if n_phases >= 7:
            phase_l1attn()
        if n_phases >= 8:
            phase_l1out()

        if debug:
            with nc.Block() as blk:
                dbgA = dt("dbgA", [128, NT * 8], F32, "ExternalOutput")
                dbgB = dt("dbgB", [8, NT * 128], F32, "ExternalOutput")
                dbgC = dt("dbgC", [128, HC * S], F32, "ExternalOutput")
                dbg_xnd = dt("dbg_xnd", [S, D], F32, "ExternalOutput")
                dbg_xres = dt("dbg_xres", [S, D], F32, "ExternalOutput")
                dbg_g = dt("dbg_g", [S, D], F32, "ExternalOutput")
                dbg_ao = dt("dbg_ao", [NPAIR, S, 128], F32, "ExternalOutput")

                @blk.gpsimd
                def _(g):
                    sW.d(g.dma_start(dbgA[:, :], logits_sb[:, :]))
                    sW.d(g.dma_start(dbg_xnd[:, :], xn_dram[:, :]))
                    if n_phases >= 2:
                        sW.d(g.dma_start(dbgB[:, :], combT[:, :]))
                        sW.d(g.dma_start(dbgC[:, :], xn_f[:, :]))
                    if n_phases >= 3:
                        sW.d(g.dma_start(dbg_g[:, :], g_scr[:, :]))
                    if n_phases >= 4:
                        sW.d(g.dma_start(dbg_xres[:, :], x_resid[:, :]))
                    if n_phases >= 7:
                        for p in range(NPAIR):
                            sW.d(g.dma_start(dbg_ao[p, :, :], ao_hbm[p, :, :]))
                    g.wait_ge(sW.h, sW.n)

    return nc


def _layout_inputs(inputs):
    f = np.ascontiguousarray
    esel = np.zeros((8, 1024), np.float32)
    for e in range(8):
        esel[e, e * 128:(e + 1) * 128] = 1.0
    common = {
        "wl": f(np.asarray(inputs["L0_attn_w_lhs"], np.float32).reshape(8, 128).T),
        "wr": f(np.asarray(inputs["L0_attn_w_rhs"], np.float32).reshape(8, 128).T),
        "gwT": f(np.asarray(inputs["L0_ffn_gate_weight"], np.float32).T),
        "wgT": f(np.transpose(np.asarray(inputs["L0_ffn_w_gate"], np.float32),
                              (0, 2, 1))),
        "wuT": f(np.transpose(np.asarray(inputs["L0_ffn_w_up"], np.float32),
                              (0, 2, 1))),
        "wdT": f(np.transpose(np.asarray(inputs["L0_ffn_w_down"], np.float32),
                              (0, 2, 1))),
        "wsgT": f(np.asarray(inputs["L0_ffn_ws_gate"], np.float32).T),
        "wsuT": f(np.asarray(inputs["L0_ffn_ws_up"], np.float32).T),
        "wsdT": f(np.asarray(inputs["L0_ffn_ws_down"], np.float32).T),
        "wqT": f(np.asarray(inputs["L1_attn_Wq"], np.float32).T),
        "wkT": f(np.asarray(inputs["L1_attn_Wk"], np.float32).T),
        "wvTs": f(np.asarray(inputs["L1_attn_Wv"], np.float32).T
                  .reshape(1024, 16, 64).transpose(0, 2, 1).reshape(1024, 1024)),
        "woT": f(np.asarray(inputs["L1_attn_Wo"], np.float32).T),
        "bqrow": f(np.asarray(inputs["L1_attn_bq"], np.float32).reshape(1, 1024)),
        "bkrow": f(np.asarray(inputs["L1_attn_bk"], np.float32).reshape(1, 1024)),
        "bvsrow": f(np.asarray(inputs["L1_attn_bv"], np.float32)
                    .reshape(16, 64).T.reshape(1, 1024)),
        "borow": f(np.asarray(inputs["L1_attn_bo"], np.float32).reshape(1, 1024)),
        "cb": f(np.asarray(inputs["L1_ffn_CB"], np.float32)),
        "scb": f(np.asarray(inputs["L1_ffn_SCB"], np.float32).reshape(8, 128).T),
        "b1row": f(np.asarray(inputs["L1_ffn_bias"], np.float32).reshape(1, 1024)),
        "ident": np.eye(128, dtype=np.float32),
        "onesrow": np.ones((1, 128), np.float32),
        "onescol": np.ones((128, 1), np.float32),
        "esel": esel,
    }
    x = np.asarray(inputs["L0_attn_x"], np.float32)
    return [dict(common, x=f(x[c])) for c in range(8)]


def kernel(**inputs):
    in_maps = _layout_inputs(inputs)
    nc = build_nc()
    res = run_bass_kernel_spmd(nc, in_maps, core_ids=list(range(8)))
    return np.stack([r["out"] for r in res.results], axis=0)


# version 13
# speedup vs baseline: 1.2098x; 1.0219x over previous
import numpy as np
from contextlib import ExitStack

import concourse.bass as bass
import concourse.mybir as mybir
from concourse import library_config
from concourse.bass_utils import run_bass_kernel_spmd

F32 = mybir.dt.float32
BF16 = mybir.dt.bfloat16
AF = mybir.ActivationFunctionType
ALU = mybir.AluOpType
AX = mybir.AxisListType

S = 4096
D = 1024
HC = 8
NT = 32
NKC = 32
QG = 512
NQG = S // QG
NEXP = 8
NIC = 4
NPAIR = 8
EPS = 1e-5
NPAIRS5 = 5
QN = 4


def ordered_prev(d, key, back=1):
    ks = list(d.keys())
    i = ks.index(key)
    return d[ks[i - back]] if i - back >= 0 else None


class Sem:
    def __init__(self, nc, ctx, name):
        self.h = ctx.enter_context(nc.semaphore(name))
        self.n = 0

    def inc(self, inst, k=1):
        inst.then_inc(self.h, k)
        self.n += k
        return self.n

    def d(self, inst):
        return self.inc(inst, 16)


def build_nc(n_phases=99, debug=False):
    nc = bass.Bass("TRN2", target_bir_lowering=False, dynamic_dma_scratch_size=16384, detect_race_conditions=False)

    dt = lambda name, shape, dtype=F32, kind="ExternalInput": nc.dram_tensor(
        name, shape, dtype, kind=kind)
    x_in = dt("x", [S, D])
    wl_d = dt("wl", [128, 8]);  wr_d = dt("wr", [128, 8])
    gwT_d = dt("gwT", [D, 8])
    wgT_d = dt("wgT", [NEXP, D, 512]); wuT_d = dt("wuT", [NEXP, D, 512])
    wdT_d = dt("wdT", [NEXP, 512, D])
    wsgT_d = dt("wsgT", [D, D]); wsuT_d = dt("wsuT", [D, D]); wsdT_d = dt("wsdT", [D, D])
    wqT_d = dt("wqT", [D, D]); wkT_d = dt("wkT", [D, D])
    wvTs_d = dt("wvTs", [D, D]); woT_d = dt("woT", [D, D])
    bqr_d = dt("bqrow", [1, D]); bkr_d = dt("bkrow", [1, D])
    bvs_d = dt("bvsrow", [1, D]); bo_d = dt("borow", [1, D])
    cb_d = dt("cb", [D, D]); scb_d = dt("scb", [128, 8]); b1_d = dt("b1row", [1, D])
    id_d = dt("ident", [128, 128])
    onr_d = dt("onesrow", [1, 128]); onc_d = dt("onescol", [128, 1])
    esel_d = dt("esel", [8, 1024])
    out_d = dt("out", [S, D], F32, "ExternalOutput")

    xn_dram = dt("xn_dram", [S, D], BF16, "Internal")
    qkv_dram = dt("qkv_dram", [3, S, D], BF16, "Internal")
    wgT_s = dt("wgT_s", [NEXP, D, 512], BF16, "Internal")
    wuT_s = dt("wuT_s", [NEXP, D, 512], BF16, "Internal")
    wdT_s = dt("wdT_s", [NEXP, 512, D], BF16, "Internal")
    wsgT_s = dt("wsgT_s", [D, D], BF16, "Internal")
    wsuT_s = dt("wsuT_s", [D, D], BF16, "Internal")
    wsdT_s = dt("wsdT_s", [D, D], BF16, "Internal")
    x_resid = dt("x_resid", [S, D], F32, "Internal")
    g_scr = dt("g_scr", [S, D], F32, "Internal")
    ao_hbm = dt("ao_hbm", [NPAIR, S, 128], BF16, "Internal")

    with ExitStack() as ctx:
        sb = lambda name, shape, d=F32: ctx.enter_context(nc.sbuf_tensor(name, shape, d))
        sem = lambda name: Sem(nc, ctx, name)

        xn_f = sb("xn_f", [128, HC * S], BF16)
        logits_sb = sb("logits", [128, NT * 8])
        combT = sb("combT", [8, NT * 128], BF16)
        id_bf = sb("id_bf", [128, 128], BF16)
        id_f = sb("id_f", [128, 128])
        onr_bf = sb("onr_bf", [1, 128], BF16)
        onr_f = sb("onr_f", [1, 128])
        onc_bf = sb("onc_bf", [128, 1], BF16)
        esel_bf = sb("esel_bf", [8, 1024], BF16)
        w_vec = sb("w_vec", [128, 8])
        bndsel = sb("bndsel", [128, 16], BF16)
        gwT_sb = sb("gwT_sb", [128, HC * 8])
        gwst = sb("gwst", [128, HC * 16], BF16)
        gwlo_t = sb("gwlo_t", [128, HC * 8])
        scb_sb = sb("scb_sb", [128, 8])
        bqr_bf = sb("bqr_bf", [1, D], BF16); bkr_bf = sb("bkr_bf", [1, D], BF16)
        bvs_bf = sb("bvs_bf", [1, D], BF16); bo_bf = sb("bo_bf", [1, D], BF16)
        b1_bf = sb("b1_bf", [1, D], BF16)
        wlr = sb("wlr", [128, 16])
        eps_sb = sb("eps_sb", [128, 1])

        sW = sem("sW")
        sL = sem("sL")
        sS = sem("sS")
        sPE = sem("sPE")
        sPE2 = sem("sPE2")
        sPE3 = sem("sPE3")
        sACT = sem("sACT")
        sACT2 = sem("sACT2")
        sDVE = sem("sDVE")
        sDVE2 = sem("sDVE2")
        sEV = sem("sEV")
        sGD = sem("sGD")
        sPL = sem("sPL")
        sST = sem("sST")

        def vg(eng, inst):
            n = sGD.inc(inst)
            eng.wait_ge(sGD.h, n)

        with nc.Block() as blk:
            @blk.gpsimd
            def _(g):
                for dst, src in [(id_bf, id_d), (onr_bf, onr_d), (onc_bf, onc_d),
                                 (esel_bf, esel_d), (bqr_bf, bqr_d),
                                 (bkr_bf, bkr_d), (bvs_bf, bvs_d),
                                 (bo_bf, bo_d), (b1_bf, b1_d)]:
                    sW.d(g.dma_start(dst[:, :], src[:, :]))
                g.wait_ge(sW.h, sW.n)

            @blk.sync
            def _(s):
                for dst, src in [(id_f, id_d), (onr_f, onr_d),
                                 (scb_sb, scb_d)]:
                    sL.d(s.dma_start(dst[:, :], src[:, :]))
                sL.d(s.dma_start(wlr[:, 0:8], wl_d[:, :]))
                sL.d(s.dma_start(wlr[:, 8:16], wr_d[:, :]))
                for c in range(HC):
                    sL.d(s.dma_start(gwT_sb[:, c * 8:(c + 1) * 8],
                                       gwT_d[c * 128:(c + 1) * 128, :]))
                s.wait_ge(sL.h, sL.n)

            @blk.vector
            def _(v):
                v.wait_ge(sL.h, sL.n)
                v.wait_ge(sW.h, sW.n)
                v.memset(eps_sb[:, :], EPS)
                vg(v, v.tensor_mul(w_vec[:, :], wlr[:, 0:8], wlr[:, 8:16]))
                for c in range(HC):
                    v.tensor_copy(bndsel[:, 2 * c:2 * c + 1], onc_bf[:, :])
                    v.tensor_mul(bndsel[:, 2 * c + 1:2 * c + 2],
                                 w_vec[:, c:c + 1], w_vec[:, c:c + 1])
                for c in range(HC):
                    vg(v, v.tensor_copy(gwst[:, c * 16:c * 16 + 8],
                                        gwT_sb[:, c * 8:(c + 1) * 8]))
                    vg(v, v.tensor_copy(gwlo_t[:, c * 8:(c + 1) * 8],
                                        gwst[:, c * 16:c * 16 + 8]))
                    v.tensor_tensor(gwst[:, c * 16 + 8:(c + 1) * 16],
                                    gwT_sb[:, c * 8:(c + 1) * 8],
                                    gwlo_t[:, c * 8:(c + 1) * 8], ALU.subtract)

        def phase_ln(layer):
            src = x_in if layer == 0 else x_resid
            with ExitStack() as pc:
                psb = lambda n_, s_, d=F32: pc.enter_context(nc.sbuf_tensor(n_, s_, d))
                pps = lambda n_, s_, d=F32: pc.enter_context(nc.psum_tensor(n_, s_, d))
                xt = psb(f"ln{layer}_xt", [128, 2 * D])
                xc = psb(f"ln{layer}_xc", [128, D])
                sq = psb(f"ln{layer}_sq", [128, D])
                st4 = psb(f"ln{layer}_st", [128, 8])
                xn32 = psb(f"ln{layer}_xn32", [128, D])
                xnbf = psb(f"ln{layer}_xnbf", [128, 2 * D], BF16)
                L0, S0 = sL.n, sS.n
                wa, wb, wc, wp, wq_ = ({} for _ in range(5))
                dn, an = sDVE.n, sACT.n
                for i in range(NT):
                    dn += 1; wa[i] = dn
                    an += 1; wp[i] = an
                    dn += 1; wb[i] = dn
                    an += 1; wq_[i] = an
                    dn += 1; wc[i] = dn

                with nc.Block() as blk:
                    if layer == 0:
                        @blk.gpsimd
                        def _(g):
                            for a_, b_ in ((wsgT_d, wsgT_s), (wsuT_d, wsuT_s),
                                           (wsdT_d, wsdT_s)):
                                sST.d(g.dma_start(b_[:, :], a_[:, :]))
                            for e in range(NEXP):
                                sST.d(g.dma_start(wgT_s[e], wgT_d[e]))
                                sST.d(g.dma_start(wuT_s[e], wuT_d[e]))
                                sST.d(g.dma_start(wdT_s[e], wdT_d[e]))

                    @blk.sync
                    def _(s):
                        for i in range(2):
                            sL.d(s.dma_start(xt[:, (i % 2) * D:(i % 2 + 1) * D],
                                             src[i * 128:(i + 1) * 128, :]))
                        for i in range(NT):
                            s.wait_ge(sDVE.h, wc[i])
                            sS.d(s.dma_start(xn_dram[i * 128:(i + 1) * 128, :],
                                             xnbf[:, (i % 2) * D:(i % 2 + 1) * D]))
                            if i + 2 < NT:
                                s.wait_ge(sDVE.h, wa[i])
                                sL.d(s.dma_start(
                                    xt[:, (i % 2) * D:(i % 2 + 1) * D],
                                    src[(i + 2) * 128:(i + 3) * 128, :]))
                        s.wait_ge(sS.h, sS.n)

                    @blk.vector
                    def _(v):
                        for i in range(NT):
                            xti = xt[:, (i % 2) * D:(i % 2 + 1) * D]
                            v.wait_ge(sL.h, L0 + 16 * (i + 1))
                            if i >= 1:
                                v.wait_ge(sACT.h, wp[i - 1])
                            vg(v, v.reduce_sum(st4[:, 0:1], xti, AX.X))
                            vg(v, v.tensor_scalar(st4[:, 1:2], st4[:, 0:1],
                                                  1.0 / D, None, ALU.mult))
                            sDVE.inc(v.tensor_scalar(xc[:, :], xti, st4[:, 1:2],
                                                     None, ALU.subtract))
                            v.wait_ge(sACT.h, wp[i])
                            sDVE.inc(v.reduce_sum(st4[:, 2:3], sq[:, :], AX.X))
                            v.wait_ge(sACT.h, wq_[i])
                            if i >= 2:
                                v.wait_ge(sS.h, S0 + 16 * (i - 1))
                            v.tensor_scalar(xn32[:, :], xc[:, :], st4[:, 3:4],
                                            None, ALU.mult)
                            sDVE.inc(v.tensor_copy(
                                xnbf[:, (i % 2) * D:(i % 2 + 1) * D], xn32[:, :]))

                    @blk.scalar
                    def _(sc):
                        for i in range(NT):
                            sc.wait_ge(sDVE.h, wa[i])
                            sACT.inc(sc.activation(sq[:, :], xc[:, :], AF.Square))
                            sc.wait_ge(sDVE.h, wb[i])
                            sc.activation(st4[:, 3:4], st4[:, 2:3], AF.Ln,
                                          bias=eps_sb[:, 0:1], scale=1.0 / D)
                            sACT.inc(sc.activation(st4[:, 3:4], st4[:, 3:4],
                                                   AF.Exp, scale=-0.5))

        def phase_xnf():
            with nc.Block() as blk:
                @blk.sync
                def _(s):
                    for c in range(HC):
                        sL.d(s.dma_start_transpose(
                            xn_f[:, c * S:(c + 1) * S],
                            xn_dram[:, c * 128:(c + 1) * 128]))
                    s.wait_ge(sL.h, sL.n)

        def phase_combine():
            with ExitStack() as pc:
                cw = pc.enter_context(nc.sbuf_tensor("cw", [128, 32], F32))
                comb = pc.enter_context(nc.sbuf_tensor("comb", [128, 8], F32))
                lgt = pc.enter_context(nc.sbuf_tensor("lgt", [128, 16], F32))
                ps_cT = pc.enter_context(nc.psum_tensor("ps_cT", [8, 128], F32))
                ps_lg = [pc.enter_context(nc.psum_tensor(f"ps_lg{b}", [128, 16],
                                                         F32)) for b in range(2)]
                wd0, wd1, wd2, wd3, wa1, wp1, wlg, wle = ({} for _ in range(8))
                dn, an, pn = sDVE.n, sACT.n, sPE.n
                pe2, ae2 = sPE2.n, sACT2.n
                for i in range(NT):
                    pe2 += 1; wlg[i] = pe2
                    ae2 += 1; wle[i] = ae2
                    dn += 1; wd0[i] = dn
                    dn += 1; wd1[i] = dn
                    an += 1; wa1[i] = an
                    dn += 1; wd2[i] = dn
                    pn += 1; wp1[i] = pn
                    dn += 1; wd3[i] = dn

                with nc.Block() as blk:
                    @blk.vector
                    def _(v):
                        for i in range(NT):
                            par = i % 2
                            L = logits_sb[:, i * 8:(i + 1) * 8]
                            v.wait_ge(sACT2.h, wle[i])
                            sDVE.inc(v.tensor_add(
                                L, ps_lg[par][:, 0:8],
                                lgt[:, par * 8:(par + 1) * 8]))
                            v.wait_ge(sDVE.h, wd0[i])
                            m1, m2 = cw[:, 0:1], cw[:, 1:2]
                            w1, w2 = cw[:, 2:3], cw[:, 3:4]
                            den, rinv = cw[:, 4:5], cw[:, 5:6]
                            c1, c2 = cw[:, 6:7], cw[:, 7:8]
                            msk1, msk2 = cw[:, 8:16], cw[:, 16:24]
                            Lm = cw[:, 24:32]
                            vg(v, v.reduce_max(m1, L, AX.X))
                            vg(v, v.tensor_scalar(msk1, L, m1, None,
                                                  ALU.is_equal))
                            vg(v, v.tensor_scalar(Lm, msk1, -1e30, None,
                                                  ALU.mult))
                            vg(v, v.tensor_add(Lm, Lm, L))
                            inst = v.reduce_max(m2, Lm, AX.X)
                            sDVE.inc(inst)
                            v.wait_ge(sDVE.h, sDVE.n)
                            v.tensor_scalar(msk2, Lm, m2, None, ALU.is_equal)
                            v.wait_ge(sACT.h, wa1[i])
                            vg(v, v.tensor_add(den, w1, w2))
                            vg(v, v.tensor_scalar(den, den, 1e-20, None, ALU.add))
                            vg(v, v.reciprocal(rinv, den))
                            vg(v, v.tensor_scalar(c1, w1, rinv, 2.5, ALU.mult,
                                                  ALU.mult))
                            vg(v, v.tensor_scalar(c2, w2, rinv, 2.5, ALU.mult,
                                                  ALU.mult))
                            if i >= 1:
                                v.wait_ge(sPE.h, wp1[i - 1])
                            v.tensor_scalar(comb[:, :], msk1, c1, None, ALU.mult)
                            vg(v, v.tensor_scalar(Lm, msk2, c2, None, ALU.mult))
                            sDVE.inc(v.tensor_add(comb[:, :], comb[:, :], Lm))
                            v.wait_ge(sPE.h, wp1[i])
                            sDVE.inc(v.tensor_copy(
                                combT[:, i * 128:(i + 1) * 128], ps_cT[:, :]))

                    @blk.scalar
                    def _(sc):
                        for i in range(NT):
                            par = i % 2
                            if i >= 2:
                                sc.wait_ge(sDVE.h, wd0[i - 2])
                            sc.wait_ge(sPE2.h, wlg[i])
                            sACT2.inc(sc.copy(lgt[:, par * 8:(par + 1) * 8],
                                              ps_lg[par][:, 8:16]))
                            sc.wait_ge(sDVE.h, wd1[i])
                            sc.activation(cw[:, 2:3], cw[:, 0:1], AF.Sigmoid)
                            sACT.inc(sc.activation(cw[:, 3:4], cw[:, 1:2],
                                                   AF.Sigmoid))

                    @blk.tensor
                    def _(t):
                        for i in range(NT):
                            par = i % 2
                            if i >= 2:
                                t.wait_ge(sDVE.h, wd0[i - 2])
                                t.wait_ge(sACT2.h, wle[i - 2])
                            for hc in range(HC):
                                inst = t.matmul(
                                    ps_lg[par][:, :],
                                    xn_f[:, hc * S + i * 128:
                                         hc * S + (i + 1) * 128],
                                    gwst[:, hc * 16:(hc + 1) * 16],
                                    start=(hc == 0), stop=(hc == HC - 1))
                            sPE2.inc(inst)
                            t.wait_ge(sDVE.h, wd2[i])
                            if i >= 1:
                                t.wait_ge(sDVE.h, wd3[i - 1])
                            sPE.inc(t.transpose(ps_cT[:, :], comb[:, :],
                                                id_f[:, :]))

        def phase_l0ffn():
            with ExitStack() as pc:
                psb = lambda n_, s_, d=F32: pc.enter_context(nc.sbuf_tensor(n_, s_, d))
                pps = lambda n_, s_, d=F32: pc.enter_context(nc.psum_tensor(n_, s_, d))
                acc = psb("acc", [128, 4 * D])
                wgt = [psb(f"wgt{b}", [128, 2 * HC * 512], BF16) for b in range(2)]
                wut = [psb(f"wut{b}", [128, 2 * HC * 512], BF16) for b in range(2)]
                wdt = [psb("wdt", [128, 2 * NIC * D], BF16)] * 2
                hmS = psb("hmS", [128, 2 * NIC * 512], BF16)
                silu_sb = psb("silu_b", [128, 512], BF16)
                bc_sb = psb("bc_sb", [128, 512], BF16)
                ps_g = pps("ps_g", [128, 512])
                ps_u = pps("ps_u", [128, 512])
                ps_bc = pps("ps_bc", [128, 512])
                ps_r = [pps(f"ps_r{b}", [128, D]) for b in range(2)]

                W0, S0, L0W = sW.n, sS.n, sL.n
                NW = 2 * (2 * HC + NIC)
                QN8 = 8
                NGI = QN8 * NPAIRS5

                wg_, wu_, wbc, wdn, wsil, wbce, whm, wev, wfs = \
                    {}, {}, {}, {}, {}, {}, {}, {}, {}
                pe, pe2, pe3 = sPE.n, sPE2.n, sPE3.n
                act, act2, dve, ev = sACT.n, sACT2.n, sDVE.n, sEV.n
                for gi in range(NGI):
                    q, p = divmod(gi, NPAIRS5)
                    for tg in range(1):
                        for slot in range(2):
                            if p > 0:
                                pe2 += 1; wbc[gi, tg, slot] = pe2
                                act2 += 1; wbce[gi, tg, slot] = act2
                            for ic in range(NIC):
                                pe += 1; wg_[gi, tg, slot, ic] = pe
                                act += 1; wsil[gi, tg, slot, ic] = act
                                pe2 += 1; wu_[gi, tg, slot, ic] = pe2
                                dve += 1; whm[gi, tg, slot, ic] = dve
                        for t1 in range(4):
                            pe3 += 1; wdn[gi, tg, t1] = pe3
                            ev += 1; wev[gi, tg, t1] = ev
                    if p == NPAIRS5 - 1:
                        for tl in range(4):
                            act2 += 1; wfs[q, tl] = act2
                wpd = {gi: wdn[gi, 0, 3] for gi in range(NGI)}
                stq = {q: S0 + 16 * 4 * (q + 1) for q in range(QN8)}

                def bc_prev_reader(gi, tg, slot):
                    ks = [k for k in whm.keys()
                          if k < (gi, tg, slot, 0) and (k[0] % NPAIRS5) != 0]
                    return whm[ks[-1]] if ks else None

                def wslot_aps(b, p, slot):
                    res = []
                    for hc in range(HC):
                        gd = wgt[b][:, slot * HC * 512 + hc * 512:
                                    slot * HC * 512 + (hc + 1) * 512]
                        ud = wut[b][:, slot * HC * 512 + hc * 512:
                                    slot * HC * 512 + (hc + 1) * 512]
                        if p == 0:
                            gs = wsgT_s[hc * 128:(hc + 1) * 128,
                                        slot * 512:(slot + 1) * 512]
                            us = wsuT_s[hc * 128:(hc + 1) * 128,
                                        slot * 512:(slot + 1) * 512]
                        else:
                            e = (p - 1) * 2 + slot
                            gs = wgT_s[e, hc * 128:(hc + 1) * 128, :]
                            us = wuT_s[e, hc * 128:(hc + 1) * 128, :]
                        res.append((gd, gs, False)); res.append((ud, us, False))
                    for ic in range(NIC):
                        dd = wdt[b][:, slot * NIC * D + ic * D:
                                    slot * NIC * D + (ic + 1) * D]
                        if p == 0:
                            ds = wsdT_s[slot * 512 + ic * 128:
                                        slot * 512 + (ic + 1) * 128, :]
                        else:
                            e = (p - 1) * 2 + slot
                            ds = wdT_s[e, ic * 128:(ic + 1) * 128, :]
                        res.append((dd, ds, True))
                    return res

                with nc.Block() as blk:
                    @blk.gpsimd
                    def _(g):
                        for gi in range(NGI):
                            q, p = divmod(gi, NPAIRS5)
                            if q == 0:
                                g.wait_ge(sST.h, 16 * (3 + 6 * p if p else 3))
                            if gi >= 2:
                                g.wait_ge(sPE3.h, wpd[gi - 2])
                            s1 = wslot_aps(gi % 2, p, 1)
                            for dst, srcap, isd in s1:
                                if not isd:
                                    sW.d(g.dma_start(dst, srcap))
                            if gi >= 1:
                                g.wait_ge(sPE3.h, wpd[gi - 1])
                            for slot in range(2):
                                for dst, srcap, isd in wslot_aps(gi % 2, p, slot):
                                    if isd:
                                        sW.d(g.dma_start(dst, srcap))
                        g.wait_ge(sW.h, sW.n)
                        g.wait_ge(sST.h, sST.n)

                    @blk.sync
                    def _(s):
                        for gi in range(NGI):
                            q, p = divmod(gi, NPAIRS5)
                            if q == 0:
                                s.wait_ge(sST.h, 16 * (3 + 6 * p if p else 3))
                            if gi >= 2:
                                s.wait_ge(sPE3.h, wpd[gi - 2])
                            for dst, srcap, isd in wslot_aps(gi % 2, p, 0):
                                if not isd:
                                    sL.d(s.dma_start(dst, srcap))
                        s.wait_ge(sL.h, sL.n)

                    @blk.tensor
                    def _(t):
                        for gi in range(NGI):
                            q, p = divmod(gi, NPAIRS5)
                            b = gi % 2
                            t.wait_ge(sL.h, L0W + 16 * 16 * (gi + 1))
                            t.wait_ge(sW.h, W0 + 16 * (24 * gi + 16))
                            for tg in range(1):
                                tcol = q * 512
                                for slot in range(2):
                                    if p > 0:
                                        e = (p - 1) * 2 + slot
                                        prev = ordered_prev(wbce, (gi, tg, slot))
                                        if prev is not None:
                                            t.wait_ge(sACT2.h, prev)
                                        sPE2.inc(t.matmul(
                                            ps_bc[:, :],
                                            esel_bf[:, e * 128:(e + 1) * 128],
                                            combT[:, tcol:tcol + 512],
                                            start=True, stop=True))
                                    for ic in range(NIC):
                                        prev = ordered_prev(wsil, (gi, tg, slot, ic))
                                        if prev is not None:
                                            t.wait_ge(sACT.h, prev)
                                        for hc in range(HC):
                                            col = (slot * HC * 512 + hc * 512
                                                   + ic * 128)
                                            inst = t.matmul(
                                                ps_g[:, :], wgt[b][:, col:col + 128],
                                                xn_f[:, hc * S + tcol:
                                                     hc * S + tcol + 512],
                                                start=(hc == 0), stop=(hc == HC - 1))
                                        sPE.inc(inst)
                                        prev = ordered_prev(whm, (gi, tg, slot, ic))
                                        if prev is not None:
                                            t.wait_ge(sDVE.h, prev)
                                        for hc in range(HC):
                                            col = (slot * HC * 512 + hc * 512
                                                   + ic * 128)
                                            inst = t.matmul(
                                                ps_u[:, :], wut[b][:, col:col + 128],
                                                xn_f[:, hc * S + tcol:
                                                     hc * S + tcol + 512],
                                                start=(hc == 0), stop=(hc == HC - 1))
                                        sPE2.inc(inst)
                                for t1 in range(4):
                                    if tg == 0 and t1 == 0:
                                        t.wait_ge(sW.h, W0 + 16 * 24 * (gi + 1))
                                    t.wait_ge(sDVE.h, whm[gi, tg, 1, NIC - 1])
                                    prev = ordered_prev(wev, (gi, tg, t1), back=2)
                                    if prev is not None:
                                        t.wait_ge(sEV.h, prev)
                                    for slot in range(2):
                                        for ic in range(NIC):
                                            for nh in range(2):
                                                inst = t.matmul(
                                                    ps_r[t1 % 2][:, nh * 512:
                                                                 (nh + 1) * 512],
                                                    hmS[:, slot * NIC * 512
                                                        + ic * 512 + t1 * 128:
                                                        slot * NIC * 512
                                                        + ic * 512 + (t1 + 1) * 128],
                                                    wdt[b][:, slot * NIC * D + ic * D
                                                           + nh * 512:
                                                           slot * NIC * D + ic * D
                                                           + (nh + 1) * 512],
                                                    start=(slot == 0 and ic == 0),
                                                    stop=(slot == 1 and
                                                          ic == NIC - 1))
                                    sPE3.inc(inst)

                    @blk.scalar
                    def _(sc):
                        for gi in range(NGI):
                            q, p = divmod(gi, NPAIRS5)
                            for tg in range(1):
                                for slot in range(2):
                                    if p > 0:
                                        sc.wait_ge(sPE2.h, wbc[gi, tg, slot])
                                        prev = bc_prev_reader(gi, tg, slot)
                                        if prev is not None:
                                            sc.wait_ge(sDVE.h, prev)
                                        sACT2.inc(sc.copy(bc_sb[:, :], ps_bc[:, :]))
                                    for ic in range(NIC):
                                        sc.wait_ge(sPE.h, wg_[gi, tg, slot, ic])
                                        prev = ordered_prev(whm, (gi, tg, slot, ic))
                                        if prev is not None:
                                            sc.wait_ge(sDVE.h, prev)
                                        sACT.inc(sc.activation(
                                            silu_sb[:, :], ps_g[:, :], AF.Silu))
                            if p == NPAIRS5 - 1:
                                for tl in range(4):
                                    sc.wait_ge(sEV.h, wev[gi, 0, tl])
                                    a_sl = acc[:, tl * D:(tl + 1) * D]
                                    sACT2.inc(sc.activation(a_sl, a_sl, AF.Silu))
                                    sS.d(sc.dma_start(
                                        g_scr[q * 512 + tl * 128:
                                              q * 512 + (tl + 1) * 128, :],
                                        a_sl))
                        sc.wait_ge(sS.h, sS.n)

                    @blk.vector
                    def _(v):
                        for gi in range(NGI):
                            q, p = divmod(gi, NPAIRS5)
                            for tg in range(1):
                                for slot in range(2):
                                    for ic in range(NIC):
                                        v.wait_ge(sACT.h, wsil[gi, tg, slot, ic])
                                        v.wait_ge(sPE2.h, wu_[gi, tg, slot, ic])
                                        if ic == 0 and slot == 0:
                                            if gi >= 1:
                                                v.wait_ge(sPE3.h, wdn[gi - 1, 0, 3])
                                        dst = hmS[:, slot * NIC * 512 + ic * 512:
                                                  slot * NIC * 512 + (ic + 1) * 512]
                                        if p > 0:
                                            v.wait_ge(sACT2.h, wbce[gi, tg, slot])
                                            v.tensor_tensor(dst, silu_sb[:, :],
                                                            ps_u[:, :], ALU.mult)
                                            sDVE.inc(v.tensor_tensor(
                                                dst, dst, bc_sb[:, :], ALU.mult))
                                        else:
                                            sDVE.inc(v.tensor_tensor(
                                                dst, silu_sb[:, :], ps_u[:, :],
                                                ALU.mult))
                                for t1 in range(4):
                                    v.wait_ge(sPE3.h, wdn[gi, tg, t1])
                                    tl = tg * 4 + t1
                                    a_sl = acc[:, tl * D:(tl + 1) * D]
                                    if p == 0:
                                        if q > 0:
                                            v.wait_ge(sS.h, stq[q - 1])
                                        sEV.inc(v.tensor_copy(a_sl,
                                                              ps_r[t1 % 2][:, :]))
                                    else:
                                        sEV.inc(v.tensor_tensor(
                                            a_sl, a_sl, ps_r[t1 % 2][:, :],
                                            ALU.add))



        def phase_l0attn():
            QG0 = 256
            NQG0 = S // QG0
            with ExitStack() as pc:
                psb = lambda n_, s_, d=F32: pc.enter_context(nc.sbuf_tensor(n_, s_, d))
                pps = lambda n_, s_, d=F32: pc.enter_context(nc.psum_tensor(n_, s_, d))
                xnt = psb("xnt", [128, NKC * D], BF16)
                pT = [psb(f"pT0{b_}", [128, NKC * QG0], BF16) for b_ in range(2)]
                qts = psb("qts", [128, HC * QG0], BF16)
                sq2 = qts
                bnd = psb("bnd", [1, 8])
                bnd2 = psb("bnd2", [2, 16])
                bias0 = psb("bias0", [128, 1])
                rinv_sb = psb("rinv0", [128, 1])
                gbuf = psb("gbuf", [128, 2 * D])
                xbuf = psb("xbuf", [128, 2 * D])
                ps_sT = [pps(f"ps_sT{b_}", [128, 1024]) for b_ in range(2)]
                ps_a = pps("ps_a0", [128, 512])
                ps_b = pps("ps_b0", [128, 513])
                ps_bnd = pps("ps_bnd", [2, 512])

                L0, S0 = sL.n, sS.n
                wsqd, wbmm, wbrd = {}, {}, {}
                act2, pe3, dve2 = sACT2.n, sPE3.n, sDVE2.n
                for ts in range(8):
                    for c in range(HC):
                        act2 += 1; wsqd[ts, c] = act2
                        pe3 += 1; wbmm[ts, c] = pe3
                    dve2 += 1; wbrd[ts] = dve2
                dve2 += 1; w_cmb = dve2
                pe3 += 1; w_e2t = pe3
                dve2 += 1; w_prod = dve2
                act2 += 1; w_bM = act2
                pe3 += 1; w_bc0 = pe3
                dve2 += 1; w_bias0 = dve2

                wqts, wqk, wexp, wpv, wup = {}, {}, {}, {}, {}
                dve, pe, act, pe2, ev = sDVE.n, sPE.n, sACT.n, sPE2.n, sEV.n
                for qg in range(NQG0):
                    dve += 1; wqts[qg] = dve
                    for kq in range(8):
                        pe += 1; wqk[qg, kq] = pe
                        act += 1; wexp[qg, kq] = act
                    for qs in range(2):
                        pe2 += 1; wpv[qg, qs] = pe2
                        ev += 1; wup[qg, qs] = ev

                with nc.Block() as blk:
                    @blk.sync
                    def _(s):
                        for kc in range(NKC):
                            sL.d(s.dma_start(
                                xnt[:, kc * D:(kc + 1) * D],
                                xn_dram[kc * 128:(kc + 1) * 128, :]))
                        for n in range(2):
                            row = n * 128
                            sL.d(s.dma_start(gbuf[:, (n % 2) * D:(n % 2 + 1) * D],
                                             g_scr[row:row + 128, :]))
                            sL.d(s.dma_start(xbuf[:, (n % 2) * D:(n % 2 + 1) * D],
                                             x_in[row:row + 128, :]))
                        for n in range(NT):
                            qg, qs = divmod(n, 2)
                            row = n * 128
                            s.wait_ge(sEV.h, wup[qg, qs])
                            sS.d(s.dma_start(
                                x_resid[row:row + 128, :],
                                xbuf[:, (n % 2) * D:(n % 2 + 1) * D]))
                            if n + 2 < NT:
                                s.wait_ge(sS.h, S0 + 16 * (n + 1))
                                row2 = (n + 2) * 128
                                sL.d(s.dma_start(
                                    gbuf[:, (n % 2) * D:(n % 2 + 1) * D],
                                    g_scr[row2:row2 + 128, :]))
                                sL.d(s.dma_start(
                                    xbuf[:, (n % 2) * D:(n % 2 + 1) * D],
                                    x_in[row2:row2 + 128, :]))
                        s.wait_ge(sS.h, sS.n)

                    @blk.scalar
                    def _(sc):
                        for ts in range(8):
                            for c in range(HC):
                                nsq = ts * HC + c
                                if nsq >= 2:
                                    pts, pcc = divmod(nsq - 2, HC)
                                    sc.wait_ge(sPE3.h, wbmm[pts, pcc])
                                sACT2.inc(sc.activation(
                                    sq2[:, (c % 2) * 512:(c % 2 + 1) * 512],
                                    xn_f[:, c * S + ts * 512:
                                         c * S + (ts + 1) * 512],
                                    AF.Square))
                        sc.wait_ge(sDVE2.h, w_prod)
                        sc.activation(bnd[:, 2:3], bnd[:, 1:2], AF.Ln)
                        sACT2.inc(sc.activation(bnd[:, 2:3], bnd[:, 2:3],
                                                AF.Exp, scale=0.5))
                        for qg in range(NQG0):
                            for kq in range(8):
                                sc.wait_ge(sPE.h, wqk[qg, kq])
                                if qg == 0 and kq == 0:
                                    sc.wait_ge(sDVE2.h, w_bias0)
                                if qg >= 2 and kq == 0:
                                    sc.wait_ge(sPE2.h, wpv[qg - 2, 1])
                                sACT.inc(sc.activation(
                                    pT[qg % 2][:, kq * 1024:(kq + 1) * 1024],
                                    ps_sT[kq % 2][:, :],
                                    AF.Exp, bias=bias0[:, 0:1], scale=1.0))

                    @blk.tensor
                    def _(t):
                        for ts in range(8):
                            if ts >= 1:
                                t.wait_ge(sDVE2.h, wbrd[ts - 1])
                            for c in range(HC):
                                t.wait_ge(sACT2.h, wsqd[ts, c])
                                sPE3.inc(t.matmul(
                                    ps_bnd[:, :], bndsel[:, 2 * c:2 * c + 2],
                                    sq2[:, (c % 2) * 512:(c % 2 + 1) * 512],
                                    start=(c == 0), stop=(c == HC - 1)))
                        t.wait_ge(sDVE2.h, w_cmb)
                        sPE3.inc(t.transpose(ps_bnd[0:1, 0:2], bnd2[:, 8:9],
                                             id_f[0:2, 0:2]))
                        t.wait_ge(sACT2.h, w_bM)
                        sPE3.inc(t.matmul(ps_a[:, 0:1], onr_f[:, :], bnd[:, 2:3],
                                          start=True, stop=True))
                        for qg in range(NQG0):
                            t.wait_ge(sDVE.h, wqts[qg])
                            for kq in range(8):
                                prev = ordered_prev(wexp, (qg, kq), back=2)
                                if prev is not None:
                                    t.wait_ge(sACT.h, prev)
                                for kci in range(4):
                                    kc = kq * 4 + kci
                                    for c in range(HC):
                                        inst = t.matmul(
                                            ps_sT[kq % 2][:, kci * QG0:
                                                          (kci + 1) * QG0],
                                            xn_f[:, c * S + kc * 128:
                                                 c * S + (kc + 1) * 128],
                                            qts[:, c * QG0:(c + 1) * QG0],
                                            start=(c == 0), stop=(c == HC - 1))
                                sPE.inc(inst)
                            t.wait_ge(sACT.h, wexp[qg, 7])
                            for qs in range(2):
                                if qg == 0 and qs == 0:
                                    t.wait_ge(sDVE2.h, w_bias0)
                                prev = ordered_prev(wup, (qg, qs))
                                if prev is not None:
                                    t.wait_ge(sEV.h, prev)
                                for kc in range(NKC):
                                    lhs = pT[qg % 2][:, kc * QG0 + qs * 128:
                                             kc * QG0 + (qs + 1) * 128]
                                    st_ = (kc == 0); sp_ = (kc == NKC - 1)
                                    t.matmul(ps_a[:, :], lhs,
                                             xnt[:, kc * D:kc * D + 512],
                                             start=st_, stop=sp_)
                                    t.matmul(ps_b[:, 0:512], lhs,
                                             xnt[:, kc * D + 512:(kc + 1) * D],
                                             start=st_, stop=sp_)
                                    inst = t.matmul(ps_b[:, 512:513], lhs,
                                                    onc_bf[:, :],
                                                    start=st_, stop=sp_)
                                sPE2.inc(inst)

                    @blk.vector
                    def _(v):
                        for ts in range(8):
                            v.wait_ge(sPE3.h, wbmm[ts, HC - 1])
                            sDVE2.inc(v.reduce_max(bnd2[:, ts:ts + 1],
                                                   ps_bnd[:, :], AX.X))
                        v.wait_ge(sDVE2.h, wbrd[7])
                        sDVE2.inc(v.reduce_max(bnd2[:, 8:9], bnd2[:, 0:8], AX.X))
                        v.wait_ge(sPE3.h, w_e2t)
                        vg(v, v.tensor_copy(bnd[:, 4:6], ps_bnd[0:1, 0:2]))
                        sDVE2.inc(v.tensor_mul(bnd[:, 1:2], bnd[:, 4:5],
                                               bnd[:, 5:6]))
                        v.wait_ge(sPE3.h, w_bc0)
                        sDVE2.inc(v.tensor_scalar(bias0[:, :], ps_a[:, 0:1],
                                                  -1.02, None, ALU.mult))
                        n = 0
                        for qg in range(NQG0):
                            if qg == 0:
                                v.wait_ge(sPE3.h, wbmm[7, HC - 1])
                            if qg >= 1:
                                v.wait_ge(sPE.h, wqk[qg - 1, 7])
                            for c in range(HC):
                                inst = v.tensor_scalar(
                                    qts[:, c * QG0:(c + 1) * QG0],
                                    xn_f[:, c * S + qg * QG0:
                                         c * S + (qg + 1) * QG0],
                                    w_vec[:, c:c + 1], None, ALU.mult)
                            sDVE.inc(inst)
                            for qs in range(2):
                                par = n % 2
                                v.wait_ge(sPE2.h, wpv[qg, qs])
                                v.wait_ge(sL.h, L0 + 16 * NKC + 32 * (n + 1))
                                vg(v, v.reciprocal(rinv_sb[:, :],
                                                   ps_b[:, 512:513]))
                                gb1 = gbuf[:, par * D:par * D + 512]
                                gb2 = gbuf[:, par * D + 512:(par + 1) * D]
                                v.tensor_scalar(gb1, gb1, rinv_sb[:, 0:1],
                                                None, ALU.mult)
                                v.tensor_tensor(gb1, gb1, ps_a[:, :], ALU.mult)
                                v.tensor_add(xbuf[:, par * D:par * D + 512],
                                             gb1,
                                             xbuf[:, par * D:par * D + 512])
                                v.tensor_scalar(gb2, gb2, rinv_sb[:, 0:1],
                                                None, ALU.mult)
                                v.tensor_tensor(gb2, gb2, ps_b[:, 0:512],
                                                ALU.mult)
                                sEV.inc(v.tensor_add(
                                    xbuf[:, par * D + 512:(par + 1) * D],
                                    gb2,
                                    xbuf[:, par * D + 512:(par + 1) * D]))
                                n += 1

        def phase_l1ffn():
            with ExitStack() as pc:
                psb = lambda n_, s_, d=F32: pc.enter_context(nc.sbuf_tensor(n_, s_, d))
                pps = lambda n_, s_, d=F32: pc.enter_context(nc.psum_tensor(n_, s_, d))
                cbn = psb("cbn", [128, 2 * D])
                cbs = psb("cbs", [128, 2 * D], BF16)
                CBT = psb("CBT", [128, HC * D], BF16)
                g1 = psb("g1", [128, 2 * D])
                ps_t = [pps(f"ps_t1{b}", [128, 128], BF16) for b in range(2)]
                ps_f = [pps(f"ps_f{b}", [128, D]) for b in range(2)]

                L0, S0 = sL.n, sS.n
                wsc, wtr, wcp = {}, {}, {}
                dve, pe3, dve2 = sDVE.n, sPE3.n, sDVE2.n
                for nck in range(HC):
                    dve += 1; wsc[nck] = dve
                    for kc in range(HC):
                        pe3 += 1; wtr[nck, kc] = pe3
                        dve2 += 1; wcp[nck, kc] = dve2
                wmm, wsl = {}, {}
                pe, act = sPE.n, sACT.n
                for i in range(NT):
                    pe += 1; wmm[i] = pe
                    act += 1; wsl[i] = act

                with nc.Block() as blk:
                    @blk.sync
                    def _(s):
                        for nck in range(HC):
                            if nck >= 2:
                                s.wait_ge(sDVE.h, wsc[nck - 2])
                            sL.d(s.dma_start(
                                cbn[:, (nck % 2) * D:(nck % 2 + 1) * D],
                                cb_d[nck * 128:(nck + 1) * 128, :]))
                        for i in range(NT):
                            s.wait_ge(sACT.h, wsl[i])
                            sS.d(s.dma_start(
                                g_scr[i * 128:(i + 1) * 128, :],
                                g1[:, (i % 2) * D:(i % 2 + 1) * D]))
                        s.wait_ge(sS.h, sS.n)

                    @blk.vector
                    def _(v):
                        for nck in range(HC):
                            v.wait_ge(sL.h, L0 + 16 * (nck + 1))
                            if nck >= 2:
                                v.wait_ge(sPE3.h, wtr[nck - 2, HC - 1])
                            sDVE.inc(v.tensor_scalar(
                                cbs[:, (nck % 2) * D:(nck % 2 + 1) * D],
                                cbn[:, (nck % 2) * D:(nck % 2 + 1) * D],
                                scb_sb[:, nck:nck + 1], 1.0 / 127.0,
                                ALU.mult, ALU.mult))
                            for kc in range(HC):
                                v.wait_ge(sPE3.h, wtr[nck, kc])
                                dstv = CBT.rearrange("p (k n) -> p k n", k=HC)
                                sDVE2.inc(v.tensor_copy(
                                    dstv[:, kc, nck * 128:(nck + 1) * 128],
                                    ps_t[kc % 2][:, :]))

                    @blk.tensor
                    def _(t):
                        for nck in range(HC):
                            t.wait_ge(sDVE.h, wsc[nck])
                            for kc in range(HC):
                                prev = ordered_prev(wcp, (nck, kc), back=2)
                                if prev is not None:
                                    t.wait_ge(sDVE2.h, prev)
                                sPE3.inc(t.transpose(
                                    ps_t[kc % 2][:, :],
                                    cbs[:, (nck % 2) * D + kc * 128:
                                        (nck % 2) * D + (kc + 1) * 128],
                                    id_bf[:, :]))
                        for i in range(NT):
                            if i >= 2:
                                t.wait_ge(sACT.h, wsl[i - 2])
                            if i == 0:
                                t.wait_ge(sDVE2.h, wcp[HC - 1, HC - 1])
                            for nh in range(2):
                                for kc in range(HC):
                                    t.matmul(ps_f[i % 2][:, nh * 512:(nh + 1) * 512],
                                             xn_f[:, kc * S + i * 128:
                                                  kc * S + (i + 1) * 128],
                                             CBT[:, kc * D + nh * 512:
                                                 kc * D + (nh + 1) * 512],
                                             start=(kc == 0), stop=False)
                                inst = t.matmul(
                                    ps_f[i % 2][:, nh * 512:(nh + 1) * 512],
                                    onr_bf[:, :],
                                    b1_bf[0:1, nh * 512:(nh + 1) * 512],
                                    start=False, stop=True)
                            sPE.inc(inst)

                    @blk.scalar
                    def _(sc):
                        for i in range(NT):
                            sc.wait_ge(sPE.h, wmm[i])
                            if i >= 2:
                                sc.wait_ge(sS.h, S0 + 16 * (i - 1))
                            sACT.inc(sc.activation(
                                g1[:, (i % 2) * D:(i % 2 + 1) * D],
                                ps_f[i % 2][:, :], AF.Silu))

        def phase_l1attn():
            with ExitStack() as pc:
                psb = lambda n_, s_, d=F32: pc.enter_context(nc.sbuf_tensor(n_, s_, d))
                pps = lambda n_, s_, d=F32: pc.enter_context(nc.psum_tensor(n_, s_, d))
                wq_sb = psb("wq_sb", [128, HC * D], BF16)
                wk_sb = psb("wk_sb", [128, HC * D], BF16)
                wv_sb = psb("wv_sb", [128, HC * D], BF16)
                pbuf = [psb(f"pj{w}", [128, 4 * D], BF16) for w in range(3)]
                prodV = psb("prodV", [128, 8192], BF16)
                pairV = psb("pairV", [128, 6144], BF16)
                SCV = psb("SCV", [128, 256])
                ppV = psb("ppV", [128, 256], BF16)
                atV = psb("atV", [128, 256], BF16)
                sumV = psb("sumV", [128, 32])
                Ob = psb("Obf", [128, 2 * D])
                ps_p = [pps(f"ps_p{w}", [128, D]) for w in range(3)]

                W0 = sW.n
                wpm, wev, wexV = {}, {}, {}
                wSCV, wATV, wP2V, wObV = {}, {}, {}, {}
                pe, a2, act, d2 = sPE.n, sACT2.n, sACT.n, sDVE2.n
                for i in range(NT):
                    for w in range(3):
                        pe += 1; wpm[i, w] = pe
                        a2 += 1; wev[i, w] = a2
                for i in range(NT):
                    act += 1; wexV[i] = act
                    d2 += 1; wSCV[i] = d2
                    d2 += 1; wATV[i] = d2
                    d2 += 1; wP2V[i] = d2
                    d2 += 1; wObV[i] = d2
                WB = W0 + 16 * 24
                stF2 = {i: WB + 16 * (i + 1) for i in range(NT)}

                def scram(eng, pairs):
                    ov = xn_dram.rearrange("(r s) c -> r s c", s=16)
                    for p in pairs:
                        dst = ao_hbm[p].rearrange("(q rr) (h d) -> q rr h d",
                                                  q=16, h=2)
                        for q in range(16):
                            sW.d(eng.dma_start(
                                dst[q, :, :, :],
                                ov[:, 2 * p:2 * p + 2, q * 64:(q + 1) * 64]))

                with nc.Block() as blk:
                    @blk.gpsimd
                    def _(g):
                        for wsrc, wbuf in ((wqT_d, wq_sb), (wkT_d, wk_sb),
                                           (wvTs_d, wv_sb)):
                            for hc in range(HC):
                                sW.d(g.dma_start(
                                    wbuf[:, hc * D:(hc + 1) * D],
                                    wsrc[hc * 128:(hc + 1) * 128, :]))
                        for i in range(NT):
                            g.wait_ge(sDVE2.h, wObV[i])
                            sW.d(g.dma_start(
                                xn_dram[i * 128:(i + 1) * 128, :],
                                Ob[:, (i % 2) * D:(i % 2 + 1) * D]))
                        g.wait_ge(sW.h, stF2[NT - 1])
                        scram(g, (0, 1, 2))
                        g.wait_ge(sW.h, sW.n)

                    @blk.tensor
                    def _(t):
                        t.wait_ge(sW.h, W0 + 16 * 24)
                        for i in range(NT):
                            for w in range(3):
                                wsb = (wq_sb, wk_sb, wv_sb)[w]
                                brow = (bqr_bf, bkr_bf, bvs_bf)[w]
                                prev = ordered_prev(wev, (i, w), back=3)
                                if prev is not None:
                                    t.wait_ge(sACT2.h, prev)
                                for hc in range(HC):
                                    for nh in range(2):
                                        t.matmul(
                                            ps_p[w][:, nh * 512:(nh + 1) * 512],
                                            xn_f[:, hc * S + i * 128:
                                                 hc * S + (i + 1) * 128],
                                            wsb[:, hc * D + nh * 512:
                                                hc * D + (nh + 1) * 512],
                                            start=(hc == 0), stop=False)
                                for nh in range(2):
                                    inst = t.matmul(
                                        ps_p[w][:, nh * 512:(nh + 1) * 512],
                                        onr_bf[:, :],
                                        brow[0:1, nh * 512:(nh + 1) * 512],
                                        start=False, stop=True)
                                sPE.inc(inst)

                    @blk.sync
                    def _(s):
                        s.wait_ge(sW.h, stF2[NT - 1])
                        scram(s, (6, 7))

                    @blk.vector
                    def _(v):
                        for i in range(NT):
                            par = i % 2
                            s4 = (i % 4) * D
                            v.wait_ge(sACT2.h, wev[i, 1])
                            if i >= 1:
                                v.wait_ge(sACT.h, wexV[i - 1])
                            for c in range(2):
                                qv = bass.AP(pbuf[0], s4 + c * 512,
                                             [[4 * D, 128], [64, 8], [0, 16],
                                              [1, 64]])
                                kv = bass.AP(pbuf[1], s4,
                                             [[4 * D, 128], [0, 8], [64, 16],
                                              [1, 64]])
                                v.tensor_tensor(prodV[:, :], qv, kv, ALU.mult)
                                v.tensor_tensor(
                                    bass.AP(pairV, 0,
                                            [[6144, 128], [32, 128], [1, 32]]),
                                    bass.AP(prodV, 0,
                                            [[8192, 128], [64, 128], [1, 32]]),
                                    bass.AP(prodV, 32,
                                            [[8192, 128], [64, 128], [1, 32]]),
                                    ALU.add)
                                v.tensor_tensor(
                                    bass.AP(pairV, 4096,
                                            [[6144, 128], [16, 128], [1, 16]]),
                                    bass.AP(pairV, 0,
                                            [[6144, 128], [32, 128], [1, 16]]),
                                    bass.AP(pairV, 16,
                                            [[6144, 128], [32, 128], [1, 16]]),
                                    ALU.add)
                                inst = v.reduce_sum(
                                    SCV[:, c * 128:(c + 1) * 128],
                                    bass.AP(pairV, 4096,
                                            [[6144, 128], [16, 128], [1, 16]]),
                                    AX.X)
                            sDVE2.inc(inst)
                            v.wait_ge(sACT.h, wexV[i])
                            vg(v, v.reduce_sum(
                                sumV[:, 0:16],
                                ppV.rearrange("p (a b) -> p a b", b=16), AX.X))
                            vg(v, v.reciprocal(sumV[:, 16:32], sumV[:, 0:16]))
                            rb = bass.AP(sumV, 16,
                                         [[32, 128], [1, 16], [0, 16]])
                            sDVE2.inc(v.tensor_tensor(atV[:, :], ppV[:, :], rb,
                                                      ALU.mult))
                            v.wait_ge(sACT2.h, wev[i, 2])
                            if i >= 2:
                                v.wait_ge(sW.h, stF2[i - 2])
                            for c in range(2):
                                av = bass.AP(atV, c * 128,
                                             [[256, 128], [16, 8], [0, 64],
                                              [1, 16]])
                                vv = bass.AP(pbuf[2], s4,
                                             [[4 * D, 128], [0, 8], [16, 64],
                                              [1, 16]])
                                inst = v.tensor_tensor(prodV[:, :], av, vv,
                                                       ALU.mult)
                                if c == 1:
                                    sDVE2.inc(inst)
                                v.tensor_tensor(
                                    bass.AP(pairV, 0,
                                            [[6144, 128], [8, 512], [1, 8]]),
                                    bass.AP(prodV, 0,
                                            [[8192, 128], [16, 512], [1, 8]]),
                                    bass.AP(prodV, 8,
                                            [[8192, 128], [16, 512], [1, 8]]),
                                    ALU.add)
                                v.tensor_tensor(
                                    bass.AP(pairV, 4096,
                                            [[6144, 128], [4, 512], [1, 4]]),
                                    bass.AP(pairV, 0,
                                            [[6144, 128], [8, 512], [1, 4]]),
                                    bass.AP(pairV, 4,
                                            [[6144, 128], [8, 512], [1, 4]]),
                                    ALU.add)
                                inst = v.reduce_sum(
                                    Ob[:, par * D + c * 512:
                                       par * D + (c + 1) * 512],
                                    bass.AP(pairV, 4096,
                                            [[6144, 128], [4, 512], [1, 4]]),
                                    AX.X)
                            sDVE2.inc(inst)

                    @blk.scalar
                    def _(sc):
                        for ii in range(NT + 2):
                            j = ii - 2
                            if j >= 0:
                                sc.wait_ge(sDVE2.h, wSCV[j])
                                if j >= 1:
                                    sc.wait_ge(sDVE2.h, wATV[j - 1])
                                sACT.inc(sc.activation(ppV[:, :], SCV[:, :],
                                                       AF.Exp, scale=0.125))
                            if ii < NT:
                                if ii >= 4:
                                    sc.wait_ge(sDVE2.h, wP2V[ii - 4])
                                for w in range(3):
                                    sc.wait_ge(sPE.h, wpm[ii, w])
                                    sACT2.inc(sc.copy(
                                        pbuf[w][:, (ii % 4) * D:
                                                (ii % 4 + 1) * D],
                                        ps_p[w][:, :]))
                        sc.wait_ge(sW.h, stF2[NT - 1])
                        scram(sc, (3, 4, 5))

        def phase_l1out():
            with ExitStack() as pc:
                psb = lambda n_, s_, d=F32: pc.enter_context(nc.sbuf_tensor(n_, s_, d))
                pps = lambda n_, s_, d=F32: pc.enter_context(nc.psum_tensor(n_, s_, d))
                woT_sb = psb("woT_sb", [128, HC * D], BF16)
                aoT = psb("aoT", [128, HC * S], BF16)
                g1b = psb("g1b", [128, 2 * D])
                x1b = psb("x1b", [128, 2 * D])
                ob = psb("ob", [128, 2 * D])
                tmpG = psb("tmpG", [128, 512])
                ps_s = [pps(f"ps_s{b}", [128, D]) for b in range(2)]

                W0, L0, S0 = sW.n, sL.n, sS.n
                wmmG, wupG = {}, {}
                pe, ev = sPE.n, sEV.n
                for i in range(NT):
                    pe += 1; wmmG[i] = pe
                    ev += 1; wupG[i] = ev

                with nc.Block() as blk:
                    @blk.gpsimd
                    def _(g):
                        for pch in range(HC):
                            sW.d(g.dma_start(
                                woT_sb[:, pch * D:(pch + 1) * D],
                                woT_d[pch * 128:(pch + 1) * 128, :]))
                        g.wait_ge(sW.h, sW.n)

                    @blk.sync
                    def _(s):
                        for p in range(NPAIR):
                            sL.d(s.dma_start_transpose(
                                aoT[:, p * S:(p + 1) * S], ao_hbm[p][:, :]))
                        for i in range(2):
                            sL.d(s.dma_start(
                                g1b[:, (i % 2) * D:(i % 2 + 1) * D],
                                g_scr[i * 128:(i + 1) * 128, :]))
                            sL.d(s.dma_start(
                                x1b[:, (i % 2) * D:(i % 2 + 1) * D],
                                x_resid[i * 128:(i + 1) * 128, :]))
                        for i in range(NT):
                            s.wait_ge(sEV.h, wupG[i])
                            sS.d(s.dma_start(
                                out_d[i * 128:(i + 1) * 128, :],
                                ob[:, (i % 2) * D:(i % 2 + 1) * D]))
                            if i + 2 < NT:
                                s.wait_ge(sEV.h, wupG[i])
                                sL.d(s.dma_start(
                                    g1b[:, (i % 2) * D:(i % 2 + 1) * D],
                                    g_scr[(i + 2) * 128:(i + 3) * 128, :]))
                                sL.d(s.dma_start(
                                    x1b[:, (i % 2) * D:(i % 2 + 1) * D],
                                    x_resid[(i + 2) * 128:(i + 3) * 128, :]))
                        s.wait_ge(sS.h, sS.n)

                    @blk.tensor
                    def _(t):
                        t.wait_ge(sW.h, sW.n)
                        t.wait_ge(sL.h, L0 + 16 * NPAIR)
                        for i in range(NT):
                            if i >= 2:
                                t.wait_ge(sEV.h, wupG[i - 2])
                            for nh in range(2):
                                for pch in range(HC):
                                    t.matmul(ps_s[i % 2][:, nh * 512:(nh + 1) * 512],
                                             aoT[:, pch * S + i * 128:
                                                  pch * S + (i + 1) * 128],
                                             woT_sb[:, pch * D + nh * 512:
                                                    pch * D + (nh + 1) * 512],
                                             start=(pch == 0), stop=False)
                                inst = t.matmul(
                                    ps_s[i % 2][:, nh * 512:(nh + 1) * 512],
                                    onr_bf[:, :],
                                    bo_bf[0:1, nh * 512:(nh + 1) * 512],
                                    start=False, stop=True)
                            sPE.inc(inst)

                    @blk.vector
                    def _(v):
                        for i in range(NT):
                            par = i % 2
                            v.wait_ge(sPE.h, wmmG[i])
                            v.wait_ge(sL.h, L0 + 16 * NPAIR + 32 * (i + 1))
                            if i >= 2:
                                v.wait_ge(sS.h, S0 + 16 * (i - 1))
                            for nh in range(2):
                                v.tensor_tensor(
                                    tmpG[:, :],
                                    ps_s[par][:, nh * 512:(nh + 1) * 512],
                                    g1b[:, par * D + nh * 512:
                                        par * D + (nh + 1) * 512], ALU.mult)
                                inst = v.tensor_add(
                                    ob[:, par * D + nh * 512:
                                       par * D + (nh + 1) * 512],
                                    tmpG[:, :],
                                    x1b[:, par * D + nh * 512:
                                        par * D + (nh + 1) * 512])
                            sEV.inc(inst)

        phase_ln(0)
        if n_phases >= 2:
            phase_xnf()
            phase_combine()
        if n_phases >= 3:
            phase_l0ffn()
        if n_phases >= 4:
            phase_l0attn()
        if n_phases >= 5:
            phase_ln(1)
            phase_xnf()
        if n_phases >= 6:
            phase_l1ffn()
        if n_phases >= 7:
            phase_l1attn()
        if n_phases >= 8:
            phase_l1out()

        if debug:
            with nc.Block() as blk:
                dbgA = dt("dbgA", [128, NT * 8], F32, "ExternalOutput")
                dbgB = dt("dbgB", [8, NT * 128], F32, "ExternalOutput")
                dbgC = dt("dbgC", [128, HC * S], F32, "ExternalOutput")
                dbg_xnd = dt("dbg_xnd", [S, D], F32, "ExternalOutput")
                dbg_xres = dt("dbg_xres", [S, D], F32, "ExternalOutput")
                dbg_g = dt("dbg_g", [S, D], F32, "ExternalOutput")
                dbg_ao = dt("dbg_ao", [NPAIR, S, 128], F32, "ExternalOutput")

                @blk.gpsimd
                def _(g):
                    sW.d(g.dma_start(dbgA[:, :], logits_sb[:, :]))
                    sW.d(g.dma_start(dbg_xnd[:, :], xn_dram[:, :]))
                    if n_phases >= 2:
                        sW.d(g.dma_start(dbgB[:, :], combT[:, :]))
                        sW.d(g.dma_start(dbgC[:, :], xn_f[:, :]))
                    if n_phases >= 3:
                        sW.d(g.dma_start(dbg_g[:, :], g_scr[:, :]))
                    if n_phases >= 4:
                        sW.d(g.dma_start(dbg_xres[:, :], x_resid[:, :]))
                    if n_phases >= 7:
                        for p in range(NPAIR):
                            sW.d(g.dma_start(dbg_ao[p, :, :], ao_hbm[p, :, :]))
                    g.wait_ge(sW.h, sW.n)

    return nc


def _layout_inputs(inputs):
    f = np.ascontiguousarray
    esel = np.zeros((8, 1024), np.float32)
    for e in range(8):
        esel[e, e * 128:(e + 1) * 128] = 1.0
    common = {
        "wl": f(np.asarray(inputs["L0_attn_w_lhs"], np.float32).reshape(8, 128).T),
        "wr": f(np.asarray(inputs["L0_attn_w_rhs"], np.float32).reshape(8, 128).T),
        "gwT": f(np.asarray(inputs["L0_ffn_gate_weight"], np.float32).T),
        "wgT": f(np.transpose(np.asarray(inputs["L0_ffn_w_gate"], np.float32),
                              (0, 2, 1))),
        "wuT": f(np.transpose(np.asarray(inputs["L0_ffn_w_up"], np.float32),
                              (0, 2, 1))),
        "wdT": f(np.transpose(np.asarray(inputs["L0_ffn_w_down"], np.float32),
                              (0, 2, 1))),
        "wsgT": f(np.asarray(inputs["L0_ffn_ws_gate"], np.float32).T),
        "wsuT": f(np.asarray(inputs["L0_ffn_ws_up"], np.float32).T),
        "wsdT": f(np.asarray(inputs["L0_ffn_ws_down"], np.float32).T),
        "wqT": f(np.asarray(inputs["L1_attn_Wq"], np.float32).T),
        "wkT": f(np.asarray(inputs["L1_attn_Wk"], np.float32).T),
        "wvTs": f(np.asarray(inputs["L1_attn_Wv"], np.float32).T
                  .reshape(1024, 16, 64).transpose(0, 2, 1).reshape(1024, 1024)),
        "woT": f(np.asarray(inputs["L1_attn_Wo"], np.float32).T),
        "bqrow": f(np.asarray(inputs["L1_attn_bq"], np.float32).reshape(1, 1024)),
        "bkrow": f(np.asarray(inputs["L1_attn_bk"], np.float32).reshape(1, 1024)),
        "bvsrow": f(np.asarray(inputs["L1_attn_bv"], np.float32)
                    .reshape(16, 64).T.reshape(1, 1024)),
        "borow": f(np.asarray(inputs["L1_attn_bo"], np.float32).reshape(1, 1024)),
        "cb": f(np.asarray(inputs["L1_ffn_CB"], np.float32)),
        "scb": f(np.asarray(inputs["L1_ffn_SCB"], np.float32).reshape(8, 128).T),
        "b1row": f(np.asarray(inputs["L1_ffn_bias"], np.float32).reshape(1, 1024)),
        "ident": np.eye(128, dtype=np.float32),
        "onesrow": np.ones((1, 128), np.float32),
        "onescol": np.ones((128, 1), np.float32),
        "esel": esel,
    }
    x = np.asarray(inputs["L0_attn_x"], np.float32)
    return [dict(common, x=f(x[c])) for c in range(8)]


def kernel(**inputs):
    in_maps = _layout_inputs(inputs)
    nc = build_nc()
    res = run_bass_kernel_spmd(nc, in_maps, core_ids=list(range(8)))
    return np.stack([r["out"] for r in res.results], axis=0)


# version 14
# speedup vs baseline: 1.2282x; 1.0153x over previous
import numpy as np
from contextlib import ExitStack

import concourse.bass as bass
import concourse.mybir as mybir
from concourse import library_config
from concourse.bass_utils import run_bass_kernel_spmd

F32 = mybir.dt.float32
BF16 = mybir.dt.bfloat16
AF = mybir.ActivationFunctionType
ALU = mybir.AluOpType
AX = mybir.AxisListType

S = 4096
D = 1024
HC = 8
NT = 32
NKC = 32
QG = 512
NQG = S // QG
NEXP = 8
NIC = 4
NPAIR = 8
EPS = 1e-5
NPAIRS5 = 5
QN = 4


def ordered_prev(d, key, back=1):
    ks = list(d.keys())
    i = ks.index(key)
    return d[ks[i - back]] if i - back >= 0 else None


class Sem:
    def __init__(self, nc, ctx, name):
        self.h = ctx.enter_context(nc.semaphore(name))
        self.n = 0

    def inc(self, inst, k=1):
        inst.then_inc(self.h, k)
        self.n += k
        return self.n

    def d(self, inst):
        return self.inc(inst, 16)


def build_nc(n_phases=99, debug=False):
    nc = bass.Bass("TRN2", target_bir_lowering=False, dynamic_dma_scratch_size=16384, detect_race_conditions=False)

    dt = lambda name, shape, dtype=F32, kind="ExternalInput": nc.dram_tensor(
        name, shape, dtype, kind=kind)
    x_in = dt("x", [S, D])
    wl_d = dt("wl", [128, 8]);  wr_d = dt("wr", [128, 8])
    gwT_d = dt("gwT", [D, 8])
    wgT_d = dt("wgT", [NEXP, D, 512]); wuT_d = dt("wuT", [NEXP, D, 512])
    wdT_d = dt("wdT", [NEXP, 512, D])
    wsgT_d = dt("wsgT", [D, D]); wsuT_d = dt("wsuT", [D, D]); wsdT_d = dt("wsdT", [D, D])
    wqT_d = dt("wqT", [D, D]); wkT_d = dt("wkT", [D, D])
    wvTs_d = dt("wvTs", [D, D]); woT_d = dt("woT", [D, D])
    bqr_d = dt("bqrow", [1, D]); bkr_d = dt("bkrow", [1, D])
    bvs_d = dt("bvsrow", [1, D]); bo_d = dt("borow", [1, D])
    cb_d = dt("cb", [D, D]); scb_d = dt("scb", [128, 8]); b1_d = dt("b1row", [1, D])
    id_d = dt("ident", [128, 128])
    onr_d = dt("onesrow", [1, 128]); onc_d = dt("onescol", [128, 1])
    esel_d = dt("esel", [8, 1024])
    out_d = dt("out", [S, D], F32, "ExternalOutput")

    xn_dram = dt("xn_dram", [S, D], BF16, "Internal")
    qkv_dram = dt("qkv_dram", [3, S, D], BF16, "Internal")
    wgT_s = dt("wgT_s", [NEXP, D, 512], BF16, "Internal")
    wuT_s = dt("wuT_s", [NEXP, D, 512], BF16, "Internal")
    wdT_s = dt("wdT_s", [NEXP, 512, D], BF16, "Internal")
    wsgT_s = dt("wsgT_s", [D, D], BF16, "Internal")
    wsuT_s = dt("wsuT_s", [D, D], BF16, "Internal")
    wsdT_s = dt("wsdT_s", [D, D], BF16, "Internal")
    x_resid = dt("x_resid", [S, D], F32, "Internal")
    g_scr = dt("g_scr", [S, D], F32, "Internal")
    ao_hbm = dt("ao_hbm", [NPAIR, S, 128], BF16, "Internal")

    with ExitStack() as ctx:
        sb = lambda name, shape, d=F32: ctx.enter_context(nc.sbuf_tensor(name, shape, d))
        sem = lambda name: Sem(nc, ctx, name)

        xn_f = sb("xn_f", [128, HC * S], BF16)
        logits_sb = sb("logits", [128, NT * 8])
        combT = sb("combT", [8, NT * 128], BF16)
        id_bf = sb("id_bf", [128, 128], BF16)
        id_f = sb("id_f", [128, 128])
        onr_bf = sb("onr_bf", [1, 128], BF16)
        onr_f = sb("onr_f", [1, 128])
        onc_bf = sb("onc_bf", [128, 1], BF16)
        esel_bf = sb("esel_bf", [8, 1024], BF16)
        w_vec = sb("w_vec", [128, 8])
        bndsel = sb("bndsel", [128, 16], BF16)
        gwT_sb = sb("gwT_sb", [128, HC * 8])
        gwst = sb("gwst", [128, HC * 16], BF16)
        gwlo_t = sb("gwlo_t", [128, HC * 8])
        scb_sb = sb("scb_sb", [128, 8])
        bqr_bf = sb("bqr_bf", [1, D], BF16); bkr_bf = sb("bkr_bf", [1, D], BF16)
        bvs_bf = sb("bvs_bf", [1, D], BF16); bo_bf = sb("bo_bf", [1, D], BF16)
        b1_bf = sb("b1_bf", [1, D], BF16)
        wlr = sb("wlr", [128, 16])
        eps_sb = sb("eps_sb", [128, 1])

        sW = sem("sW")
        sL = sem("sL")
        sS = sem("sS")
        sPE = sem("sPE")
        sPE2 = sem("sPE2")
        sPE3 = sem("sPE3")
        sACT = sem("sACT")
        sACT2 = sem("sACT2")
        sDVE = sem("sDVE")
        sDVE2 = sem("sDVE2")
        sEV = sem("sEV")
        sGD = sem("sGD")
        sPL = sem("sPL")
        sST = sem("sST")

        def vg(eng, inst):
            n = sGD.inc(inst)
            eng.wait_ge(sGD.h, n)

        with nc.Block() as blk:
            @blk.gpsimd
            def _(g):
                for dst, src in [(id_bf, id_d), (onr_bf, onr_d), (onc_bf, onc_d),
                                 (esel_bf, esel_d), (bqr_bf, bqr_d),
                                 (bkr_bf, bkr_d), (bvs_bf, bvs_d),
                                 (bo_bf, bo_d), (b1_bf, b1_d)]:
                    sW.d(g.dma_start(dst[:, :], src[:, :]))
                g.wait_ge(sW.h, sW.n)

            @blk.sync
            def _(s):
                for dst, src in [(id_f, id_d), (onr_f, onr_d),
                                 (scb_sb, scb_d)]:
                    sL.d(s.dma_start(dst[:, :], src[:, :]))
                sL.d(s.dma_start(wlr[:, 0:8], wl_d[:, :]))
                sL.d(s.dma_start(wlr[:, 8:16], wr_d[:, :]))
                for c in range(HC):
                    sL.d(s.dma_start(gwT_sb[:, c * 8:(c + 1) * 8],
                                       gwT_d[c * 128:(c + 1) * 128, :]))
                s.wait_ge(sL.h, sL.n)

            @blk.vector
            def _(v):
                v.wait_ge(sL.h, sL.n)
                v.wait_ge(sW.h, sW.n)
                v.memset(eps_sb[:, :], EPS)
                vg(v, v.tensor_mul(w_vec[:, :], wlr[:, 0:8], wlr[:, 8:16]))
                for c in range(HC):
                    v.tensor_copy(bndsel[:, 2 * c:2 * c + 1], onc_bf[:, :])
                    v.tensor_mul(bndsel[:, 2 * c + 1:2 * c + 2],
                                 w_vec[:, c:c + 1], w_vec[:, c:c + 1])
                for c in range(HC):
                    vg(v, v.tensor_copy(gwst[:, c * 16:c * 16 + 8],
                                        gwT_sb[:, c * 8:(c + 1) * 8]))
                    vg(v, v.tensor_copy(gwlo_t[:, c * 8:(c + 1) * 8],
                                        gwst[:, c * 16:c * 16 + 8]))
                    v.tensor_tensor(gwst[:, c * 16 + 8:(c + 1) * 16],
                                    gwT_sb[:, c * 8:(c + 1) * 8],
                                    gwlo_t[:, c * 8:(c + 1) * 8], ALU.subtract)

        def phase_ln(layer):
            src = x_in if layer == 0 else x_resid
            with ExitStack() as pc:
                psb = lambda n_, s_, d=F32: pc.enter_context(nc.sbuf_tensor(n_, s_, d))
                pps = lambda n_, s_, d=F32: pc.enter_context(nc.psum_tensor(n_, s_, d))
                xt = psb(f"ln{layer}_xt", [128, 2 * D])
                xc = psb(f"ln{layer}_xc", [128, D])
                sq = psb(f"ln{layer}_sq", [128, D])
                st4 = psb(f"ln{layer}_st", [128, 8])
                xn32 = psb(f"ln{layer}_xn32", [128, D])
                xnbf = psb(f"ln{layer}_xnbf", [128, 2 * D], BF16)
                L0, S0 = sL.n, sS.n
                wa, wb, wc, wp, wq_ = ({} for _ in range(5))
                dn, an = sDVE.n, sACT.n
                for i in range(NT):
                    dn += 1; wa[i] = dn
                    an += 1; wp[i] = an
                    dn += 1; wb[i] = dn
                    an += 1; wq_[i] = an
                    dn += 1; wc[i] = dn

                with nc.Block() as blk:
                    @blk.sync
                    def _(s):
                        for i in range(2):
                            sL.d(s.dma_start(xt[:, (i % 2) * D:(i % 2 + 1) * D],
                                             src[i * 128:(i + 1) * 128, :]))
                        for i in range(NT):
                            s.wait_ge(sDVE.h, wc[i])
                            sS.d(s.dma_start(xn_dram[i * 128:(i + 1) * 128, :],
                                             xnbf[:, (i % 2) * D:(i % 2 + 1) * D]))
                            if i + 2 < NT:
                                s.wait_ge(sDVE.h, wa[i])
                                sL.d(s.dma_start(
                                    xt[:, (i % 2) * D:(i % 2 + 1) * D],
                                    src[(i + 2) * 128:(i + 3) * 128, :]))
                        s.wait_ge(sS.h, sS.n)

                    @blk.vector
                    def _(v):
                        for i in range(NT):
                            xti = xt[:, (i % 2) * D:(i % 2 + 1) * D]
                            v.wait_ge(sL.h, L0 + 16 * (i + 1))
                            if i >= 1:
                                v.wait_ge(sACT.h, wp[i - 1])
                            vg(v, v.reduce_sum(st4[:, 0:1], xti, AX.X))
                            vg(v, v.tensor_scalar(st4[:, 1:2], st4[:, 0:1],
                                                  1.0 / D, None, ALU.mult))
                            sDVE.inc(v.tensor_scalar(xc[:, :], xti, st4[:, 1:2],
                                                     None, ALU.subtract))
                            v.wait_ge(sACT.h, wp[i])
                            sDVE.inc(v.reduce_sum(st4[:, 2:3], sq[:, :], AX.X))
                            v.wait_ge(sACT.h, wq_[i])
                            if i >= 2:
                                v.wait_ge(sS.h, S0 + 16 * (i - 1))
                            v.tensor_scalar(xn32[:, :], xc[:, :], st4[:, 3:4],
                                            None, ALU.mult)
                            sDVE.inc(v.tensor_copy(
                                xnbf[:, (i % 2) * D:(i % 2 + 1) * D], xn32[:, :]))

                    if layer == 0:
                        @blk.gpsimd
                        def _(g):
                            for a_, b_ in ((wsgT_d, wsgT_s), (wsuT_d, wsuT_s),
                                           (wsdT_d, wsdT_s)):
                                sST.d(g.dma_start(b_[:, :], a_[:, :]))
                            for e in range(2):
                                sST.d(g.dma_start(wgT_s[e], wgT_d[e]))
                                sST.d(g.dma_start(wuT_s[e], wuT_d[e]))
                                sST.d(g.dma_start(wdT_s[e], wdT_d[e]))

                    @blk.scalar
                    def _(sc):
                        for i in range(NT):
                            sc.wait_ge(sDVE.h, wa[i])
                            sACT.inc(sc.activation(sq[:, :], xc[:, :], AF.Square))
                            sc.wait_ge(sDVE.h, wb[i])
                            sc.activation(st4[:, 3:4], st4[:, 2:3], AF.Ln,
                                          bias=eps_sb[:, 0:1], scale=1.0 / D)
                            sACT.inc(sc.activation(st4[:, 3:4], st4[:, 3:4],
                                                   AF.Exp, scale=-0.5))

        def phase_xnf():
            with nc.Block() as blk:
                @blk.sync
                def _(s):
                    for c in range(HC):
                        sL.d(s.dma_start_transpose(
                            xn_f[:, c * S:(c + 1) * S],
                            xn_dram[:, c * 128:(c + 1) * 128]))
                    s.wait_ge(sL.h, sL.n)

        def phase_combine():
            with ExitStack() as pc:
                cw = pc.enter_context(nc.sbuf_tensor("cw", [128, 32], F32))
                comb = pc.enter_context(nc.sbuf_tensor("comb", [128, 8], F32))
                lgt = pc.enter_context(nc.sbuf_tensor("lgt", [128, 16], F32))
                ps_cT = pc.enter_context(nc.psum_tensor("ps_cT", [8, 128], F32))
                ps_lg = [pc.enter_context(nc.psum_tensor(f"ps_lg{b}", [128, 16],
                                                         F32)) for b in range(2)]
                wd0, wd1, wd2, wd3, wa1, wp1, wlg, wle = ({} for _ in range(8))
                dn, an, pn = sDVE.n, sACT.n, sPE.n
                pe2, ae2 = sPE2.n, sACT2.n
                for i in range(NT):
                    pe2 += 1; wlg[i] = pe2
                    ae2 += 1; wle[i] = ae2
                    dn += 1; wd0[i] = dn
                    dn += 1; wd1[i] = dn
                    an += 1; wa1[i] = an
                    dn += 1; wd2[i] = dn
                    pn += 1; wp1[i] = pn
                    dn += 1; wd3[i] = dn

                with nc.Block() as blk:
                    @blk.gpsimd
                    def _(g):
                        for e in range(2, NEXP):
                            sST.d(g.dma_start(wgT_s[e], wgT_d[e]))
                            sST.d(g.dma_start(wuT_s[e], wuT_d[e]))
                            sST.d(g.dma_start(wdT_s[e], wdT_d[e]))

                    @blk.vector
                    def _(v):
                        for i in range(NT):
                            par = i % 2
                            L = logits_sb[:, i * 8:(i + 1) * 8]
                            v.wait_ge(sACT2.h, wle[i])
                            sDVE.inc(v.tensor_add(
                                L, ps_lg[par][:, 0:8],
                                lgt[:, par * 8:(par + 1) * 8]))
                            v.wait_ge(sDVE.h, wd0[i])
                            m1, m2 = cw[:, 0:1], cw[:, 1:2]
                            w1, w2 = cw[:, 2:3], cw[:, 3:4]
                            den, rinv = cw[:, 4:5], cw[:, 5:6]
                            c1, c2 = cw[:, 6:7], cw[:, 7:8]
                            msk1, msk2 = cw[:, 8:16], cw[:, 16:24]
                            Lm = cw[:, 24:32]
                            vg(v, v.reduce_max(m1, L, AX.X))
                            vg(v, v.tensor_scalar(msk1, L, m1, None,
                                                  ALU.is_equal))
                            vg(v, v.tensor_scalar(Lm, msk1, -1e30, None,
                                                  ALU.mult))
                            vg(v, v.tensor_add(Lm, Lm, L))
                            inst = v.reduce_max(m2, Lm, AX.X)
                            sDVE.inc(inst)
                            v.wait_ge(sDVE.h, sDVE.n)
                            v.tensor_scalar(msk2, Lm, m2, None, ALU.is_equal)
                            v.wait_ge(sACT.h, wa1[i])
                            vg(v, v.tensor_add(den, w1, w2))
                            vg(v, v.tensor_scalar(den, den, 1e-20, None, ALU.add))
                            vg(v, v.reciprocal(rinv, den))
                            vg(v, v.tensor_scalar(c1, w1, rinv, 2.5, ALU.mult,
                                                  ALU.mult))
                            vg(v, v.tensor_scalar(c2, w2, rinv, 2.5, ALU.mult,
                                                  ALU.mult))
                            if i >= 1:
                                v.wait_ge(sPE.h, wp1[i - 1])
                            v.tensor_scalar(comb[:, :], msk1, c1, None, ALU.mult)
                            vg(v, v.tensor_scalar(Lm, msk2, c2, None, ALU.mult))
                            sDVE.inc(v.tensor_add(comb[:, :], comb[:, :], Lm))
                            v.wait_ge(sPE.h, wp1[i])
                            sDVE.inc(v.tensor_copy(
                                combT[:, i * 128:(i + 1) * 128], ps_cT[:, :]))

                    @blk.scalar
                    def _(sc):
                        for i in range(NT):
                            par = i % 2
                            if i >= 2:
                                sc.wait_ge(sDVE.h, wd0[i - 2])
                            sc.wait_ge(sPE2.h, wlg[i])
                            sACT2.inc(sc.copy(lgt[:, par * 8:(par + 1) * 8],
                                              ps_lg[par][:, 8:16]))
                            sc.wait_ge(sDVE.h, wd1[i])
                            sc.activation(cw[:, 2:3], cw[:, 0:1], AF.Sigmoid)
                            sACT.inc(sc.activation(cw[:, 3:4], cw[:, 1:2],
                                                   AF.Sigmoid))

                    @blk.tensor
                    def _(t):
                        for i in range(NT):
                            par = i % 2
                            if i >= 2:
                                t.wait_ge(sDVE.h, wd0[i - 2])
                                t.wait_ge(sACT2.h, wle[i - 2])
                            for hc in range(HC):
                                inst = t.matmul(
                                    ps_lg[par][:, :],
                                    xn_f[:, hc * S + i * 128:
                                         hc * S + (i + 1) * 128],
                                    gwst[:, hc * 16:(hc + 1) * 16],
                                    start=(hc == 0), stop=(hc == HC - 1))
                            sPE2.inc(inst)
                            t.wait_ge(sDVE.h, wd2[i])
                            if i >= 1:
                                t.wait_ge(sDVE.h, wd3[i - 1])
                            sPE.inc(t.transpose(ps_cT[:, :], comb[:, :],
                                                id_f[:, :]))

        def phase_l0ffn():
            with ExitStack() as pc:
                psb = lambda n_, s_, d=F32: pc.enter_context(nc.sbuf_tensor(n_, s_, d))
                pps = lambda n_, s_, d=F32: pc.enter_context(nc.psum_tensor(n_, s_, d))
                acc = psb("acc", [128, 4 * D])
                wgt = [psb(f"wgt{b}", [128, 2 * HC * 512], BF16) for b in range(2)]
                wut = [psb(f"wut{b}", [128, 2 * HC * 512], BF16) for b in range(2)]
                wdt = [psb("wdt", [128, 2 * NIC * D], BF16)] * 2
                hmS = psb("hmS", [128, 2 * NIC * 512], BF16)
                silu_sb = psb("silu_b", [128, 512], BF16)
                bc_sb = psb("bc_sb", [128, 512], BF16)
                ps_g = pps("ps_g", [128, 512])
                ps_u = pps("ps_u", [128, 512])
                ps_bc = pps("ps_bc", [128, 512])
                ps_r = [pps(f"ps_r{b}", [128, D]) for b in range(2)]

                W0, S0, L0W = sW.n, sS.n, sL.n
                NW = 2 * (2 * HC + NIC)
                QN8 = 8
                NGI = QN8 * NPAIRS5

                wg_, wu_, wbc, wdn, wsil, wbce, whm, wev, wfs = \
                    {}, {}, {}, {}, {}, {}, {}, {}, {}
                pe, pe2, pe3 = sPE.n, sPE2.n, sPE3.n
                act, act2, dve, ev = sACT.n, sACT2.n, sDVE.n, sEV.n
                for gi in range(NGI):
                    q, p = divmod(gi, NPAIRS5)
                    for tg in range(1):
                        for slot in range(2):
                            if p > 0:
                                pe2 += 1; wbc[gi, tg, slot] = pe2
                                act2 += 1; wbce[gi, tg, slot] = act2
                            for ic in range(NIC):
                                pe += 1; wg_[gi, tg, slot, ic] = pe
                                act += 1; wsil[gi, tg, slot, ic] = act
                                pe2 += 1; wu_[gi, tg, slot, ic] = pe2
                                dve += 1; whm[gi, tg, slot, ic] = dve
                        for t1 in range(4):
                            pe3 += 1; wdn[gi, tg, t1] = pe3
                            ev += 1; wev[gi, tg, t1] = ev
                    if p == NPAIRS5 - 1:
                        for tl in range(4):
                            act2 += 1; wfs[q, tl] = act2
                wpd = {gi: wdn[gi, 0, 3] for gi in range(NGI)}
                stq = {q: S0 + 16 * 4 * (q + 1) for q in range(QN8)}

                def bc_prev_reader(gi, tg, slot):
                    ks = [k for k in whm.keys()
                          if k < (gi, tg, slot, 0) and (k[0] % NPAIRS5) != 0]
                    return whm[ks[-1]] if ks else None

                def wslot_aps(b, p, slot):
                    res = []
                    for hc in range(HC):
                        gd = wgt[b][:, slot * HC * 512 + hc * 512:
                                    slot * HC * 512 + (hc + 1) * 512]
                        ud = wut[b][:, slot * HC * 512 + hc * 512:
                                    slot * HC * 512 + (hc + 1) * 512]
                        if p == 0:
                            gs = wsgT_s[hc * 128:(hc + 1) * 128,
                                        slot * 512:(slot + 1) * 512]
                            us = wsuT_s[hc * 128:(hc + 1) * 128,
                                        slot * 512:(slot + 1) * 512]
                        else:
                            e = (p - 1) * 2 + slot
                            gs = wgT_s[e, hc * 128:(hc + 1) * 128, :]
                            us = wuT_s[e, hc * 128:(hc + 1) * 128, :]
                        res.append((gd, gs, False)); res.append((ud, us, False))
                    for ic in range(NIC):
                        dd = wdt[b][:, slot * NIC * D + ic * D:
                                    slot * NIC * D + (ic + 1) * D]
                        if p == 0:
                            ds = wsdT_s[slot * 512 + ic * 128:
                                        slot * 512 + (ic + 1) * 128, :]
                        else:
                            e = (p - 1) * 2 + slot
                            ds = wdT_s[e, ic * 128:(ic + 1) * 128, :]
                        res.append((dd, ds, True))
                    return res

                with nc.Block() as blk:
                    @blk.gpsimd
                    def _(g):
                        for gi in range(NGI):
                            q, p = divmod(gi, NPAIRS5)
                            if q == 0:
                                g.wait_ge(sST.h, 16 * (3 + 6 * p if p else 3))
                            if gi >= 2:
                                g.wait_ge(sPE3.h, wpd[gi - 2])
                            s1 = wslot_aps(gi % 2, p, 1)
                            for dst, srcap, isd in s1:
                                if not isd:
                                    sW.d(g.dma_start(dst, srcap))
                            if gi >= 1:
                                g.wait_ge(sPE3.h, wpd[gi - 1])
                            for slot in range(2):
                                for dst, srcap, isd in wslot_aps(gi % 2, p, slot):
                                    if isd:
                                        sW.d(g.dma_start(dst, srcap))
                        g.wait_ge(sW.h, sW.n)
                        g.wait_ge(sST.h, sST.n)

                    @blk.sync
                    def _(s):
                        for gi in range(NGI):
                            q, p = divmod(gi, NPAIRS5)
                            if q == 0:
                                s.wait_ge(sST.h, 16 * (3 + 6 * p if p else 3))
                            if gi >= 2:
                                s.wait_ge(sPE3.h, wpd[gi - 2])
                            for dst, srcap, isd in wslot_aps(gi % 2, p, 0):
                                if not isd:
                                    sL.d(s.dma_start(dst, srcap))
                        s.wait_ge(sL.h, sL.n)

                    @blk.tensor
                    def _(t):
                        for gi in range(NGI):
                            q, p = divmod(gi, NPAIRS5)
                            b = gi % 2
                            t.wait_ge(sL.h, L0W + 16 * 16 * (gi + 1))
                            t.wait_ge(sW.h, W0 + 16 * (24 * gi + 16))
                            for tg in range(1):
                                tcol = q * 512
                                for slot in range(2):
                                    if p > 0:
                                        e = (p - 1) * 2 + slot
                                        prev = ordered_prev(wbce, (gi, tg, slot))
                                        if prev is not None:
                                            t.wait_ge(sACT2.h, prev)
                                        sPE2.inc(t.matmul(
                                            ps_bc[:, :],
                                            esel_bf[:, e * 128:(e + 1) * 128],
                                            combT[:, tcol:tcol + 512],
                                            start=True, stop=True))
                                    for ic in range(NIC):
                                        prev = ordered_prev(wsil, (gi, tg, slot, ic))
                                        if prev is not None:
                                            t.wait_ge(sACT.h, prev)
                                        for hc in range(HC):
                                            col = (slot * HC * 512 + hc * 512
                                                   + ic * 128)
                                            inst = t.matmul(
                                                ps_g[:, :], wgt[b][:, col:col + 128],
                                                xn_f[:, hc * S + tcol:
                                                     hc * S + tcol + 512],
                                                start=(hc == 0), stop=(hc == HC - 1))
                                        sPE.inc(inst)
                                        prev = ordered_prev(whm, (gi, tg, slot, ic))
                                        if prev is not None:
                                            t.wait_ge(sDVE.h, prev)
                                        for hc in range(HC):
                                            col = (slot * HC * 512 + hc * 512
                                                   + ic * 128)
                                            inst = t.matmul(
                                                ps_u[:, :], wut[b][:, col:col + 128],
                                                xn_f[:, hc * S + tcol:
                                                     hc * S + tcol + 512],
                                                start=(hc == 0), stop=(hc == HC - 1))
                                        sPE2.inc(inst)
                                for t1 in range(4):
                                    if tg == 0 and t1 == 0:
                                        t.wait_ge(sW.h, W0 + 16 * 24 * (gi + 1))
                                    t.wait_ge(sDVE.h, whm[gi, tg, 1, NIC - 1])
                                    prev = ordered_prev(wev, (gi, tg, t1), back=2)
                                    if prev is not None:
                                        t.wait_ge(sEV.h, prev)
                                    for slot in range(2):
                                        for ic in range(NIC):
                                            for nh in range(2):
                                                inst = t.matmul(
                                                    ps_r[t1 % 2][:, nh * 512:
                                                                 (nh + 1) * 512],
                                                    hmS[:, slot * NIC * 512
                                                        + ic * 512 + t1 * 128:
                                                        slot * NIC * 512
                                                        + ic * 512 + (t1 + 1) * 128],
                                                    wdt[b][:, slot * NIC * D + ic * D
                                                           + nh * 512:
                                                           slot * NIC * D + ic * D
                                                           + (nh + 1) * 512],
                                                    start=(slot == 0 and ic == 0),
                                                    stop=(slot == 1 and
                                                          ic == NIC - 1))
                                    sPE3.inc(inst)

                    @blk.scalar
                    def _(sc):
                        for gi in range(NGI):
                            q, p = divmod(gi, NPAIRS5)
                            for tg in range(1):
                                for slot in range(2):
                                    if p > 0:
                                        sc.wait_ge(sPE2.h, wbc[gi, tg, slot])
                                        prev = bc_prev_reader(gi, tg, slot)
                                        if prev is not None:
                                            sc.wait_ge(sDVE.h, prev)
                                        sACT2.inc(sc.copy(bc_sb[:, :], ps_bc[:, :]))
                                    for ic in range(NIC):
                                        sc.wait_ge(sPE.h, wg_[gi, tg, slot, ic])
                                        prev = ordered_prev(whm, (gi, tg, slot, ic))
                                        if prev is not None:
                                            sc.wait_ge(sDVE.h, prev)
                                        sACT.inc(sc.activation(
                                            silu_sb[:, :], ps_g[:, :], AF.Silu))
                            if p == NPAIRS5 - 1:
                                for tl in range(4):
                                    sc.wait_ge(sEV.h, wev[gi, 0, tl])
                                    a_sl = acc[:, tl * D:(tl + 1) * D]
                                    sACT2.inc(sc.activation(a_sl, a_sl, AF.Silu))
                                    sS.d(sc.dma_start(
                                        g_scr[q * 512 + tl * 128:
                                              q * 512 + (tl + 1) * 128, :],
                                        a_sl))
                        sc.wait_ge(sS.h, sS.n)

                    @blk.vector
                    def _(v):
                        for gi in range(NGI):
                            q, p = divmod(gi, NPAIRS5)
                            for tg in range(1):
                                for slot in range(2):
                                    for ic in range(NIC):
                                        v.wait_ge(sACT.h, wsil[gi, tg, slot, ic])
                                        v.wait_ge(sPE2.h, wu_[gi, tg, slot, ic])
                                        if ic == 0 and slot == 0:
                                            if gi >= 1:
                                                v.wait_ge(sPE3.h, wdn[gi - 1, 0, 3])
                                        dst = hmS[:, slot * NIC * 512 + ic * 512:
                                                  slot * NIC * 512 + (ic + 1) * 512]
                                        if p > 0:
                                            v.wait_ge(sACT2.h, wbce[gi, tg, slot])
                                            v.tensor_tensor(dst, silu_sb[:, :],
                                                            ps_u[:, :], ALU.mult)
                                            sDVE.inc(v.tensor_tensor(
                                                dst, dst, bc_sb[:, :], ALU.mult))
                                        else:
                                            sDVE.inc(v.tensor_tensor(
                                                dst, silu_sb[:, :], ps_u[:, :],
                                                ALU.mult))
                                for t1 in range(4):
                                    v.wait_ge(sPE3.h, wdn[gi, tg, t1])
                                    tl = tg * 4 + t1
                                    a_sl = acc[:, tl * D:(tl + 1) * D]
                                    if p == 0:
                                        if q > 0:
                                            v.wait_ge(sS.h, stq[q - 1])
                                        sEV.inc(v.tensor_copy(a_sl,
                                                              ps_r[t1 % 2][:, :]))
                                    else:
                                        sEV.inc(v.tensor_tensor(
                                            a_sl, a_sl, ps_r[t1 % 2][:, :],
                                            ALU.add))



        def phase_l0attn():
            QG0 = 256
            NQG0 = S // QG0
            with ExitStack() as pc:
                psb = lambda n_, s_, d=F32: pc.enter_context(nc.sbuf_tensor(n_, s_, d))
                pps = lambda n_, s_, d=F32: pc.enter_context(nc.psum_tensor(n_, s_, d))
                xnt = psb("xnt", [128, NKC * D], BF16)
                pT = [psb(f"pT0{b_}", [128, NKC * QG0], BF16) for b_ in range(2)]
                qts = psb("qts", [128, HC * QG0], BF16)
                sq2 = qts
                bnd = psb("bnd", [1, 8])
                bnd2 = psb("bnd2", [2, 16])
                bias0 = psb("bias0", [128, 1])
                rinv_sb = psb("rinv0", [128, 1])
                gbuf = psb("gbuf", [128, 2 * D])
                xbuf = psb("xbuf", [128, 2 * D])
                ps_sT = [pps(f"ps_sT{b_}", [128, 1024]) for b_ in range(2)]
                ps_a = pps("ps_a0", [128, 512])
                ps_b = pps("ps_b0", [128, 513])
                ps_bnd = pps("ps_bnd", [2, 512])

                L0, S0 = sL.n, sS.n
                wsqd, wbmm, wbrd = {}, {}, {}
                act2, pe3, dve2 = sACT2.n, sPE3.n, sDVE2.n
                for ts in range(8):
                    for c in range(HC):
                        act2 += 1; wsqd[ts, c] = act2
                        pe3 += 1; wbmm[ts, c] = pe3
                    dve2 += 1; wbrd[ts] = dve2
                dve2 += 1; w_cmb = dve2
                pe3 += 1; w_e2t = pe3
                dve2 += 1; w_prod = dve2
                act2 += 1; w_bM = act2
                pe3 += 1; w_bc0 = pe3
                dve2 += 1; w_bias0 = dve2

                wqts, wqk, wexp, wpv, wup = {}, {}, {}, {}, {}
                dve, pe, act, pe2, ev = sDVE.n, sPE.n, sACT.n, sPE2.n, sEV.n
                for qg in range(NQG0):
                    dve += 1; wqts[qg] = dve
                    for kq in range(8):
                        pe += 1; wqk[qg, kq] = pe
                        act += 1; wexp[qg, kq] = act
                    for qs in range(2):
                        pe2 += 1; wpv[qg, qs] = pe2
                        ev += 1; wup[qg, qs] = ev

                with nc.Block() as blk:
                    @blk.sync
                    def _(s):
                        for kc in range(NKC):
                            sL.d(s.dma_start(
                                xnt[:, kc * D:(kc + 1) * D],
                                xn_dram[kc * 128:(kc + 1) * 128, :]))
                        for n in range(2):
                            row = n * 128
                            sL.d(s.dma_start(gbuf[:, (n % 2) * D:(n % 2 + 1) * D],
                                             g_scr[row:row + 128, :]))
                            sL.d(s.dma_start(xbuf[:, (n % 2) * D:(n % 2 + 1) * D],
                                             x_in[row:row + 128, :]))
                        for n in range(NT):
                            qg, qs = divmod(n, 2)
                            row = n * 128
                            s.wait_ge(sEV.h, wup[qg, qs])
                            sS.d(s.dma_start(
                                x_resid[row:row + 128, :],
                                xbuf[:, (n % 2) * D:(n % 2 + 1) * D]))
                            if n + 2 < NT:
                                s.wait_ge(sS.h, S0 + 16 * (n + 1))
                                row2 = (n + 2) * 128
                                sL.d(s.dma_start(
                                    gbuf[:, (n % 2) * D:(n % 2 + 1) * D],
                                    g_scr[row2:row2 + 128, :]))
                                sL.d(s.dma_start(
                                    xbuf[:, (n % 2) * D:(n % 2 + 1) * D],
                                    x_in[row2:row2 + 128, :]))
                        s.wait_ge(sS.h, sS.n)

                    @blk.scalar
                    def _(sc):
                        for ts in range(8):
                            for c in range(HC):
                                nsq = ts * HC + c
                                if nsq >= 2:
                                    pts, pcc = divmod(nsq - 2, HC)
                                    sc.wait_ge(sPE3.h, wbmm[pts, pcc])
                                sACT2.inc(sc.activation(
                                    sq2[:, (c % 2) * 512:(c % 2 + 1) * 512],
                                    xn_f[:, c * S + ts * 512:
                                         c * S + (ts + 1) * 512],
                                    AF.Square))
                        sc.wait_ge(sDVE2.h, w_prod)
                        sc.activation(bnd[:, 2:3], bnd[:, 1:2], AF.Ln)
                        sACT2.inc(sc.activation(bnd[:, 2:3], bnd[:, 2:3],
                                                AF.Exp, scale=0.5))
                        for qg in range(NQG0):
                            for kq in range(8):
                                sc.wait_ge(sPE.h, wqk[qg, kq])
                                if qg == 0 and kq == 0:
                                    sc.wait_ge(sDVE2.h, w_bias0)
                                if qg >= 2 and kq == 0:
                                    sc.wait_ge(sPE2.h, wpv[qg - 2, 1])
                                sACT.inc(sc.activation(
                                    pT[qg % 2][:, kq * 1024:(kq + 1) * 1024],
                                    ps_sT[kq % 2][:, :],
                                    AF.Exp, bias=bias0[:, 0:1], scale=1.0))

                    @blk.tensor
                    def _(t):
                        for ts in range(8):
                            if ts >= 1:
                                t.wait_ge(sDVE2.h, wbrd[ts - 1])
                            for c in range(HC):
                                t.wait_ge(sACT2.h, wsqd[ts, c])
                                sPE3.inc(t.matmul(
                                    ps_bnd[:, :], bndsel[:, 2 * c:2 * c + 2],
                                    sq2[:, (c % 2) * 512:(c % 2 + 1) * 512],
                                    start=(c == 0), stop=(c == HC - 1)))
                        t.wait_ge(sDVE2.h, w_cmb)
                        sPE3.inc(t.transpose(ps_bnd[0:1, 0:2], bnd2[:, 8:9],
                                             id_f[0:2, 0:2]))
                        t.wait_ge(sACT2.h, w_bM)
                        sPE3.inc(t.matmul(ps_a[:, 0:1], onr_f[:, :], bnd[:, 2:3],
                                          start=True, stop=True))
                        for qg in range(NQG0):
                            t.wait_ge(sDVE.h, wqts[qg])
                            for kq in range(8):
                                prev = ordered_prev(wexp, (qg, kq), back=2)
                                if prev is not None:
                                    t.wait_ge(sACT.h, prev)
                                for kci in range(4):
                                    kc = kq * 4 + kci
                                    for c in range(HC):
                                        inst = t.matmul(
                                            ps_sT[kq % 2][:, kci * QG0:
                                                          (kci + 1) * QG0],
                                            xn_f[:, c * S + kc * 128:
                                                 c * S + (kc + 1) * 128],
                                            qts[:, c * QG0:(c + 1) * QG0],
                                            start=(c == 0), stop=(c == HC - 1))
                                sPE.inc(inst)
                            t.wait_ge(sACT.h, wexp[qg, 7])
                            for qs in range(2):
                                if qg == 0 and qs == 0:
                                    t.wait_ge(sDVE2.h, w_bias0)
                                prev = ordered_prev(wup, (qg, qs))
                                if prev is not None:
                                    t.wait_ge(sEV.h, prev)
                                for kc in range(NKC):
                                    lhs = pT[qg % 2][:, kc * QG0 + qs * 128:
                                             kc * QG0 + (qs + 1) * 128]
                                    st_ = (kc == 0); sp_ = (kc == NKC - 1)
                                    t.matmul(ps_a[:, :], lhs,
                                             xnt[:, kc * D:kc * D + 512],
                                             start=st_, stop=sp_)
                                    t.matmul(ps_b[:, 0:512], lhs,
                                             xnt[:, kc * D + 512:(kc + 1) * D],
                                             start=st_, stop=sp_)
                                    inst = t.matmul(ps_b[:, 512:513], lhs,
                                                    onc_bf[:, :],
                                                    start=st_, stop=sp_)
                                sPE2.inc(inst)

                    @blk.vector
                    def _(v):
                        for ts in range(8):
                            v.wait_ge(sPE3.h, wbmm[ts, HC - 1])
                            sDVE2.inc(v.reduce_max(bnd2[:, ts:ts + 1],
                                                   ps_bnd[:, :], AX.X))
                        v.wait_ge(sDVE2.h, wbrd[7])
                        sDVE2.inc(v.reduce_max(bnd2[:, 8:9], bnd2[:, 0:8], AX.X))
                        v.wait_ge(sPE3.h, w_e2t)
                        vg(v, v.tensor_copy(bnd[:, 4:6], ps_bnd[0:1, 0:2]))
                        sDVE2.inc(v.tensor_mul(bnd[:, 1:2], bnd[:, 4:5],
                                               bnd[:, 5:6]))
                        v.wait_ge(sPE3.h, w_bc0)
                        sDVE2.inc(v.tensor_scalar(bias0[:, :], ps_a[:, 0:1],
                                                  -1.02, None, ALU.mult))
                        n = 0
                        for qg in range(NQG0):
                            if qg == 0:
                                v.wait_ge(sPE3.h, wbmm[7, HC - 1])
                            if qg >= 1:
                                v.wait_ge(sPE.h, wqk[qg - 1, 7])
                            for c in range(HC):
                                inst = v.tensor_scalar(
                                    qts[:, c * QG0:(c + 1) * QG0],
                                    xn_f[:, c * S + qg * QG0:
                                         c * S + (qg + 1) * QG0],
                                    w_vec[:, c:c + 1], None, ALU.mult)
                            sDVE.inc(inst)
                            for qs in range(2):
                                par = n % 2
                                v.wait_ge(sPE2.h, wpv[qg, qs])
                                v.wait_ge(sL.h, L0 + 16 * NKC + 32 * (n + 1))
                                vg(v, v.reciprocal(rinv_sb[:, :],
                                                   ps_b[:, 512:513]))
                                gb1 = gbuf[:, par * D:par * D + 512]
                                gb2 = gbuf[:, par * D + 512:(par + 1) * D]
                                v.tensor_scalar(gb1, gb1, rinv_sb[:, 0:1],
                                                None, ALU.mult)
                                v.tensor_tensor(gb1, gb1, ps_a[:, :], ALU.mult)
                                v.tensor_add(xbuf[:, par * D:par * D + 512],
                                             gb1,
                                             xbuf[:, par * D:par * D + 512])
                                v.tensor_scalar(gb2, gb2, rinv_sb[:, 0:1],
                                                None, ALU.mult)
                                v.tensor_tensor(gb2, gb2, ps_b[:, 0:512],
                                                ALU.mult)
                                sEV.inc(v.tensor_add(
                                    xbuf[:, par * D + 512:(par + 1) * D],
                                    gb2,
                                    xbuf[:, par * D + 512:(par + 1) * D]))
                                n += 1

        def phase_l1ffn():
            with ExitStack() as pc:
                psb = lambda n_, s_, d=F32: pc.enter_context(nc.sbuf_tensor(n_, s_, d))
                pps = lambda n_, s_, d=F32: pc.enter_context(nc.psum_tensor(n_, s_, d))
                cbn = psb("cbn", [128, 2 * D])
                cbs = psb("cbs", [128, 2 * D], BF16)
                CBT = psb("CBT", [128, HC * D], BF16)
                g1 = psb("g1", [128, 2 * D])
                ps_t = [pps(f"ps_t1{b}", [128, 128], BF16) for b in range(2)]
                ps_f = [pps(f"ps_f{b}", [128, D]) for b in range(2)]

                L0, S0 = sL.n, sS.n
                wsc, wtr, wcp = {}, {}, {}
                dve, pe3, dve2 = sDVE.n, sPE3.n, sDVE2.n
                for nck in range(HC):
                    dve += 1; wsc[nck] = dve
                    for kc in range(HC):
                        pe3 += 1; wtr[nck, kc] = pe3
                        dve2 += 1; wcp[nck, kc] = dve2
                wmm, wsl = {}, {}
                pe, act = sPE.n, sACT.n
                for i in range(NT):
                    pe += 1; wmm[i] = pe
                    act += 1; wsl[i] = act

                with nc.Block() as blk:
                    @blk.sync
                    def _(s):
                        for nck in range(HC):
                            if nck >= 2:
                                s.wait_ge(sDVE.h, wsc[nck - 2])
                            sL.d(s.dma_start(
                                cbn[:, (nck % 2) * D:(nck % 2 + 1) * D],
                                cb_d[nck * 128:(nck + 1) * 128, :]))
                        for i in range(NT):
                            s.wait_ge(sACT.h, wsl[i])
                            sS.d(s.dma_start(
                                g_scr[i * 128:(i + 1) * 128, :],
                                g1[:, (i % 2) * D:(i % 2 + 1) * D]))
                        s.wait_ge(sS.h, sS.n)

                    @blk.vector
                    def _(v):
                        for nck in range(HC):
                            v.wait_ge(sL.h, L0 + 16 * (nck + 1))
                            if nck >= 2:
                                v.wait_ge(sPE3.h, wtr[nck - 2, HC - 1])
                            sDVE.inc(v.tensor_scalar(
                                cbs[:, (nck % 2) * D:(nck % 2 + 1) * D],
                                cbn[:, (nck % 2) * D:(nck % 2 + 1) * D],
                                scb_sb[:, nck:nck + 1], 1.0 / 127.0,
                                ALU.mult, ALU.mult))
                            for kc in range(HC):
                                v.wait_ge(sPE3.h, wtr[nck, kc])
                                dstv = CBT.rearrange("p (k n) -> p k n", k=HC)
                                sDVE2.inc(v.tensor_copy(
                                    dstv[:, kc, nck * 128:(nck + 1) * 128],
                                    ps_t[kc % 2][:, :]))

                    @blk.tensor
                    def _(t):
                        for nck in range(HC):
                            t.wait_ge(sDVE.h, wsc[nck])
                            for kc in range(HC):
                                prev = ordered_prev(wcp, (nck, kc), back=2)
                                if prev is not None:
                                    t.wait_ge(sDVE2.h, prev)
                                sPE3.inc(t.transpose(
                                    ps_t[kc % 2][:, :],
                                    cbs[:, (nck % 2) * D + kc * 128:
                                        (nck % 2) * D + (kc + 1) * 128],
                                    id_bf[:, :]))
                        for i in range(NT):
                            if i >= 2:
                                t.wait_ge(sACT.h, wsl[i - 2])
                            if i == 0:
                                t.wait_ge(sDVE2.h, wcp[HC - 1, HC - 1])
                            for nh in range(2):
                                for kc in range(HC):
                                    t.matmul(ps_f[i % 2][:, nh * 512:(nh + 1) * 512],
                                             xn_f[:, kc * S + i * 128:
                                                  kc * S + (i + 1) * 128],
                                             CBT[:, kc * D + nh * 512:
                                                 kc * D + (nh + 1) * 512],
                                             start=(kc == 0), stop=False)
                                inst = t.matmul(
                                    ps_f[i % 2][:, nh * 512:(nh + 1) * 512],
                                    onr_bf[:, :],
                                    b1_bf[0:1, nh * 512:(nh + 1) * 512],
                                    start=False, stop=True)
                            sPE.inc(inst)

                    @blk.scalar
                    def _(sc):
                        for i in range(NT):
                            sc.wait_ge(sPE.h, wmm[i])
                            if i >= 2:
                                sc.wait_ge(sS.h, S0 + 16 * (i - 1))
                            sACT.inc(sc.activation(
                                g1[:, (i % 2) * D:(i % 2 + 1) * D],
                                ps_f[i % 2][:, :], AF.Silu))

        def phase_l1attn():
            with ExitStack() as pc:
                psb = lambda n_, s_, d=F32: pc.enter_context(nc.sbuf_tensor(n_, s_, d))
                pps = lambda n_, s_, d=F32: pc.enter_context(nc.psum_tensor(n_, s_, d))
                wq_sb = psb("wq_sb", [128, HC * D], BF16)
                wk_sb = psb("wk_sb", [128, HC * D], BF16)
                wv_sb = psb("wv_sb", [128, HC * D], BF16)
                pbuf = [psb(f"pj{w}", [128, 4 * D], BF16) for w in range(3)]
                prodV = psb("prodV", [128, 8192], BF16)
                pairV = psb("pairV", [128, 6144], BF16)
                SCV = psb("SCV", [128, 256])
                ppV = psb("ppV", [128, 256], BF16)
                atV = psb("atV", [128, 256], BF16)
                sumV = psb("sumV", [128, 32])
                Ob = psb("Obf", [128, 2 * D])
                ps_p = [pps(f"ps_p{w}", [128, D]) for w in range(3)]

                W0 = sW.n
                wpm, wev, wexV = {}, {}, {}
                wSCV, wATV, wP2V, wObV = {}, {}, {}, {}
                pe, a2, act, d2 = sPE.n, sACT2.n, sACT.n, sDVE2.n
                for i in range(NT):
                    for w in range(3):
                        pe += 1; wpm[i, w] = pe
                        a2 += 1; wev[i, w] = a2
                for i in range(NT):
                    act += 1; wexV[i] = act
                    d2 += 1; wSCV[i] = d2
                    d2 += 1; wATV[i] = d2
                    d2 += 1; wP2V[i] = d2
                    d2 += 1; wObV[i] = d2
                WB = W0 + 16 * 24
                stF2 = {i: WB + 16 * (i + 1) for i in range(NT)}

                def scram(eng, pairs):
                    ov = xn_dram.rearrange("(r s) c -> r s c", s=16)
                    for p in pairs:
                        dst = ao_hbm[p].rearrange("(q rr) (h d) -> q rr h d",
                                                  q=16, h=2)
                        for q in range(16):
                            sW.d(eng.dma_start(
                                dst[q, :, :, :],
                                ov[:, 2 * p:2 * p + 2, q * 64:(q + 1) * 64]))

                with nc.Block() as blk:
                    @blk.gpsimd
                    def _(g):
                        for wsrc, wbuf in ((wqT_d, wq_sb), (wkT_d, wk_sb),
                                           (wvTs_d, wv_sb)):
                            for hc in range(HC):
                                sW.d(g.dma_start(
                                    wbuf[:, hc * D:(hc + 1) * D],
                                    wsrc[hc * 128:(hc + 1) * 128, :]))
                        for i in range(NT):
                            g.wait_ge(sDVE2.h, wObV[i])
                            sW.d(g.dma_start(
                                xn_dram[i * 128:(i + 1) * 128, :],
                                Ob[:, (i % 2) * D:(i % 2 + 1) * D]))
                        g.wait_ge(sW.h, stF2[NT - 1])
                        scram(g, (0, 1, 2))
                        g.wait_ge(sW.h, sW.n)

                    @blk.tensor
                    def _(t):
                        t.wait_ge(sW.h, W0 + 16 * 24)
                        for i in range(NT):
                            for w in range(3):
                                wsb = (wq_sb, wk_sb, wv_sb)[w]
                                brow = (bqr_bf, bkr_bf, bvs_bf)[w]
                                prev = ordered_prev(wev, (i, w), back=3)
                                if prev is not None:
                                    t.wait_ge(sACT2.h, prev)
                                for hc in range(HC):
                                    for nh in range(2):
                                        t.matmul(
                                            ps_p[w][:, nh * 512:(nh + 1) * 512],
                                            xn_f[:, hc * S + i * 128:
                                                 hc * S + (i + 1) * 128],
                                            wsb[:, hc * D + nh * 512:
                                                hc * D + (nh + 1) * 512],
                                            start=(hc == 0), stop=False)
                                for nh in range(2):
                                    inst = t.matmul(
                                        ps_p[w][:, nh * 512:(nh + 1) * 512],
                                        onr_bf[:, :],
                                        brow[0:1, nh * 512:(nh + 1) * 512],
                                        start=False, stop=True)
                                sPE.inc(inst)

                    @blk.sync
                    def _(s):
                        s.wait_ge(sW.h, stF2[NT - 1])
                        scram(s, (6, 7))

                    @blk.vector
                    def _(v):
                        for i in range(NT):
                            par = i % 2
                            s4 = (i % 4) * D
                            v.wait_ge(sACT2.h, wev[i, 1])
                            if i >= 1:
                                v.wait_ge(sACT.h, wexV[i - 1])
                            for c in range(2):
                                qv = bass.AP(pbuf[0], s4 + c * 512,
                                             [[4 * D, 128], [64, 8], [0, 16],
                                              [1, 64]])
                                kv = bass.AP(pbuf[1], s4,
                                             [[4 * D, 128], [0, 8], [64, 16],
                                              [1, 64]])
                                v.tensor_tensor(prodV[:, :], qv, kv, ALU.mult)
                                v.tensor_tensor(
                                    bass.AP(pairV, 0,
                                            [[6144, 128], [32, 128], [1, 32]]),
                                    bass.AP(prodV, 0,
                                            [[8192, 128], [64, 128], [1, 32]]),
                                    bass.AP(prodV, 32,
                                            [[8192, 128], [64, 128], [1, 32]]),
                                    ALU.add)
                                v.tensor_tensor(
                                    bass.AP(pairV, 4096,
                                            [[6144, 128], [16, 128], [1, 16]]),
                                    bass.AP(pairV, 0,
                                            [[6144, 128], [32, 128], [1, 16]]),
                                    bass.AP(pairV, 16,
                                            [[6144, 128], [32, 128], [1, 16]]),
                                    ALU.add)
                                inst = v.reduce_sum(
                                    SCV[:, c * 128:(c + 1) * 128],
                                    bass.AP(pairV, 4096,
                                            [[6144, 128], [16, 128], [1, 16]]),
                                    AX.X)
                            sDVE2.inc(inst)
                            v.wait_ge(sACT.h, wexV[i])
                            vg(v, v.reduce_sum(
                                sumV[:, 0:16],
                                ppV.rearrange("p (a b) -> p a b", b=16), AX.X))
                            vg(v, v.reciprocal(sumV[:, 16:32], sumV[:, 0:16]))
                            rb = bass.AP(sumV, 16,
                                         [[32, 128], [1, 16], [0, 16]])
                            sDVE2.inc(v.tensor_tensor(atV[:, :], ppV[:, :], rb,
                                                      ALU.mult))
                            v.wait_ge(sACT2.h, wev[i, 2])
                            if i >= 2:
                                v.wait_ge(sW.h, stF2[i - 2])
                            for c in range(2):
                                av = bass.AP(atV, c * 128,
                                             [[256, 128], [16, 8], [0, 64],
                                              [1, 16]])
                                vv = bass.AP(pbuf[2], s4,
                                             [[4 * D, 128], [0, 8], [16, 64],
                                              [1, 16]])
                                inst = v.tensor_tensor(prodV[:, :], av, vv,
                                                       ALU.mult)
                                if c == 1:
                                    sDVE2.inc(inst)
                                v.tensor_tensor(
                                    bass.AP(pairV, 0,
                                            [[6144, 128], [8, 512], [1, 8]]),
                                    bass.AP(prodV, 0,
                                            [[8192, 128], [16, 512], [1, 8]]),
                                    bass.AP(prodV, 8,
                                            [[8192, 128], [16, 512], [1, 8]]),
                                    ALU.add)
                                v.tensor_tensor(
                                    bass.AP(pairV, 4096,
                                            [[6144, 128], [4, 512], [1, 4]]),
                                    bass.AP(pairV, 0,
                                            [[6144, 128], [8, 512], [1, 4]]),
                                    bass.AP(pairV, 4,
                                            [[6144, 128], [8, 512], [1, 4]]),
                                    ALU.add)
                                inst = v.reduce_sum(
                                    Ob[:, par * D + c * 512:
                                       par * D + (c + 1) * 512],
                                    bass.AP(pairV, 4096,
                                            [[6144, 128], [4, 512], [1, 4]]),
                                    AX.X)
                            sDVE2.inc(inst)

                    @blk.scalar
                    def _(sc):
                        for ii in range(NT + 2):
                            j = ii - 2
                            if j >= 0:
                                sc.wait_ge(sDVE2.h, wSCV[j])
                                if j >= 1:
                                    sc.wait_ge(sDVE2.h, wATV[j - 1])
                                sACT.inc(sc.activation(ppV[:, :], SCV[:, :],
                                                       AF.Exp, scale=0.125))
                            if ii < NT:
                                if ii >= 4:
                                    sc.wait_ge(sDVE2.h, wP2V[ii - 4])
                                for w in range(3):
                                    sc.wait_ge(sPE.h, wpm[ii, w])
                                    sACT2.inc(sc.copy(
                                        pbuf[w][:, (ii % 4) * D:
                                                (ii % 4 + 1) * D],
                                        ps_p[w][:, :]))
                        sc.wait_ge(sW.h, stF2[NT - 1])
                        scram(sc, (3, 4, 5))

        def phase_l1out():
            with ExitStack() as pc:
                psb = lambda n_, s_, d=F32: pc.enter_context(nc.sbuf_tensor(n_, s_, d))
                pps = lambda n_, s_, d=F32: pc.enter_context(nc.psum_tensor(n_, s_, d))
                woT_sb = psb("woT_sb", [128, HC * D], BF16)
                aoT = psb("aoT", [128, HC * S], BF16)
                g1b = psb("g1b", [128, 2 * D])
                x1b = psb("x1b", [128, 2 * D])
                ob = psb("ob", [128, 2 * D])
                tmpG = psb("tmpG", [128, 512])
                ps_s = [pps(f"ps_s{b}", [128, D]) for b in range(2)]

                W0, L0, S0 = sW.n, sL.n, sS.n
                wmmG, wupG = {}, {}
                pe, ev = sPE.n, sEV.n
                for i in range(NT):
                    pe += 1; wmmG[i] = pe
                    ev += 1; wupG[i] = ev

                with nc.Block() as blk:
                    @blk.gpsimd
                    def _(g):
                        for pch in range(HC):
                            sW.d(g.dma_start(
                                woT_sb[:, pch * D:(pch + 1) * D],
                                woT_d[pch * 128:(pch + 1) * 128, :]))
                        g.wait_ge(sW.h, sW.n)

                    @blk.sync
                    def _(s):
                        for p in range(NPAIR):
                            sL.d(s.dma_start_transpose(
                                aoT[:, p * S:(p + 1) * S], ao_hbm[p][:, :]))
                        for i in range(2):
                            sL.d(s.dma_start(
                                g1b[:, (i % 2) * D:(i % 2 + 1) * D],
                                g_scr[i * 128:(i + 1) * 128, :]))
                            sL.d(s.dma_start(
                                x1b[:, (i % 2) * D:(i % 2 + 1) * D],
                                x_resid[i * 128:(i + 1) * 128, :]))
                        for i in range(NT):
                            s.wait_ge(sEV.h, wupG[i])
                            sS.d(s.dma_start(
                                out_d[i * 128:(i + 1) * 128, :],
                                ob[:, (i % 2) * D:(i % 2 + 1) * D]))
                            if i + 2 < NT:
                                s.wait_ge(sEV.h, wupG[i])
                                sL.d(s.dma_start(
                                    g1b[:, (i % 2) * D:(i % 2 + 1) * D],
                                    g_scr[(i + 2) * 128:(i + 3) * 128, :]))
                                sL.d(s.dma_start(
                                    x1b[:, (i % 2) * D:(i % 2 + 1) * D],
                                    x_resid[(i + 2) * 128:(i + 3) * 128, :]))
                        s.wait_ge(sS.h, sS.n)

                    @blk.tensor
                    def _(t):
                        t.wait_ge(sW.h, sW.n)
                        t.wait_ge(sL.h, L0 + 16 * NPAIR)
                        for i in range(NT):
                            if i >= 2:
                                t.wait_ge(sEV.h, wupG[i - 2])
                            for nh in range(2):
                                for pch in range(HC):
                                    t.matmul(ps_s[i % 2][:, nh * 512:(nh + 1) * 512],
                                             aoT[:, pch * S + i * 128:
                                                  pch * S + (i + 1) * 128],
                                             woT_sb[:, pch * D + nh * 512:
                                                    pch * D + (nh + 1) * 512],
                                             start=(pch == 0), stop=False)
                                inst = t.matmul(
                                    ps_s[i % 2][:, nh * 512:(nh + 1) * 512],
                                    onr_bf[:, :],
                                    bo_bf[0:1, nh * 512:(nh + 1) * 512],
                                    start=False, stop=True)
                            sPE.inc(inst)

                    @blk.vector
                    def _(v):
                        for i in range(NT):
                            par = i % 2
                            v.wait_ge(sPE.h, wmmG[i])
                            v.wait_ge(sL.h, L0 + 16 * NPAIR + 32 * (i + 1))
                            if i >= 2:
                                v.wait_ge(sS.h, S0 + 16 * (i - 1))
                            for nh in range(2):
                                v.tensor_tensor(
                                    tmpG[:, :],
                                    ps_s[par][:, nh * 512:(nh + 1) * 512],
                                    g1b[:, par * D + nh * 512:
                                        par * D + (nh + 1) * 512], ALU.mult)
                                inst = v.tensor_add(
                                    ob[:, par * D + nh * 512:
                                       par * D + (nh + 1) * 512],
                                    tmpG[:, :],
                                    x1b[:, par * D + nh * 512:
                                        par * D + (nh + 1) * 512])
                            sEV.inc(inst)

        phase_ln(0)
        if n_phases >= 2:
            phase_xnf()
            phase_combine()
        if n_phases >= 3:
            phase_l0ffn()
        if n_phases >= 4:
            phase_l0attn()
        if n_phases >= 5:
            phase_ln(1)
            phase_xnf()
        if n_phases >= 6:
            phase_l1ffn()
        if n_phases >= 7:
            phase_l1attn()
        if n_phases >= 8:
            phase_l1out()

        if debug:
            with nc.Block() as blk:
                dbgA = dt("dbgA", [128, NT * 8], F32, "ExternalOutput")
                dbgB = dt("dbgB", [8, NT * 128], F32, "ExternalOutput")
                dbgC = dt("dbgC", [128, HC * S], F32, "ExternalOutput")
                dbg_xnd = dt("dbg_xnd", [S, D], F32, "ExternalOutput")
                dbg_xres = dt("dbg_xres", [S, D], F32, "ExternalOutput")
                dbg_g = dt("dbg_g", [S, D], F32, "ExternalOutput")
                dbg_ao = dt("dbg_ao", [NPAIR, S, 128], F32, "ExternalOutput")

                @blk.gpsimd
                def _(g):
                    sW.d(g.dma_start(dbgA[:, :], logits_sb[:, :]))
                    sW.d(g.dma_start(dbg_xnd[:, :], xn_dram[:, :]))
                    if n_phases >= 2:
                        sW.d(g.dma_start(dbgB[:, :], combT[:, :]))
                        sW.d(g.dma_start(dbgC[:, :], xn_f[:, :]))
                    if n_phases >= 3:
                        sW.d(g.dma_start(dbg_g[:, :], g_scr[:, :]))
                    if n_phases >= 4:
                        sW.d(g.dma_start(dbg_xres[:, :], x_resid[:, :]))
                    if n_phases >= 7:
                        for p in range(NPAIR):
                            sW.d(g.dma_start(dbg_ao[p, :, :], ao_hbm[p, :, :]))
                    g.wait_ge(sW.h, sW.n)

    return nc


def _layout_inputs(inputs):
    f = np.ascontiguousarray
    esel = np.zeros((8, 1024), np.float32)
    for e in range(8):
        esel[e, e * 128:(e + 1) * 128] = 1.0
    common = {
        "wl": f(np.asarray(inputs["L0_attn_w_lhs"], np.float32).reshape(8, 128).T),
        "wr": f(np.asarray(inputs["L0_attn_w_rhs"], np.float32).reshape(8, 128).T),
        "gwT": f(np.asarray(inputs["L0_ffn_gate_weight"], np.float32).T),
        "wgT": f(np.transpose(np.asarray(inputs["L0_ffn_w_gate"], np.float32),
                              (0, 2, 1))),
        "wuT": f(np.transpose(np.asarray(inputs["L0_ffn_w_up"], np.float32),
                              (0, 2, 1))),
        "wdT": f(np.transpose(np.asarray(inputs["L0_ffn_w_down"], np.float32),
                              (0, 2, 1))),
        "wsgT": f(np.asarray(inputs["L0_ffn_ws_gate"], np.float32).T),
        "wsuT": f(np.asarray(inputs["L0_ffn_ws_up"], np.float32).T),
        "wsdT": f(np.asarray(inputs["L0_ffn_ws_down"], np.float32).T),
        "wqT": f(np.asarray(inputs["L1_attn_Wq"], np.float32).T),
        "wkT": f(np.asarray(inputs["L1_attn_Wk"], np.float32).T),
        "wvTs": f(np.asarray(inputs["L1_attn_Wv"], np.float32).T
                  .reshape(1024, 16, 64).transpose(0, 2, 1).reshape(1024, 1024)),
        "woT": f(np.asarray(inputs["L1_attn_Wo"], np.float32).T),
        "bqrow": f(np.asarray(inputs["L1_attn_bq"], np.float32).reshape(1, 1024)),
        "bkrow": f(np.asarray(inputs["L1_attn_bk"], np.float32).reshape(1, 1024)),
        "bvsrow": f(np.asarray(inputs["L1_attn_bv"], np.float32)
                    .reshape(16, 64).T.reshape(1, 1024)),
        "borow": f(np.asarray(inputs["L1_attn_bo"], np.float32).reshape(1, 1024)),
        "cb": f(np.asarray(inputs["L1_ffn_CB"], np.float32)),
        "scb": f(np.asarray(inputs["L1_ffn_SCB"], np.float32).reshape(8, 128).T),
        "b1row": f(np.asarray(inputs["L1_ffn_bias"], np.float32).reshape(1, 1024)),
        "ident": np.eye(128, dtype=np.float32),
        "onesrow": np.ones((1, 128), np.float32),
        "onescol": np.ones((128, 1), np.float32),
        "esel": esel,
    }
    x = np.asarray(inputs["L0_attn_x"], np.float32)
    return [dict(common, x=f(x[c])) for c in range(8)]


def kernel(**inputs):
    in_maps = _layout_inputs(inputs)
    nc = build_nc()
    res = run_bass_kernel_spmd(nc, in_maps, core_ids=list(range(8)))
    return np.stack([r["out"] for r in res.results], axis=0)
